# version 04
# baseline (speedup 1.0000x reference)
import contextlib
import numpy as np
import ml_dtypes

import concourse.bass as bass
import concourse.mybir as mybir
from concourse.bass_utils import run_bass_kernel_spmd

F32 = mybir.dt.float32
BF16 = mybir.dt.bfloat16
AF = mybir.ActivationFunctionType
ALU = mybir.AluOpType
AX = mybir.AxisListType

HID = 4096
H = 128
P = 64
NST = 128
G = 8
CH = 128
INTER = 8192
CONV_DIM = 10240
PROJ = 18560
T = 1024
TEXT = 1152
NTOK = 1027
MT = 259
MTOK = 256
XNW = 1286
NCHUNK = 8
NFT = 145
FT_HBC0 = 64
FT_DT = 144
EPS = 1e-5
SCALING = 2.0
QWIN = 1436.0

REPLICA_GROUPS = [[0, 1], [2, 3], [4, 5], [6, 7]]


class Ctr:
    def __init__(self, st, nc, name):
        self.sem = st.enter_context(nc.semaphore(name))
        self.n = 0
        self.fence = 0
        self.fence_waited = 0

    def mark(self):
        self.fence = max(self.fence, self.n)
        return self.n


def build_nc(replica_groups, upto="full", dbg=False):
    nc = bass.Bass("TRN2", target_bir_lowering=False, debug=False)
    kdbg = "ExternalOutput" if dbg else "Internal"

    x_ext = nc.dram_tensor("x", [TEXT, HID], F32, kind="ExternalInput").ap()
    x_oth = nc.dram_tensor("x_oth", [TEXT, HID], F32, kind="ExternalInput").ap()
    w_in_s = nc.dram_tensor("w_in_s", [NFT, 128, 32 * 128], F32, kind="ExternalInput").ap()
    w_out_s = nc.dram_tensor("w_out_s", [32, 128, 64 * 128], F32, kind="ExternalInput").ap()
    w_ffn_s = nc.dram_tensor("w_ffn_s", [32, 128, 32 * 128], F32, kind="ExternalInput").ap()
    a_s = nc.dram_tensor("a_s", [128, 32 * 128], F32, kind="ExternalInput").ap()
    b_flat = nc.dram_tensor("b_flat", [128, HID], F32, kind="ExternalInput").ap()
    rwt_s = nc.dram_tensor("rwt_s", [128, 32 * 8], F32, kind="ExternalInput").ap()
    convw = nc.dram_tensor("convw", [128, 320], F32, kind="ExternalInput").ap()
    convb = nc.dram_tensor("convb", [128, 80], F32, kind="ExternalInput").ap()
    dt_bias = nc.dram_tensor("dt_bias", [128, 1], F32, kind="ExternalInput").ap()
    a_log = nc.dram_tensor("a_log", [128, 1], F32, kind="ExternalInput").ap()
    d128 = nc.dram_tensor("d128", [128, 128], F32, kind="ExternalInput").ap()
    nwt = nc.dram_tensor("nwt", [128, 64], F32, kind="ExternalInput").ap()
    router_b = nc.dram_tensor("router_b", [1, 8], F32, kind="ExternalInput").ap()
    ffn_bias = nc.dram_tensor("ffn_bias", [128, 32], F32, kind="ExternalInput").ap()
    ident = nc.dram_tensor("ident", [128, 128], BF16, kind="ExternalInput").ap()
    identf = nc.dram_tensor("identf", [128, 128], F32, kind="ExternalInput").ap()
    maskfill = nc.dram_tensor("maskfill", [128, 128], F32, kind="ExternalInput").ap()
    qmask = nc.dram_tensor("qmask", [128, 9], BF16, kind="ExternalInput").ap()
    qmask2 = nc.dram_tensor("qmask2", [128, 9], BF16, kind="ExternalInput").ap()
    isodd = nc.dram_tensor("isodd", [128, 1], F32, kind="ExternalInput").ap()
    notodd = nc.dram_tensor("notodd", [128, 1], F32, kind="ExternalInput").ap()
    out = nc.dram_tensor("out", [T, HID], F32, kind="ExternalOutput").ap()

    scr_hs = nc.dram_tensor("scr_hs", [NCHUNK, 128, INTER], BF16, kind=kdbg).ap()
    scr_g = nc.dram_tensor("scr_g", [NCHUNK, 128, INTER], BF16, kind=kdbg).ap()
    scr_s = nc.dram_tensor("scr_s", [NCHUNK, 128, INTER], BF16, kind=kdbg).ap()
    scr_g2 = nc.dram_tensor("scr_g2", [NCHUNK, 128, 1024], BF16, kind=kdbg).ap()
    scr_base = nc.dram_tensor("scr_base", [32, 128, T], BF16, kind=kdbg).ap()
    scr_n = nc.dram_tensor("scr_n", [2, 64, 128, 512], BF16, kind=kdbg).ap()
    scr_gg = nc.dram_tensor("scr_gg", [HID, T], BF16, kind=kdbg).ap()
    scr_acs = nc.dram_tensor("scr_acs", [2, 128 * 128], F32, kind=kdbg).ap()
    scr_dexp = nc.dram_tensor("scr_dexp", [2, INTER], F32, kind=kdbg).ap()
    scr_dexp2 = nc.dram_tensor("scr_dexp2", [2, INTER], F32, kind=kdbg).ap()
    scr_acs2 = nc.dram_tensor("scr_acs2", [2, 128 * 128], F32, kind=kdbg).ap()
    scr_route = nc.dram_tensor("scr_route", [1, 8], F32, kind=kdbg).ap()
    scr_q = nc.dram_tensor("scr_q", [1, HID], F32, kind=kdbg).ap()
    scr_hsm = nc.dram_tensor("scr_hsm", [2, 128, INTER], BF16, kind=kdbg).ap()
    scr_nc = nc.dram_tensor("scr_nc", [128, INTER], F32, kind=kdbg).ap()
    dbg_xnt = nc.dram_tensor("dbg_xnt", [128, 32 * XNW], BF16, kind=kdbg).ap() if dbg else None

    ENGS = ["sync", "gpsimd", "tensor", "vector", "scalar"]
    plan = {e: [] for e in ENGS}

    class E:
        def __init__(self, eng):
            self.eng = eng

        def do(self, fn):
            plan[self.eng].append(fn)

        def wait(self, ctr, val=None):
            v = ctr.n if val is None else val
            sem = ctr.sem
            if v > 0:
                ctr.fence = max(ctr.fence, v)
                self.do(lambda e: e.wait_ge(sem, v))

    SY, GP, PE, VE, SC = E("sync"), E("gpsimd"), E("tensor"), E("vector"), E("scalar")

    def run(eng_obj, ctr, fn, k=1):
        if k == 16 and ctr.fence > ctr.fence_waited:
            fv = ctr.fence
            sem = ctr.sem
            eng_obj.do(lambda e: e.wait_ge(sem, fv))
            ctr.fence_waited = fv
        if k == 1 and eng_obj.eng in ("vector", "scalar") and ctr.n > 0:
            sv = ctr.n
            sem = ctr.sem
            eng_obj.do(lambda e: e.wait_ge(sem, sv))

        def g(e):
            fn(e).then_inc(ctr.sem, k)
        eng_obj.do(g)
        ctr.n += k
        return ctr.n

    st0 = contextlib.ExitStack()
    with st0:
        ident_sb = st0.enter_context(nc.sbuf_tensor("ident_sb", [128, 128], BF16))
        identf_sb = st0.enter_context(nc.sbuf_tensor("identf_sb", [128, 128], F32))
        maskf_sb = st0.enter_context(nc.sbuf_tensor("maskf_sb", [128, 128], F32))
        convw_sb = st0.enter_context(nc.sbuf_tensor("convw_sb", [128, 320], F32))
        convb_sb = st0.enter_context(nc.sbuf_tensor("convb_sb", [128, 80], F32))
        dtb_sb = st0.enter_context(nc.sbuf_tensor("dtb_sb", [128, 1], F32))
        alog_sb = st0.enter_context(nc.sbuf_tensor("alog_sb", [128, 1], F32))
        acol_sb = st0.enter_context(nc.sbuf_tensor("acol_sb", [128, 1], F32))
        d128_sb = st0.enter_context(nc.sbuf_tensor("d128_sb", [128, 128], F32))
        nwt_sb = st0.enter_context(nc.sbuf_tensor("nwt_sb", [128, 64], F32))
        rb_sb = st0.enter_context(nc.sbuf_tensor("rb_sb", [1, 8], F32))
        fbias_sb = st0.enter_context(nc.sbuf_tensor("fbias_sb", [128, 32], F32))
        qm_sb = st0.enter_context(nc.sbuf_tensor("qm_sb", [128, 9], BF16))
        qm2_sb = st0.enter_context(nc.sbuf_tensor("qm2_sb", [128, 9], BF16))
        iso_sb = st0.enter_context(nc.sbuf_tensor("iso_sb", [128, 1], F32))
        noto_sb = st0.enter_context(nc.sbuf_tensor("noto_sb", [128, 1], F32))
        bone = st0.enter_context(nc.sbuf_tensor("bone", [128, 1], F32))
        beps = st0.enter_context(nc.sbuf_tensor("beps", [128, 1], F32))
        ones_f = st0.enter_context(nc.sbuf_tensor("ones_f", [128, 128], F32))
        ones_b1 = st0.enter_context(nc.sbuf_tensor("ones_b1", [1, 128], BF16))

        cS = Ctr(st0, nc, "cS")
        cG = Ctr(st0, nc, "cG")
        cG0 = Ctr(st0, nc, "cG0")
        cG1 = Ctr(st0, nc, "cG1")
        cGc = Ctr(st0, nc, "cGc")
        cS2 = Ctr(st0, nc, "cS2")
        cPE = Ctr(st0, nc, "cPE")
        cV = Ctr(st0, nc, "cV")
        cSc = Ctr(st0, nc, "cSc")
        ALLC = [cS, cG, cG0, cG1, cGc, cS2, cPE, cV, cSc]

        def barrier():
            vals = [(c, c.n) for c in ALLC]
            for Eo in (SY, GP, PE, VE, SC):
                for (c, v) in vals:
                    Eo.wait(c, v)

        for (dst, src) in [(ident_sb, ident), (identf_sb, identf), (maskf_sb, maskfill),
                           (convw_sb, convw), (convb_sb, convb), (dtb_sb, dt_bias),
                           (alog_sb, a_log), (d128_sb, d128), (nwt_sb, nwt),
                           (rb_sb, router_b), (fbias_sb, ffn_bias), (qm_sb, qmask),
                           (qm2_sb, qmask2),
                           (iso_sb, isodd), (noto_sb, notodd)]:
            run(SY, cS, lambda e, d=dst, s=src: e.dma_start(out=d[:, :], in_=s), 16)
        c_loads = cS.mark()
        VE.wait(cS, c_loads)
        run(VE, cV, lambda e: e.memset(bone[:, :], 1.0))
        run(VE, cV, lambda e: e.memset(beps[:, :], EPS))
        run(VE, cV, lambda e: e.memset(ones_f[:, :], 1.0))
        run(VE, cV, lambda e: e.memset(ones_b1[:, :], 1.0))
        SC.wait(cS, c_loads)
        run(SC, cSc, lambda e: e.activation(acol_sb[:, :], alog_sb[:, :], AF.Exp))
        VE.wait(cSc, cSc.n)
        run(VE, cV, lambda e: e.tensor_scalar(acol_sb[:, :], acol_sb[:, :], -1.0, None, ALU.mult))
        c_consts_v = cV.n

        st1 = contextlib.ExitStack()
        B_sb = st1.enter_context(nc.sbuf_tensor("B_sb", [128, G * T], BF16))
        C_sb = st1.enter_context(nc.sbuf_tensor("C_sb", [128, G * T], BF16))
        dt_sb = st1.enter_context(nc.sbuf_tensor("dt_sb", [128, T], F32))
        acs_sb = st1.enter_context(nc.sbuf_tensor("acs_sb", [128, T], F32))
        t2raw = st1.enter_context(nc.sbuf_tensor("t2raw", [128, T], BF16))
        bflat_sb = st1.enter_context(nc.sbuf_tensor("bflat_sb", [128, HID], BF16))
        B_m = st1.enter_context(nc.sbuf_tensor("B_m", [128, G * MTOK], BF16))
        dt_m = st1.enter_context(nc.sbuf_tensor("dt_m", [128, MTOK], F32))
        acs_m = st1.enter_context(nc.sbuf_tensor("acs_m", [128, MTOK], F32))
        cdump = st1.enter_context(nc.sbuf_tensor("cdump", [128, MTOK], BF16))
        run(GP, cG, lambda e: e.dma_start(out=bflat_sb[:, :], in_=b_flat), 16)

        stA = contextlib.ExitStack()
        xn_T = stA.enter_context(nc.sbuf_tensor("xn_T", [128, 32 * XNW], BF16))
        wbuf0 = stA.enter_context(nc.sbuf_tensor("wbuf0", [128, 32 * 128], BF16))
        wbuf1 = stA.enter_context(nc.sbuf_tensor("wbuf1", [128, 32 * 128], BF16))

        stA0 = contextlib.ExitStack()
        xt_sb = stA0.enter_context(nc.sbuf_tensor("xt_sb", [128, HID], F32))
        xn_tok = stA0.enter_context(nc.sbuf_tensor("xn_tok", [128, HID], BF16))
        qacc = stA0.enter_context(nc.sbuf_tensor("qacc", [1, HID], F32))
        m_col = stA0.enter_context(nc.sbuf_tensor("m_col", [128, 1], F32))
        nm_col = stA0.enter_context(nc.sbuf_tensor("nm_col", [128, 1], F32))
        s2_col = stA0.enter_context(nc.sbuf_tensor("s2_col", [128, 1], F32))
        v_col = stA0.enter_context(nc.sbuf_tensor("v_col", [128, 1], F32))
        sd_col = stA0.enter_context(nc.sbuf_tensor("sd_col", [128, 1], F32))
        rs_col = stA0.enter_context(nc.sbuf_tensor("rs_col", [128, 1], F32))
        pT0_a0 = stA0.enter_context(nc.psum_tensor("pT0_a0", [128, 128], BF16))
        pT1_a0 = stA0.enter_context(nc.psum_tensor("pT1_a0", [128, 128], BF16))
        pQ0_a0 = stA0.enter_context(nc.psum_tensor("pQ0_a0", [128, 512], F32))
        pQ1_a0 = stA0.enter_context(nc.psum_tensor("pQ1_a0", [128, 512], F32))

        VE.wait(cS, c_loads)
        VE.wait(cV, c_consts_v)
        xt_free_v = 0
        xt_free_s = 0
        xn_free_pe = 0
        qev = {}
        trev = {}
        tile_specs = []
        for ti in range(9):
            tile_specs.append((x_ext, qm_sb, ti, ti * 128, 128 if ti < 8 else 3))
        for ti in range(9):
            tcol = 1027 + (ti - 6) * 128 if ti >= 6 else None
            nco = (128 if ti < 8 else 3) if tcol is not None else 0
            tile_specs.append((x_oth, qm2_sb, ti, tcol, nco))
        for tix, (xsrc, qsb, ti, tcol, ncols) in enumerate(tile_specs):
            SY.wait(cV, xt_free_v)
            SY.wait(cSc, xt_free_s)
            run(SY, cS, lambda e, ti=ti, xsrc=xsrc: e.dma_start(
                out=xt_sb[:, :], in_=xsrc[ti * 128:(ti + 1) * 128, :]), 16)
            ld = cS.mark()
            VE.wait(cS, ld)
            run(VE, cV, lambda e: e.reduce_sum(m_col[:, :], xt_sb[:, :], axis=AX.X))
            VE.wait(cV, cV.n)
            run(VE, cV, lambda e: e.tensor_scalar(nm_col[:, :], m_col[:, :], -1.0 / HID, None, ALU.mult))
            run(VE, cV, lambda e: e.tensor_scalar(m_col[:, :], m_col[:, :], 1.0 / HID, None, ALU.mult))
            nm_v = cV.n
            SC.wait(cV, nm_v)
            SC.wait(cS, ld)
            SC.wait(cPE, xn_free_pe)
            run(SC, cSc, lambda e: e.activation(xn_tok[:, :], xt_sb[:, :], AF.Square,
                                                bias=nm_col[:, 0:1]))
            sq_s = cSc.n
            VE.wait(cSc, sq_s)
            run(VE, cV, lambda e: e.reduce_sum(s2_col[:, :], xn_tok[:, :], axis=AX.X))
            VE.wait(cV, cV.n)
            run(VE, cV, lambda e: e.tensor_scalar(v_col[:, :], s2_col[:, :], 1.0 / HID, None, ALU.mult))
            SC.wait(cV, cV.n)
            run(SC, cSc, lambda e: e.activation(sd_col[:, :], v_col[:, :], AF.Sqrt,
                                                bias=beps[:, 0:1]))
            VE.wait(cSc, cSc.n)
            run(VE, cV, lambda e: e.reciprocal(rs_col[:, :], sd_col[:, :]))
            VE.wait(cV, cV.n)
            run(VE, cV, lambda e: e.tensor_scalar(
                xn_tok[:, :], xt_sb[:, :], m_col[:, 0:1], rs_col[:, 0:1],
                ALU.subtract, ALU.mult))
            xn_v = cV.n
            xt_free_v = xn_v
            xt_free_s = sq_s
            PE.wait(cV, xn_v)
            PE.wait(cS, c_loads)
            for db in range(8):
                pq = pQ0_a0 if db % 2 == 0 else pQ1_a0
                if tix > 0 or db >= 2:
                    prev = qev.get((tix, db - 2) if db >= 2 else (tix - 1, 6 + db))
                    if prev:
                        PE.wait(cV, prev)
                run(PE, cPE, lambda e, db=db, ti=ti, pq=pq, qsb=qsb: e.matmul(
                    pq[0:1, :], qsb[:, ti:ti + 1], xn_tok[:, db * 512:(db + 1) * 512],
                    start=True, stop=True, skip_group_check=True))
                mm = cPE.n
                VE.wait(cPE, mm)
                if tix == 0:
                    run(VE, cV, lambda e, db=db, pq=pq: e.tensor_copy(
                        qacc[0:1, db * 512:(db + 1) * 512], pq[0:1, :]))
                else:
                    run(VE, cV, lambda e, db=db, pq=pq: e.tensor_tensor(
                        qacc[0:1, db * 512:(db + 1) * 512],
                        qacc[0:1, db * 512:(db + 1) * 512], pq[0:1, :], ALU.add))
                qev[(tix, db)] = cV.n
            if tcol is not None:
                for j in range(32):
                    pt = pT0_a0 if j % 2 == 0 else pT1_a0
                    prev = trev.get((tix, j - 2) if j >= 2 else (tix - 1, 30 + j))
                    if prev:
                        PE.wait(cV, prev)
                    run(PE, cPE, lambda e, j=j, pt=pt: e.transpose(
                        pt[:, :], xn_tok[:, j * 128:(j + 1) * 128], ident_sb[:, :]))
                    mm = cPE.n
                    VE.wait(cPE, mm)
                    run(VE, cV, lambda e, j=j, tcol=tcol, pt=pt, ncols=ncols: e.tensor_copy(
                        xn_T[:, j * XNW + tcol: j * XNW + tcol + ncols],
                        pt[:, 0:ncols]))
                    trev[(tix, j)] = cV.n
            xn_free_pe = cPE.n
        SY.wait(cV, cV.n)
        run(SY, cS, lambda e: e.dma_start(out=scr_q, in_=qacc[0:1, :]), 16)
        barrier()
        if dbg and upto == "ph0":
            run(SY, cS, lambda e: e.dma_start(out=dbg_xnt, in_=xn_T[:, :]), 16)
            barrier()
        stA0.close()
        if upto == "ph0":
            stA.close(); st1.close()
            _finish(plan, SY, ALLC)
            _emit_block(nc, st0, plan)
            return nc

        stA1 = contextlib.ExitStack()
        hbc_sb = stA1.enter_context(nc.sbuf_tensor("hbc_sb", [128, 1292], F32))
        gsil_sb = stA1.enter_context(nc.sbuf_tensor("gsil_sb", [128, T], BF16))
        acc0 = stA1.enter_context(nc.sbuf_tensor("acc0", [128, 1280], F32))
        acc1 = stA1.enter_context(nc.sbuf_tensor("acc1", [128, 1280], F32))
        accs = stA1.enter_context(nc.sbuf_tensor("accs", [128, 1280], BF16))
        tbuf8 = stA1.enter_context(nc.sbuf_tensor("tbuf8", [128, T], BF16))
        tb_m = stA1.enter_context(nc.sbuf_tensor("tb_m", [128, MTOK], BF16))
        adt_sb = stA1.enter_context(nc.sbuf_tensor("adt_sb", [128, T], F32))
        pT0_a1 = stA1.enter_context(nc.psum_tensor("pT0b", [128, 128], BF16))
        pT1_a1 = stA1.enter_context(nc.psum_tensor("pT1b", [128, 128], BF16))
        pP = [stA1.enter_context(nc.psum_tensor(f"pP{i}", [128, 429], F32))
              for i in range(6)]
        psets = [(pP[0], pP[1], pP[2]), (pP[3], pP[4], pP[5])]
        TB_GATE = [(3, 343), (346, 343), (689, 338)]
        TB_EXT = [(0, 429), (429, 429), (858, 428)]
        TB_C = [(0, 429), (429, 429), (858, 169)]

        slab_pe = {}
        slab_ld = {}
        evac_done = {}
        last_tr = {0: 0, 1: 0}
        tb_st = 0
        tbm_st = 0

        def slab_dma(ft):
            if ft - 2 in slab_pe:
                GP.wait(cPE, slab_pe[ft - 2])
            wb = wbuf0 if ft % 2 == 0 else wbuf1
            ctr = cG0 if ft % 2 == 0 else cG1
            run(GP, ctr, lambda e, wb=wb, ft=ft: e.dma_start(
                out=wb[:, :], in_=w_in_s[ft]), 16)
            slab_ld[ft] = (ctr, ctr.mark())

        def conv_chain(cf, obase, ibase, ln):
            VE.wait(cV, cV.n)
            run(VE, cV, lambda e, cf=cf: e.tensor_scalar(
                acc0[:, obase:obase + ln], hbc_sb[:, ibase:ibase + ln],
                convw_sb[:, cf * 4: cf * 4 + 1], None, ALU.mult))
            for kk in range(1, 4):
                src = acc0 if kk % 2 == 0 else acc1
                dst = acc1 if kk % 2 == 0 else acc0
            run(VE, cV, lambda e, cf=cf: e.scalar_tensor_tensor(
                acc1[:, obase:obase + ln], hbc_sb[:, ibase + 1:ibase + 1 + ln],
                convw_sb[:, cf * 4 + 1: cf * 4 + 2], acc0[:, obase:obase + ln],
                ALU.mult, ALU.add))
            run(VE, cV, lambda e, cf=cf: e.scalar_tensor_tensor(
                acc0[:, obase:obase + ln], hbc_sb[:, ibase + 2:ibase + 2 + ln],
                convw_sb[:, cf * 4 + 2: cf * 4 + 3], acc1[:, obase:obase + ln],
                ALU.mult, ALU.add))
            run(VE, cV, lambda e, cf=cf: e.scalar_tensor_tensor(
                acc1[:, obase:obase + ln], hbc_sb[:, ibase + 3:ibase + 3 + ln],
                convw_sb[:, cf * 4 + 3: cf * 4 + 4], acc0[:, obase:obase + ln],
                ALU.mult, ALU.add))

        slab_dma(0)
        slab_dma(1)
        for ft in range(NFT):
            wb = wbuf0 if ft % 2 == 0 else wbuf1
            ps3 = psets[ft % 2]
            if ft < FT_HBC0:
                TB = TB_GATE
            elif ft < 136 or ft == FT_DT:
                TB = TB_EXT
            else:
                TB = TB_C
            PE.wait(*slab_ld[ft])
            if ft >= 2:
                ev = evac_done[ft - 2]
                PE.wait(cV, ev[0])
                PE.wait(cSc, ev[1])
            for tb in range(3):
                off, ln = TB[tb]
                ps = ps3[tb]
                for dc in range(32):
                    run(PE, cPE, lambda e, ps=ps, wb=wb, off=off, ln=ln, dc=dc: e.matmul(
                        ps[:, 0:ln], wb[:, dc * 128:(dc + 1) * 128],
                        xn_T[:, dc * XNW + off: dc * XNW + off + ln],
                        start=(dc == 0), stop=(dc == 31), skip_group_check=True))
            mm_done = cPE.n
            slab_pe[ft] = mm_done
            if ft + 2 < NFT:
                slab_dma(ft + 2)
            if ft < FT_HBC0:
                SC.wait(cPE, mm_done)
                if ft >= 1:
                    SC.wait(cPE, gs_tr_pe)
                for tb in range(3):
                    off, ln = TB[tb]
                    ps = ps3[tb]
                    run(SC, cSc, lambda e, ps=ps, off=off, ln=ln: e.activation(
                        gsil_sb[:, off - 3:off - 3 + ln], ps[:, 0:ln], AF.Silu))
                ev_s = cSc.n
                PE.wait(cSc, ev_s)
                for j in range(8):
                    pt = pT0_a1 if j % 2 == 0 else pT1_a1
                    PE.wait(cV, last_tr[j % 2])
                    run(PE, cPE, lambda e, j=j, pt=pt: e.transpose(
                        pt[:, :], gsil_sb[:, j * 128:(j + 1) * 128], ident_sb[:, :]))
                    VE.wait(cPE, cPE.n)
                    VE.wait(cS, tb_st)
                    run(VE, cV, lambda e, j=j, pt=pt: e.tensor_copy(
                        tbuf8[:, j * 128:(j + 1) * 128], pt[:, :]))
                    last_tr[j % 2] = cV.n
                gs_tr_pe = cPE.n
                SY.wait(cV, cV.n)
                run(SY, cS, lambda e, ft=ft: e.dma_start(
                    out=scr_g[:, :, ft * 128:(ft + 1) * 128].rearrange("k t f -> t k f"),
                    in_=tbuf8[:, :].rearrange("t (k f) -> t k f", k=8)), 16)
                tb_st = cS.mark()
                evac_done[ft] = (cV.n, ev_s)
            elif ft < FT_DT:
                is_ext = ft < 136
                VE.wait(cPE, mm_done)
                VE.wait(cSc, cSc.n)
                for tb in range(3):
                    off, ln = TB[tb]
                    ps = ps3[tb]
                    run(VE, cV, lambda e, ps=ps, off=off, ln=ln: e.tensor_copy(
                        hbc_sb[:, off:off + ln], ps[:, 0:ln]))
                cf = ft - FT_HBC0
                conv_chain(cf, 0, 0, T)
                if is_ext:
                    conv_chain(cf, T, 1027, MTOK)
                conv_v = cV.n
                SC.wait(cV, conv_v)
                if ft < 128:
                    hsi = ft - FT_HBC0
                    if hsi >= 1:
                        SC.wait(cPE, hs_tr_pe)
                    run(SC, cSc, lambda e, cf=cf: e.activation(
                        accs[:, :], acc1[:, :], AF.Silu, bias=convb_sb[:, cf:cf + 1]))
                    ev_s = cSc.n
                    PE.wait(cSc, ev_s)
                    for j in range(10):
                        pt = pT0_a1 if j % 2 == 0 else pT1_a1
                        PE.wait(cV, last_tr[j % 2])
                        run(PE, cPE, lambda e, j=j, pt=pt: e.transpose(
                            pt[:, :], accs[:, j * 128:(j + 1) * 128], ident_sb[:, :]))
                        VE.wait(cPE, cPE.n)
                        if j < 8:
                            VE.wait(cS, tb_st)
                            run(VE, cV, lambda e, j=j, pt=pt: e.tensor_copy(
                                tbuf8[:, j * 128:(j + 1) * 128], pt[:, :]))
                        else:
                            VE.wait(cS, tbm_st)
                            run(VE, cV, lambda e, j=j, pt=pt: e.tensor_copy(
                                tb_m[:, (j - 8) * 128:(j - 7) * 128], pt[:, :]))
                        last_tr[j % 2] = cV.n
                    hs_tr_pe = cPE.n
                    SY.wait(cV, cV.n)
                    run(SY, cS, lambda e, hsi=hsi: e.dma_start(
                        out=scr_hs[:, :, hsi * 128:(hsi + 1) * 128].rearrange("k t f -> t k f"),
                        in_=tbuf8[:, :].rearrange("t (k f) -> t k f", k=8)), 16)
                    tb_st = cS.mark()
                    run(SY, cS, lambda e, hsi=hsi: e.dma_start(
                        out=scr_hsm[:, :, hsi * 128:(hsi + 1) * 128].rearrange("k t f -> t k f"),
                        in_=tb_m[:, :].rearrange("t (k f) -> t k f", k=2)), 16)
                    tbm_st = cS.mark()
                    evac_done[ft] = (cV.n, ev_s)
                else:
                    gi = ft - 128
                    dst = B_sb if gi < 8 else C_sb
                    gg = gi % 8
                    run(SC, cSc, lambda e, cf=cf, dst=dst, gg=gg: e.activation(
                        dst[:, gg * T:(gg + 1) * T], acc1[:, 0:T], AF.Silu,
                        bias=convb_sb[:, cf:cf + 1]))
                    if gi < 8:
                        SC.wait(cSc, cSc.n)
                        run(SC, cSc, lambda e, cf=cf, gg=gg: e.activation(
                            B_m[:, gg * MTOK:(gg + 1) * MTOK], acc1[:, T:1280], AF.Silu,
                            bias=convb_sb[:, cf:cf + 1]))
                    evac_done[ft] = (cV.n, cSc.n)
            else:
                VE.wait(cPE, mm_done)
                VE.wait(cSc, cSc.n)
                for tb in range(3):
                    off, ln = TB[tb]
                    ps = ps3[tb]
                    run(VE, cV, lambda e, ps=ps, off=off, ln=ln: e.tensor_copy(
                        hbc_sb[:, off:off + ln], ps[:, 0:ln]))
                ev_v = cV.n
                SC.wait(cV, ev_v)
                run(SC, cSc, lambda e: e.activation(
                    acc0[:, 0:T], hbc_sb[:, 3:3 + T], AF.Exp, bias=dtb_sb[:, 0:1]))
                SC.wait(cSc, cSc.n)
                run(SC, cSc, lambda e: e.activation(
                    dt_sb[:, :], acc0[:, 0:T], AF.Ln, bias=bone[:, 0:1]))
                SC.wait(cSc, cSc.n)
                run(SC, cSc, lambda e: e.activation(
                    acc0[:, T:1280], hbc_sb[:, 1030:1286], AF.Exp, bias=dtb_sb[:, 0:1]))
                SC.wait(cSc, cSc.n)
                run(SC, cSc, lambda e: e.activation(
                    dt_m[:, :], acc0[:, T:1280], AF.Ln, bias=bone[:, 0:1]))
                dt_s = cSc.n
                VE.wait(cSc, dt_s)
                run(VE, cV, lambda e: e.tensor_scalar(
                    adt_sb[:, :], dt_sb[:, :], acol_sb[:, 0:1], None, ALU.mult))
                run(VE, cV, lambda e: e.tensor_scalar(
                    acc1[:, 0:MTOK], dt_m[:, :], acol_sb[:, 0:1], None, ALU.mult))
                VE.wait(cV, cV.n)
                for k in range(NCHUNK):
                    run(VE, cV, lambda e, k=k: e.tensor_tensor_scan(
                        acs_sb[:, k * 128:(k + 1) * 128], ones_f[:, :],
                        adt_sb[:, k * 128:(k + 1) * 128], 0.0, ALU.mult, ALU.add))
                for k in range(2):
                    run(VE, cV, lambda e, k=k: e.tensor_tensor_scan(
                        acs_m[:, k * 128:(k + 1) * 128], ones_f[:, :],
                        acc1[:, k * 128:(k + 1) * 128], 0.0, ALU.mult, ALU.add))
                evac_done[ft] = (cV.n, cSc.n)
        barrier()
        stA1.close()
        if upto == "ph1":
            stA.close(); st1.close()
            _finish(plan, SY, ALLC)
            _emit_block(nc, st0, plan)
            return nc

        stA3 = contextlib.ExitStack()
        bse_a3 = stA3.enter_context(nc.sbuf_tensor("bse_a3", [128, T], BF16))
        pQ0_a3 = stA3.enter_context(nc.psum_tensor("pQ0d", [128, 512], F32))
        pQ1_a3 = stA3.enter_context(nc.psum_tensor("pQ1d", [128, 512], F32))
        run(GP, cG0, lambda e: e.dma_start(out=wbuf0[:, :], in_=a_s), 16)
        a_ld = cG0.mark()
        PE.wait(cG0, a_ld)
        for blk in range(2):
            pq = pQ0_a3 if blk == 0 else pQ1_a3
            for dc in range(32):
                run(PE, cPE, lambda e, pq=pq, blk=blk, dc=dc: e.matmul(
                    pq[:, :], wbuf0[:, dc * 128:(dc + 1) * 128],
                    xn_T[:, dc * XNW + 3 + blk * 512: dc * XNW + 3 + (blk + 1) * 512],
                    start=(dc == 0), stop=(dc == 31), skip_group_check=True))
            VE.wait(cPE, cPE.n)
            run(VE, cV, lambda e, pq=pq, blk=blk: e.tensor_copy(
                t2raw[:, blk * 512:(blk + 1) * 512], pq[:, :]))
        t2_v = cV.n
        base_pe = {}
        for ot in range(32):
            wb = wbuf1 if ot % 2 == 0 else wbuf0
            if ot == 0:
                GP.wait(cPE, cPE.n)
            elif ot == 1:
                GP.wait(cPE, cPE.n)
            else:
                GP.wait(cPE, base_pe[ot - 2])
            ctr = cG1 if ot % 2 == 0 else cG0
            run(GP, ctr, lambda e, wb=wb, ot=ot: e.dma_start(
                out=wb[:, :], in_=w_ffn_s[ot]), 16)
            PE.wait(ctr, ctr.mark())
            if ot == 0:
                PE.wait(cV, t2_v)
            if ot >= 1:
                PE.wait(cV, bse_v)
            for blk in range(2):
                pq = pQ0_a3 if blk == 0 else pQ1_a3
                for dc in range(32):
                    run(PE, cPE, lambda e, pq=pq, wb=wb, blk=blk, dc=dc: e.matmul(
                        pq[:, :], wb[:, dc * 128:(dc + 1) * 128],
                        xn_T[:, dc * XNW + 3 + blk * 512: dc * XNW + 3 + (blk + 1) * 512],
                        start=(dc == 0), stop=(dc == 31), skip_group_check=True))
            base_pe[ot] = cPE.n
            VE.wait(cPE, cPE.n)
            if ot >= 1:
                VE.wait(cS, bse_st)
            run(VE, cV, lambda e: e.tensor_copy(bse_a3[:, 0:512], pQ0_a3[:, :]))
            run(VE, cV, lambda e: e.tensor_copy(bse_a3[:, 512:1024], pQ1_a3[:, :]))
            bse_v = cV.n
            SY.wait(cV, bse_v)
            run(SY, cS, lambda e, ot=ot: e.dma_start(out=scr_base[ot], in_=bse_a3[:, :]), 16)
            bse_st = cS.mark()
        barrier()
        stA3.close()
        stA.close()
        if upto == "ph15a":
            st1.close()
            _finish(plan, SY, ALLC)
            _emit_block(nc, st0, plan)
            return nc

        stA2 = contextlib.ExitStack()
        N_a2 = stA2.enter_context(nc.sbuf_tensor("N_a2", [128, INTER], F32))
        tmpX_a2 = stA2.enter_context(nc.sbuf_tensor("tmpX_a2", [128, INTER], BF16))
        sgrp0 = stA2.enter_context(nc.sbuf_tensor("sgrp0", [128, 1024], BF16))
        sgrp1 = stA2.enter_context(nc.sbuf_tensor("sgrp1", [128, 1024], BF16))
        g2bf_a2 = stA2.enter_context(nc.sbuf_tensor("g2bf_a2", [128, 1024], BF16))
        xdec_a2 = stA2.enter_context(nc.sbuf_tensor("xdec_a2", [128, 1024], BF16))
        decays = stA2.enter_context(nc.sbuf_tensor("decays", [128, 128], F32))
        dtdec = stA2.enter_context(nc.sbuf_tensor("dtdec", [128, 128], F32))
        dtdecT = stA2.enter_context(nc.sbuf_tensor("dtdecT", [128, 128], F32))
        btg = stA2.enter_context(nc.sbuf_tensor("btg", [128, 128], BF16))
        dexp_a2 = stA2.enter_context(nc.sbuf_tensor("dexp_a2", [128, 1], F32))
        pT0_a2 = stA2.enter_context(nc.psum_tensor("pT0c", [128, 128], BF16))
        pTf_a2 = stA2.enter_context(nc.psum_tensor("pTfc", [128, 128], F32))
        pQ0_a2 = stA2.enter_context(nc.psum_tensor("pQ0c", [128, 512], F32))
        pSt = stA2.enter_context(nc.psum_tensor("pStc", [128, 1024], F32))

        run(VE, cV, lambda e: e.memset(N_a2[:, :], 0.0))
        st_pe = 0
        for mk in range(2):
            SY.wait(cV, cV.n)
            run(SY, cS, lambda e, mk=mk: e.dma_start(out=tmpX_a2[:, :], in_=scr_hsm[mk]), 16)
            hs_ld = cS.mark()
            SC.wait(cV, cV.n)
            run(SC, cSc, lambda e, mk=mk: e.activation(
                decays[:, :], acs_m[:, mk * 128:(mk + 1) * 128], AF.Exp, scale=-1.0,
                bias=acs_m[:, mk * 128 + 127: mk * 128 + 128]))
            if mk == 0:
                SC.wait(cSc, cSc.n)
                run(SC, cSc, lambda e: e.activation(
                    dexp_a2[:, :], acs_m[:, 255:256], AF.Exp))
            VE.wait(cSc, cSc.n)
            run(VE, cV, lambda e, mk=mk: e.tensor_tensor(
                dtdec[:, :], dt_m[:, mk * 128:(mk + 1) * 128], decays[:, :], ALU.mult))
            if mk == 0:
                VE.wait(cV, cV.n)
                run(VE, cV, lambda e: e.tensor_scalar(
                    dtdec[:, :], dtdec[:, :], dexp_a2[:, 0:1], None, ALU.mult))
            PE.wait(cV, cV.n)
            run(PE, cPE, lambda e: e.transpose(pTf_a2[:, :], dtdec[:, :], identf_sb[:, :]))
            VE.wait(cPE, cPE.n)
            run(VE, cV, lambda e: e.tensor_copy(dtdecT[:, :], pTf_a2[:, :]))
            dtdecT_v = cV.n
            VE.wait(cS, hs_ld)
            for g in range(G):
                PE.wait(cV, cV.n)
                run(PE, cPE, lambda e, g=g, mk=mk: e.transpose(
                    pT0_a2[:, :], B_m[:, g * MTOK + mk * 128: g * MTOK + (mk + 1) * 128],
                    ident_sb[:, :]))
                VE.wait(cPE, cPE.n)
                run(VE, cV, lambda e: e.tensor_copy(btg[:, :], pT0_a2[:, :]))
                VE.wait(cV, dtdecT_v)
                if st_pe:
                    VE.wait(cPE, st_pe)
                for i in range(16):
                    h = 8 * i + g
                    run(VE, cV, lambda e, h=h, i=i: e.tensor_scalar(
                        xdec_a2[:, i * 64:(i + 1) * 64], tmpX_a2[:, h * 64:(h + 1) * 64],
                        dtdecT[:, h:h + 1], None, ALU.mult))
                xd_v = cV.n
                PE.wait(cV, xd_v)
                for i in range(16):
                    run(PE, cPE, lambda e, i=i: e.matmul(
                        pSt[:, i * 64:(i + 1) * 64], btg[:, :], xdec_a2[:, i * 64:(i + 1) * 64],
                        start=True, stop=True, skip_group_check=True))
                st_pe = cPE.n
                VE.wait(cPE, st_pe)
                run(VE, cV, lambda e, g=g: e.tensor_tensor(
                    N_a2[:, g * 1024:(g + 1) * 1024], N_a2[:, g * 1024:(g + 1) * 1024],
                    pSt[:, :], ALU.add))
        VE.wait(cV, cV.n)
        run(VE, cV, lambda e: e.tensor_scalar(N_a2[:, :], N_a2[:, :], iso_sb[:, 0:1], None, ALU.mult))
        SY.wait(cV, cV.n)
        run(SY, cS, lambda e: e.dma_start(out=scr_nc, in_=N_a2[:, :]), 16)

        hs_ld = 0
        sg_st = {}
        for k in range(NCHUNK):
            SY.wait(cV, cV.n)
            run(SY, cS, lambda e, k=k: e.dma_start(out=tmpX_a2[:, :], in_=scr_hs[k]), 16)
            hs_ld = cS.mark()
            SC.wait(cV, cV.n)
            run(SC, cSc, lambda e, k=k: e.activation(
                decays[:, :], acs_sb[:, k * 128:(k + 1) * 128], AF.Exp, scale=-1.0,
                bias=acs_sb[:, k * 128 + 127: k * 128 + 128]))
            VE.wait(cSc, cSc.n)
            run(VE, cV, lambda e, k=k: e.tensor_tensor(
                dtdec[:, :], dt_sb[:, k * 128:(k + 1) * 128], decays[:, :], ALU.mult))
            PE.wait(cV, cV.n)
            run(PE, cPE, lambda e: e.transpose(pTf_a2[:, :], dtdec[:, :], identf_sb[:, :]))
            VE.wait(cPE, cPE.n)
            run(VE, cV, lambda e: e.tensor_copy(dtdecT[:, :], pTf_a2[:, :]))
            dtdecT_v = cV.n
            for g in range(G):
                PE.wait(cV, cV.n)
                run(PE, cPE, lambda e, g=g, k=k: e.matmul(
                    pQ0_a2[:, 0:128], B_sb[:, g * T + k * 128: g * T + (k + 1) * 128],
                    C_sb[:, g * T + k * 128: g * T + (k + 1) * 128],
                    start=True, stop=True, skip_group_check=True))
                VE.wait(cPE, cPE.n)
                if k > 0 and g == 0:
                    VE.wait(cS, g2_st)
                run(VE, cV, lambda e, g=g: e.tensor_copy(
                    g2bf_a2[:, g * 128:(g + 1) * 128], pQ0_a2[:, 0:128]))
            SY.wait(cV, cV.n)
            run(SY, cS, lambda e, k=k: e.dma_start(out=scr_g2[k], in_=g2bf_a2[:, :]), 16)
            g2_st = cS.mark()
            VE.wait(cS, hs_ld)
            for g in range(G):
                sg = sgrp0 if g % 2 == 0 else sgrp1
                PE.wait(cV, cV.n)
                run(PE, cPE, lambda e, g=g, k=k: e.transpose(
                    pT0_a2[:, :], B_sb[:, g * T + k * 128: g * T + (k + 1) * 128], ident_sb[:, :]))
                VE.wait(cPE, cPE.n)
                run(VE, cV, lambda e: e.tensor_copy(btg[:, :], pT0_a2[:, :]))
                VE.wait(cV, dtdecT_v)
                VE.wait(cPE, st_pe)
                for i in range(16):
                    h = 8 * i + g
                    run(VE, cV, lambda e, h=h, i=i: e.tensor_scalar(
                        xdec_a2[:, i * 64:(i + 1) * 64], tmpX_a2[:, h * 64:(h + 1) * 64],
                        dtdecT[:, h:h + 1], None, ALU.mult))
                xd_v = cV.n
                PE.wait(cV, xd_v)
                for i in range(16):
                    run(PE, cPE, lambda e, i=i: e.matmul(
                        pSt[:, i * 64:(i + 1) * 64], btg[:, :], xdec_a2[:, i * 64:(i + 1) * 64],
                        start=True, stop=True, skip_group_check=True))
                st_pe = cPE.n
                VE.wait(cPE, st_pe)
                prev_st = sg_st.get((k, g - 2)) or (sg_st.get((k - 1, g + 6)) if k > 0 else None)
                if prev_st:
                    VE.wait(cS, prev_st)
                run(VE, cV, lambda e, sg=sg: e.tensor_copy(sg[:, :], pSt[:, :]))
                SY.wait(cV, cV.n)
                run(SY, cS, lambda e, k=k, g=g, sg=sg: e.dma_start(
                    out=scr_s[k][:, g * 1024:(g + 1) * 1024], in_=sg[:, :]), 16)
                sg_st[(k, g)] = cS.mark()
        barrier()
        stA2.close()
        if upto == "ph2a":
            st1.close()
            _finish(plan, SY, ALLC)
            _emit_block(nc, st0, plan)
            return nc

        stB = contextlib.ExitStack()
        N_sb = stB.enter_context(nc.sbuf_tensor("N_sb2", [128, INTER], F32))
        tmpNb = stB.enter_context(nc.sbuf_tensor("tmpNb", [128, INTER], BF16))
        tmpX = stB.enter_context(nc.sbuf_tensor("tmpX2", [128, INTER], BF16))
        tmpS = stB.enter_context(nc.sbuf_tensor("tmpS", [128, INTER], BF16))
        ybuf = stB.enter_context(nc.sbuf_tensor("ybuf", [128, INTER], F32))
        fill = stB.enter_context(nc.sbuf_tensor("fill", [128, 2048], F32))
        lmat = stB.enter_context(nc.sbuf_tensor("lmat", [128, 2048], BF16))
        g2bf = stB.enter_context(nc.sbuf_tensor("g2bf2", [128, 1024], BF16))
        xdec = stB.enter_context(nc.sbuf_tensor("xdec2", [128, 1024], BF16))
        gtmp = stB.enter_context(nc.sbuf_tensor("gtmp", [128, T], F32))
        dflat_b = stB.enter_context(nc.sbuf_tensor("dflat_b", [1, 2048], BF16))
        gout = stB.enter_context(nc.sbuf_tensor("gout", [128, T], BF16))
        t2T = stB.enter_context(nc.sbuf_tensor("t2T", [128, T], BF16))
        bse = stB.enter_context(nc.sbuf_tensor("bse2", [128, T], BF16))
        qar = stB.enter_context(nc.sbuf_tensor("qar", [128, 32], F32))
        qar_bf = stB.enter_context(nc.sbuf_tensor("qar_bf", [128, 32], BF16))
        rwt_sb = stB.enter_context(nc.sbuf_tensor("rwt_sb", [128, 256], BF16))
        route_col = stB.enter_context(nc.sbuf_tensor("route_col", [128, 1], F32))
        lg_sb = stB.enter_context(nc.sbuf_tensor("lg_sb", [1, 8], F32))
        mx_sb = stB.enter_context(nc.sbuf_tensor("mx_sb", [1, 1], F32))
        s1_sb = stB.enter_context(nc.sbuf_tensor("s1_sb", [1, 1], F32))
        r1_sb = stB.enter_context(nc.sbuf_tensor("r1_sb", [1, 1], F32))
        acsT = stB.enter_context(nc.sbuf_tensor("acsT", [128, 128], F32))
        sdoT = stB.enter_context(nc.sbuf_tensor("sdoT", [128, 128], F32))
        dtT = stB.enter_context(nc.sbuf_tensor("dtT", [128, 128], F32))
        dexp_c = stB.enter_context(nc.sbuf_tensor("dexp_c2", [128, 1], F32))
        dxrow_b = stB.enter_context(nc.sbuf_tensor("dxrow_b", [128, 64], F32))
        rms_v = stB.enter_context(nc.sbuf_tensor("rms_v", [128, 8], F32))
        rms_r = stB.enter_context(nc.sbuf_tensor("rms_r", [128, 8], F32))
        pT0 = stB.enter_context(nc.psum_tensor("pT0e", [128, 128], BF16))
        pT1 = stB.enter_context(nc.psum_tensor("pT1e", [128, 128], BF16))
        pTf = stB.enter_context(nc.psum_tensor("pTfe", [128, 128], F32))
        pQ0 = stB.enter_context(nc.psum_tensor("pQ0e", [128, 512], F32))
        pQ1 = stB.enter_context(nc.psum_tensor("pQ1e", [128, 512], F32))
        pYd = stB.enter_context(nc.psum_tensor("pYde", [128, 1024], F32))

        run(GP, cG, lambda e: e.dma_start(out=rwt_sb[:, :], in_=rwt_s), 16)
        rwt_ld = cG.mark()
        run(SY, cS, lambda e: e.dma_start(
            out=qar[:, :],
            in_=scr_q.rearrange("a (p c) -> (a p) c", p=128)), 16)
        qar_ld = cS.mark()
        VE.wait(cS, qar_ld)
        run(VE, cV, lambda e: e.tensor_copy(qar_bf[:, :], qar[:, :]))
        PE.wait(cV, cV.n)
        PE.wait(cG, rwt_ld)
        for dc in range(32):
            run(PE, cPE, lambda e, dc=dc: e.matmul(
                pQ0[0:1, 0:8], qar_bf[:, dc:dc + 1], rwt_sb[:, dc * 8:(dc + 1) * 8],
                start=(dc == 0), stop=(dc == 31), skip_group_check=True))
        VE.wait(cPE, cPE.n)
        run(VE, cV, lambda e: e.scalar_tensor_tensor(
            lg_sb[:, :], pQ0[0:1, 0:8], 1.0 / QWIN, rb_sb[:, :], ALU.mult, ALU.add))
        VE.wait(cV, cV.n)
        run(VE, cV, lambda e: e.reduce_max(mx_sb[:, :], lg_sb[:, :], axis=AX.X))
        VE.wait(cV, cV.n)
        run(VE, cV, lambda e: e.tensor_scalar(
            lg_sb[:, :], lg_sb[:, :], mx_sb[0:1, 0:1], None, ALU.subtract))
        SC.wait(cV, cV.n)
        run(SC, cSc, lambda e: e.activation(lg_sb[:, :], lg_sb[:, :], AF.Exp))
        VE.wait(cSc, cSc.n)
        run(VE, cV, lambda e: e.reduce_sum(s1_sb[:, :], lg_sb[:, :], axis=AX.X))
        VE.wait(cV, cV.n)
        run(VE, cV, lambda e: e.reciprocal(r1_sb[:, :], s1_sb[:, :]))
        VE.wait(cV, cV.n)
        run(VE, cV, lambda e: e.tensor_scalar(
            lg_sb[:, :], lg_sb[:, :], r1_sb[0:1, 0:1], SCALING, ALU.mult, ALU.mult))
        SY.wait(cV, cV.n)
        run(SY, cS, lambda e: e.dma_start(out=scr_route, in_=lg_sb[:, :]), 16)
        SY.wait(cS, cS.n)
        for ee in range(8):
            run(SY, cS, lambda e, ee=ee: e.dma_start(
                out=route_col[ee * 16:(ee + 1) * 16, 0:1],
                in_=scr_route[0:1, ee:ee + 1].broadcast_to((16, 1))), 16)
        VE.wait(cS, cS.n)
        run(VE, cV, lambda e: e.tensor_scalar(
            t2T[:, :], t2raw[:, :], route_col[:, 0:1], None, ALU.mult))
        t2T_v = cV.n
        PE.wait(cV, t2T_v)
        for ot in range(32):
            if ot > 0:
                SY.wait(cV, g_ev)
            run(SY, cS, lambda e, ot=ot: e.dma_start(out=bse[:, :], in_=scr_base[ot]), 16)
            bse_ld = cS.mark()
            if ot > 0:
                PE.wait(cV, g_ev)
            for blk in range(2):
                pq = pQ0 if blk == 0 else pQ1
                run(PE, cPE, lambda e, ot=ot, blk=blk, pq=pq: e.matmul(
                    pq[:, :], bflat_sb[:, ot * 128:(ot + 1) * 128],
                    t2T[:, blk * 512:(blk + 1) * 512],
                    start=True, stop=True, skip_group_check=True))
            lora_pe = cPE.n
            VE.wait(cPE, lora_pe)
            VE.wait(cS, bse_ld)
            if ot > 0:
                VE.wait(cSc, g_sil)
            run(VE, cV, lambda e: e.tensor_tensor(gtmp[:, 0:512], bse[:, 0:512], pQ0[:, :], ALU.add))
            run(VE, cV, lambda e: e.tensor_tensor(gtmp[:, 512:1024], bse[:, 512:1024], pQ1[:, :], ALU.add))
            g_ev = cV.n
            SC.wait(cV, g_ev)
            if ot > 0:
                SC.wait(cS2, gg_st)
            run(SC, cSc, lambda e, ot=ot: e.activation(
                gout[:, :], gtmp[:, :], AF.Silu, bias=fbias_sb[:, ot:ot + 1]))
            g_sil = cSc.n
            SC.wait(cSc, g_sil)
            run(SC, cS2, lambda e, ot=ot: e.dma_start(
                out=scr_gg[ot * 128:(ot + 1) * 128, :], in_=gout[:, :]), 16)
            gg_st = cS2.mark()
        if upto == "ph15b":
            barrier()
            stB.close(); st1.close()
            _finish(plan, SY, ALLC)
            _emit_block(nc, st0, plan)
            return nc

        run(SY, cS, lambda e: e.dma_start(out=N_sb[:, :], in_=scr_nc), 16)
        nc_ld = cS.mark()
        VE.wait(cS, nc_ld)
        n_v = cV.n
        y_v = 0
        nrm_st = 0
        for k in range(NCHUNK):
            SY.wait(cV, y_v)
            if k > 0:
                SY.wait(cS, nrm_st)
            run(SY, cS, lambda e, k=k: e.dma_start(out=tmpX[:, :], in_=scr_hs[k]), 16)
            hs_ld = cS.mark()
            SY.wait(cV, nupd2_v if k > 0 else 0)
            if k > 0:
                SY.wait(cPE, tr_pe)
            run(SY, cS, lambda e, k=k: e.dma_start(out=tmpS[:, :], in_=scr_s[k]), 16)
            s_ld = cS.mark()
            run(SY, cS, lambda e, k=k: e.dma_start(out=g2bf[:, :], in_=scr_g2[k]), 16)
            g2_ld = cS.mark()
            VE.wait(cV, n_v)
            if k > 0:
                VE.wait(cPE, yoff_pe)
            run(VE, cV, lambda e: e.tensor_copy(tmpNb[:, :], N_sb[:, :]))
            nb_v = cV.n
            SC.wait(cSc, cSc.n)
            SC.wait(cS2, cS2.n)
            run(SC, cSc, lambda e, k=k: e.activation(
                dexp_c[:, :], acs_sb[:, k * 128 + 127: k * 128 + 128], AF.Exp))
            VE.wait(cSc, cSc.n)
            VE.wait(cS2, cS2.n)
            run(VE, cV, lambda e: e.tensor_copy(
                dxrow_b[:, :].rearrange("h (a u) -> h a u", u=64),
                dexp_c[:, 0:1].rearrange("h (a u) -> h a u", u=1).broadcast_to((128, 1, 64))))
            SC.wait(cV, cV.n)
            run(SC, cS2, lambda e, k=k: e.dma_start(
                out=scr_dexp[k % 2:k % 2 + 1, :].rearrange("a (h u) -> h (a u)", h=128, u=64),
                in_=dxrow_b[:, :]), 16)
            SC.wait(cS2, cS2.n)
            run(SC, cS2, lambda e, k=k: e.dma_start(
                out=scr_dexp2[k % 2:k % 2 + 1, :].rearrange(
                    "a (g i p) -> a g i p", g=8, i=16, p=64),
                in_=scr_dexp[k % 2:k % 2 + 1, :].rearrange(
                    "a (i g p) -> a g i p", i=16, g=8, p=64)), 16)
            hop2_done = cS2.mark()
            nmul2_v = nb_v
            for hf in range(4):
                GP.wait(cS2, hop2_done)
                if k > 0 or hf > 0:
                    GP.wait(cPE, dfl2_pe)
                run(GP, cG, lambda e, k=k, hf=hf: e.dma_start(
                    out=dflat_b[0:1, :],
                    in_=scr_dexp2[k % 2:k % 2 + 1, hf * 2048:(hf + 1) * 2048]), 16)
                dfl2_ld = cG.mark()
                PE.wait(cG, dfl2_ld)
                PE.wait(cV, nb_v)
                for sg8 in range(4):
                    pb = pQ0 if sg8 % 2 == 0 else pQ1
                    if k > 0 or hf > 0 or sg8 >= 2:
                        PE.wait(cV, nmul2_v)
                    run(PE, cPE, lambda e, sg8=sg8, pb=pb: e.matmul(
                        pb[:, :], ones_b1[0:1, :], dflat_b[0:1, sg8 * 512:(sg8 + 1) * 512],
                        start=True, stop=True, skip_group_check=True))
                    VE.wait(cPE, cPE.n)
                    off = hf * 2048 + sg8 * 512
                    run(VE, cV, lambda e, off=off, pb=pb: e.tensor_tensor(
                        N_sb[:, off:off + 512], N_sb[:, off:off + 512], pb[:, :],
                        ALU.mult))
                    nmul2_v = cV.n
                dfl2_pe = cPE.n
            VE.wait(cV, nmul2_v)
            VE.wait(cS, s_ld)
            run(VE, cV, lambda e: e.tensor_tensor(N_sb[:, :], N_sb[:, :], tmpS[:, :], ALU.add))
            nupd2_v = cV.n
            n_v = nupd2_v
            SY.wait(cV, nupd2_v)
            run(SY, cS, lambda e, k=k: e.dma_start(out=tmpS[:, :], in_=scr_g[k]), 16)
            gate_ld = cS.mark()
            PE.wait(cSc, cSc.n)
            if k > 0:
                PE.wait(cV, dtT_v)
            run(PE, cPE, lambda e, k=k: e.transpose(
                pTf[:, :], acs_sb[:, k * 128:(k + 1) * 128], identf_sb[:, :]))
            VE.wait(cPE, cPE.n)
            VE.wait(cSc, sdo_s if k > 0 else 0)
            run(VE, cV, lambda e: e.tensor_copy(acsT[:, :], pTf[:, :]))
            acsT_v = cV.n
            PE.wait(cV, acsT_v)
            run(PE, cPE, lambda e, k=k: e.transpose(
                pTf[:, :], dt_sb[:, k * 128:(k + 1) * 128], identf_sb[:, :]))
            VE.wait(cPE, cPE.n)
            run(VE, cV, lambda e: e.tensor_copy(dtT[:, :], pTf[:, :]))
            dtT_v = cV.n
            SC.wait(cV, acsT_v)
            run(SC, cSc, lambda e: e.activation(sdoT[:, :], acsT[:, :], AF.Exp))
            sdo_s = cSc.n
            if k >= 2:
                SC.wait(cG, acs_rd)
            run(SC, cS2, lambda e, k=k: e.dma_start(
                out=scr_acs[k % 2:k % 2 + 1, :].rearrange("a (h l) -> h (a l)", h=128),
                in_=acs_sb[:, k * 128:(k + 1) * 128]), 16)
            SC.wait(cS2, cS2.n)
            run(SC, cS2, lambda e, k=k: e.dma_start(
                out=scr_acs2[k % 2:k % 2 + 1, :].rearrange(
                    "a (g i l) -> a g i l", g=8, i=16, l=128),
                in_=scr_acs[k % 2:k % 2 + 1, :].rearrange(
                    "a (i g l) -> a g i l", i=16, g=8, l=128)), 16)
            acs_stg = cS2.mark()
            acs_rd_new = None
            for g in range(G):
                VE.wait(cSc, cSc.n)
                if k > 0 or g > 0:
                    VE.wait(cG, acs_rd_prev)
                    VE.wait(cPE, yd_pe)
                run(VE, cV, lambda e, g=g: e.scalar_tensor_tensor(
                    fill[:, :].rearrange("s (i l) -> s i l", i=16),
                    acsT[:, :].rearrange("s (i g) -> s i g", g=8)[:, :, g:g + 1
                        ].broadcast_to((128, 16, 128)),
                    -1.0,
                    maskf_sb[:, :].rearrange("s (o l) -> s o l", o=1
                        ).broadcast_to((128, 16, 128)),
                    ALU.mult, ALU.add))
                fill_v = cV.n
                GP.wait(cV, fill_v)
                GP.wait(cS2, acs_stg)
                run(GP, cG, lambda e, k=k, g=g: e.dma_start(
                    out=fill[:, :],
                    in_=scr_acs2[k % 2:k % 2 + 1, g * 2048:(g + 1) * 2048
                        ].broadcast_to((128, 2048)),
                    accum_op=ALU.add), 16)
                acs_rd_prev = cG.mark()
                if g == G - 1:
                    acs_rd = cG.mark()
                SC.wait(cG, acs_rd_prev)
                SC.wait(cV, fill_v)
                run(SC, cSc, lambda e: e.activation(lmat[:, :], fill[:, :], AF.Exp))
                lm_s = cSc.n
                VE.wait(cSc, lm_s)
                VE.wait(cS, g2_ld)
                run(VE, cV, lambda e, g=g: e.tensor_tensor(
                    lmat[:, :].rearrange("s (i l) -> s i l", i=16),
                    lmat[:, :].rearrange("s (i l) -> s i l", i=16),
                    g2bf[:, g * 128:(g + 1) * 128].rearrange("s (o l) -> s o l", o=1
                        ).broadcast_to((128, 16, 128)),
                    ALU.mult))
                m_v = cV.n
                VE.wait(cS, hs_ld)
                VE.wait(cV, dtT_v)
                if k > 0 or g > 0:
                    VE.wait(cPE, yd_pe)
                for i in range(16):
                    h = 8 * i + g
                    run(VE, cV, lambda e, h=h, i=i: e.tensor_scalar(
                        xdec[:, i * 64:(i + 1) * 64], tmpX[:, h * 64:(h + 1) * 64],
                        dtT[:, h:h + 1], None, ALU.mult))
                xd_v = cV.n
                PE.wait(cV, m_v)
                PE.wait(cV, xd_v)
                PE.wait(cV, nb_v)
                if k > 0 or g > 0:
                    PE.wait(cV, yev_v)
                for i in range(16):
                    run(PE, cPE, lambda e, i=i: e.matmul(
                        pYd[:, i * 64:(i + 1) * 64], lmat[:, i * 128:(i + 1) * 128],
                        xdec[:, i * 64:(i + 1) * 64],
                        start=True, stop=True, skip_group_check=True))
                for blk in range(2):
                    pq = pQ0 if blk == 0 else pQ1
                    run(PE, cPE, lambda e, g=g, k=k, blk=blk, pq=pq: e.matmul(
                        pq[:, :], C_sb[:, g * T + k * 128: g * T + (k + 1) * 128],
                        tmpNb[:, g * 1024 + blk * 512: g * 1024 + (blk + 1) * 512],
                        start=True, stop=True, skip_group_check=True))
                yd_pe = cPE.n
                yoff_pe = cPE.n
                VE.wait(cPE, yd_pe)
                VE.wait(cSc, sdo_s)
                for blk in range(2):
                    pq = pQ0 if blk == 0 else pQ1
                    run(VE, cV, lambda e, g=g, blk=blk, pq=pq: e.tensor_tensor(
                        ybuf[:, :].rearrange("l (i g p) -> l i g p", g=8, p=64)[
                            :, blk * 8:(blk + 1) * 8, g:g + 1, :],
                        pq[:, :].rearrange("l (i u p) -> l i u p", u=1, p=64),
                        sdoT[:, :].rearrange("l (i g) -> l i g", g=8)[
                            :, blk * 8:(blk + 1) * 8, g:g + 1].rearrange(
                            "l i (g u) -> l i g u", u=1).broadcast_to((128, 8, 1, 64)),
                        ALU.mult))
                    VE.wait(cV, cV.n)
                    run(VE, cV, lambda e, g=g, blk=blk: e.tensor_tensor(
                        ybuf[:, :].rearrange("l (i g p) -> l i g p", g=8, p=64)[
                            :, blk * 8:(blk + 1) * 8, g:g + 1, :],
                        ybuf[:, :].rearrange("l (i g p) -> l i g p", g=8, p=64)[
                            :, blk * 8:(blk + 1) * 8, g:g + 1, :],
                        pYd[:, :].rearrange("l (i u p) -> l i u p", u=1, p=64)[
                            :, blk * 8:(blk + 1) * 8, :, :],
                        ALU.add))
                    VE.wait(cV, cV.n)
                yev_v = cV.n
            VE.wait(cV, yev_v)
            VE.wait(cPE, yd_pe)
            for sblk in range(8):
                run(VE, cV, lambda e, sblk=sblk: e.tensor_tensor(
                    xdec[:, :].rearrange("l (hh p) -> l hh p", p=64),
                    tmpX[:, sblk * 1024:(sblk + 1) * 1024].rearrange("l (hh p) -> l hh p", p=64),
                    d128_sb[:, sblk * 16:(sblk + 1) * 16].rearrange("l (hh o) -> l hh o", o=1
                        ).broadcast_to((128, 16, 64)),
                    ALU.mult))
                VE.wait(cV, cV.n)
                run(VE, cV, lambda e, sblk=sblk: e.tensor_tensor(
                    ybuf[:, sblk * 1024:(sblk + 1) * 1024],
                    ybuf[:, sblk * 1024:(sblk + 1) * 1024], xdec[:, :], ALU.add))
                VE.wait(cV, cV.n)
            VE.wait(cS, gate_ld)
            run(VE, cV, lambda e: e.tensor_tensor(ybuf[:, :], ybuf[:, :], tmpS[:, :], ALU.mult))
            VE.wait(cV, cV.n)
            for g8 in range(8):
                run(VE, cV, lambda e, g8=g8: e.tensor_tensor(
                    xdec[:, :], ybuf[:, g8 * 1024:(g8 + 1) * 1024],
                    ybuf[:, g8 * 1024:(g8 + 1) * 1024], ALU.mult))
                VE.wait(cV, cV.n)
                run(VE, cV, lambda e, g8=g8: e.reduce_sum(
                    rms_v[:, g8:g8 + 1], xdec[:, :], axis=AX.X))
                VE.wait(cV, cV.n)
            SC.wait(cV, cV.n)
            run(SC, cSc, lambda e: e.activation(rms_r[:, :], rms_v[:, :], AF.Ln,
                                                scale=1.0 / 1024.0, bias=beps[:, 0:1]))
            SC.wait(cSc, cSc.n)
            run(SC, cSc, lambda e: e.activation(rms_r[:, :], rms_r[:, :], AF.Exp, scale=-0.5))
            rms_s = cSc.n
            VE.wait(cSc, rms_s)
            for g8 in range(8):
                run(VE, cV, lambda e, g8=g8: e.tensor_scalar(
                    tmpS[:, g8 * 1024:(g8 + 1) * 1024], ybuf[:, g8 * 1024:(g8 + 1) * 1024],
                    rms_r[:, g8:g8 + 1], None, ALU.mult))
            yn_v = cV.n
            th, t0 = k // 4, (k % 4) * 128
            PE.wait(cV, yn_v)
            for j in range(64):
                pt = pT0 if j % 2 == 0 else pT1
                run(PE, cPE, lambda e, j=j, pt=pt: e.transpose(
                    pt[:, :], tmpS[:, j * 128:(j + 1) * 128], ident_sb[:, :]))
                VE.wait(cPE, cPE.n)
                run(VE, cV, lambda e, j=j, pt=pt: e.tensor_scalar(
                    tmpX[:, j * 128:(j + 1) * 128], pt[:, :], nwt_sb[:, j:j + 1],
                    None, ALU.mult))
                PE.wait(cV, cV.n)
            y_v = cV.n
            tr_pe = cPE.n
            SY.wait(cV, y_v)
            run(SY, cS, lambda e, th=th, t0=t0: e.dma_start(
                out=scr_n[th][:, :, t0:t0 + 128].rearrange("j p t -> p j t"),
                in_=tmpX[:, :].rearrange("p (j t) -> p j t", j=64)), 16)
            nrm_st = cS.mark()
        barrier()
        stB.close()
        st1.close()
        if upto == "ph2b":
            _finish(plan, SY, ALLC)
            _emit_block(nc, st0, plan)
            return nc

        stC = contextlib.ExitStack()
        wbuf0_c = stC.enter_context(nc.sbuf_tensor("wbc0", [128, 64 * 128], BF16))
        wbuf1_c = stC.enter_context(nc.sbuf_tensor("wbc1", [128, 64 * 128], BF16))
        nrm_sb = stC.enter_context(nc.sbuf_tensor("nrm_sb", [128, 64 * 512], BF16))
        gld = stC.enter_context(nc.sbuf_tensor("gld", [128, 512], BF16))
        sg_sb = stC.enter_context(nc.sbuf_tensor("sg_sb", [128, 512], BF16))
        rb4 = stC.enter_context(nc.sbuf_tensor("rb4", [128, 4 * HID], BF16))
        obuf = stC.enter_context(nc.sbuf_tensor("obuf", [128, HID], F32))
        xtc_sb = stC.enter_context(nc.sbuf_tensor("xt_c", [128, HID], F32))
        pT0_c = stC.enter_context(nc.psum_tensor("pT0f", [128, 128], BF16))
        pT1_c = stC.enter_context(nc.psum_tensor("pT1f", [128, 128], BF16))
        pQ0_c = stC.enter_context(nc.psum_tensor("pQ0f", [128, 512], F32))
        pQ1_c = stC.enter_context(nc.psum_tensor("pQ1f", [128, 512], F32))

        oslab_pe = {}
        for th in range(2):
            VE.wait(cV, cV.n)
            SY.wait(cV, cV.n)
            SY.wait(cPE, cPE.n)
            run(SY, cS, lambda e, th=th: e.dma_start(
                out=nrm_sb[:, :].rearrange("p (j t) -> p j t", j=64),
                in_=scr_n[th].rearrange("j p t -> p j t")), 16)
            nrm_ld = cS.mark()
            for ot in range(32):
                wb = wbuf0_c if ot % 2 == 0 else wbuf1_c
                key, prev_key = (th, ot), None
                if ot >= 2:
                    prev_key = (th, ot - 2)
                elif th > 0:
                    prev_key = (th - 1, 30 + ot)
                if prev_key:
                    GP.wait(cPE, oslab_pe[prev_key])
                ctr = cG0 if ot % 2 == 0 else cG1
                run(GP, ctr, lambda e, wb=wb, ot=ot: e.dma_start(
                    out=wb[:, :], in_=w_out_s[ot]), 16)
                w_ld = (ctr, ctr.mark())
                if th > 0 or ot > 0:
                    SC.wait(cV, sg_v)
                run(SC, cS2, lambda e, th=th, ot=ot: e.dma_start(
                    out=gld[:, :], in_=scr_gg[ot * 128:(ot + 1) * 128,
                                             th * 512:(th + 1) * 512]), 16)
                gld_ld = cS2.mark()
                PE.wait(*w_ld)
                PE.wait(cS, nrm_ld)
                pq = pQ0_c if ot % 2 == 0 else pQ1_c
                if prev_key:
                    PE.wait(cV, sg_prev.get(prev_key, 0))
                for cc in range(64):
                    run(PE, cPE, lambda e, wb=wb, cc=cc, pq=pq: e.matmul(
                        pq[:, :], wb[:, cc * 128:(cc + 1) * 128],
                        nrm_sb[:, cc * 512:(cc + 1) * 512],
                        start=(cc == 0), stop=(cc == 63), skip_group_check=True))
                if th == 0 and ot == 0:
                    sg_prev = {}
                oslab_pe[key] = cPE.n
                VE.wait(cPE, cPE.n)
                VE.wait(cS2, gld_ld)
                if th > 0 or ot > 0:
                    VE.wait(cPE, sgT_pe)
                run(VE, cV, lambda e, pq=pq: e.tensor_tensor(
                    sg_sb[:, :], pq[:, :], gld[:, :], ALU.mult))
                sg_v = cV.n
                sg_prev[key] = sg_v
                PE.wait(cV, sg_v)
                for tc in range(4):
                    pt = pT0_c if tc % 2 == 0 else pT1_c
                    run(PE, cPE, lambda e, tc=tc, pt=pt: e.transpose(
                        pt[:, :], sg_sb[:, tc * 128:(tc + 1) * 128], ident_sb[:, :]))
                    VE.wait(cPE, cPE.n)
                    run(VE, cV, lambda e, tc=tc, ot=ot, pt=pt: e.tensor_copy(
                        rb4[:, tc * HID + ot * 128: tc * HID + (ot + 1) * 128], pt[:, :]))
                    PE.wait(cV, cV.n)
                sgT_pe = cPE.n
            for tc in range(4):
                SY.wait(cV, cV.n)
                run(SY, cS, lambda e, th=th, tc=tc: e.dma_start(
                    out=xtc_sb[:, :],
                    in_=x_ext[3 + th * 512 + tc * 128: 3 + th * 512 + (tc + 1) * 128, :]), 16)
                VE.wait(cS, cS.n)
                run(VE, cV, lambda e, tc=tc: e.tensor_tensor(
                    obuf[:, :], xtc_sb[:, :], rb4[:, tc * HID:(tc + 1) * HID], ALU.add))
                SY.wait(cV, cV.n)
                run(SY, cS, lambda e, th=th, tc=tc: e.dma_start(
                    out=out[th * 512 + tc * 128: th * 512 + (tc + 1) * 128, :],
                    in_=obuf[:, :]), 16)
        barrier()
        stC.close()
        _finish(plan, SY, ALLC)
        _emit_block(nc, st0, plan)
    return nc


def barrier_no_ar(plan, SY, GP, PE, VE, SC, ALLC, cGc):
    for Eo in (SY, GP, PE, VE, SC):
        for c in ALLC:
            if c is cGc:
                continue
            Eo.wait(c, c.n)


def _finish(plan, SY, ALLC):
    for c in ALLC:
        SY.wait(c, c.n)


def _emit_block(nc, st0, plan):
    blk = st0.enter_context(nc.Block())

    @blk.sync
    def _(eng):
        for fn in plan["sync"]:
            fn(eng)

    @blk.gpsimd
    def _(eng):
        for fn in plan["gpsimd"]:
            fn(eng)

    @blk.tensor
    def _(eng):
        for fn in plan["tensor"]:
            fn(eng)

    @blk.vector
    def _(eng):
        for fn in plan["vector"]:
            fn(eng)

    @blk.scalar
    def _(eng):
        for fn in plan["scalar"]:
            fn(eng)


def stage_inputs(inputs, n_cores=8):
    f32 = np.float32
    x = np.asarray(inputs["L0_attn_hidden_states"], f32)
    w_in = np.asarray(inputs["L0_attn_in_proj_w"], f32)
    conv_w = np.asarray(inputs["L0_attn_conv_w"], f32)
    conv_b = np.asarray(inputs["L0_attn_conv_b"], f32)
    dt_bias = np.asarray(inputs["L0_attn_dt_bias"], f32)
    a_log = np.asarray(inputs["L0_attn_A_log"], f32)
    D = np.asarray(inputs["L0_attn_D"], f32)
    nw = np.asarray(inputs["L0_attn_norm_weight"], f32)
    w_out = np.asarray(inputs["L0_attn_out_proj_w"], f32)
    ffn_W = np.asarray(inputs["L0_ffn_W"], f32)
    ffn_b = np.asarray(inputs["L0_ffn_b"], f32)
    ffn_A = np.asarray(inputs["L0_ffn_A"], f32)
    ffn_B = np.asarray(inputs["L0_ffn_B"], f32)
    r_W = np.asarray(inputs["L0_ffn_router_W"], f32)
    r_b = np.asarray(inputs["L0_ffn_router_b"], f32)

    w_in_s = np.ascontiguousarray(
        w_in.T.reshape(32, 128, NFT, 128).transpose(2, 1, 0, 3).reshape(NFT, 128, 32 * 128))
    w_out_s = np.ascontiguousarray(
        w_out.T.reshape(64, 128, 32, 128).transpose(2, 1, 0, 3).reshape(32, 128, 64 * 128))
    w_ffn_s = np.ascontiguousarray(
        ffn_W.T.reshape(32, 128, 32, 128).transpose(2, 1, 0, 3).reshape(32, 128, 32 * 128))
    a_s = np.ascontiguousarray(
        ffn_A.reshape(128, HID).T.reshape(32, 128, 128).transpose(1, 0, 2).reshape(128, 32 * 128))
    rwt_s = np.ascontiguousarray(r_W.T.reshape(128, 32 * 8))

    shared = {
        "w_in_s": w_in_s,
        "w_out_s": w_out_s,
        "w_ffn_s": w_ffn_s,
        "a_s": a_s,
        "b_flat": np.ascontiguousarray(np.transpose(ffn_B, (0, 2, 1)).reshape(128, HID)),
        "rwt_s": rwt_s,
        "convw": np.ascontiguousarray(
            conv_w.reshape(80, 128, 4).transpose(1, 0, 2).reshape(128, 320)),
        "convb": np.ascontiguousarray(conv_b.reshape(80, 128).T),
        "dt_bias": np.ascontiguousarray(dt_bias.reshape(128, 1)),
        "a_log": np.ascontiguousarray(a_log.reshape(128, 1)),
        "d128": np.ascontiguousarray(np.tile(D[None, :], (128, 1))),
        "nwt": np.ascontiguousarray(nw.reshape(64, 128).T),
        "router_b": np.ascontiguousarray(r_b.reshape(1, 8)),
        "ffn_bias": np.ascontiguousarray(ffn_b.reshape(32, 128).T),
        "ident": np.eye(128, dtype=ml_dtypes.bfloat16),
        "identf": np.eye(128, dtype=f32),
        "maskfill": np.where(np.arange(128)[None, :] >= np.arange(128)[:, None],
                             0.0, -1e30).astype(f32),
    }
    def half_ext(b, half):
        t0 = half * T
        x_e = np.zeros((TEXT, HID), f32)
        x_e[3:3 + T] = x[b, t0:t0 + T]
        if half == 1:
            x_e[0:3] = x[b, t0 - 3:t0]
        r = np.arange(TEXT)
        gtok = t0 + r - 3
        qm = ((r >= 3) & (r < 3 + T) & (gtok >= 611) & (gtok <= 2046)).astype(f32)
        qm_t = np.zeros((128, 9), f32)
        for ti in range(9):
            qm_t[:, ti] = qm[ti * 128:(ti + 1) * 128]
        return x_e, qm_t

    in_maps = []
    for c in range(n_cores):
        b, half = c // 2, c % 2
        x_e, qm_t = half_ext(b, half)
        x_o, qm2_t = half_ext(b, 1 - half)
        d = dict(shared)
        d["x"] = x_e
        d["x_oth"] = x_o
        d["qmask"] = qm_t.astype(ml_dtypes.bfloat16)
        d["qmask2"] = qm2_t.astype(ml_dtypes.bfloat16)
        d["isodd"] = np.full((128, 1), float(half), f32)
        d["notodd"] = np.full((128, 1), float(1 - half), f32)
        in_maps.append(d)
    return in_maps


def kernel(**inputs):
    in_maps = stage_inputs(inputs, 8)
    nc = build_nc(REPLICA_GROUPS, upto="full")
    res = run_bass_kernel_spmd(nc, in_maps, core_ids=list(range(8)))
    o = np.zeros((4, 2048, HID), np.float32)
    for c in range(8):
        b, half = c // 2, c % 2
        o[b, half * T:(half + 1) * T] = res.results[c]["out"]
    return o


# version 05
# speedup vs baseline: 1.0039x; 1.0039x over previous
import contextlib
import numpy as np
import ml_dtypes

import concourse.bass as bass
import concourse.mybir as mybir
from concourse.bass_utils import run_bass_kernel_spmd

F32 = mybir.dt.float32
BF16 = mybir.dt.bfloat16
AF = mybir.ActivationFunctionType
ALU = mybir.AluOpType
AX = mybir.AxisListType

HID = 4096
H = 128
P = 64
NST = 128
G = 8
CH = 128
INTER = 8192
CONV_DIM = 10240
PROJ = 18560
T = 1024
TEXT = 1152
NTOK = 1027
MT = 259
MTOK = 256
XNW = 1286
NCHUNK = 8
NFT = 145
FT_HBC0 = 64
FT_DT = 144
EPS = 1e-5
SCALING = 2.0
QWIN = 1436.0

REPLICA_GROUPS = [[0, 1], [2, 3], [4, 5], [6, 7]]


class Ctr:
    def __init__(self, st, nc, name):
        self.sem = st.enter_context(nc.semaphore(name))
        self.n = 0
        self.fence = 0
        self.fence_waited = 0

    def mark(self):
        self.fence = max(self.fence, self.n)
        return self.n


def build_nc(replica_groups, upto="full", dbg=False):
    nc = bass.Bass("TRN2", target_bir_lowering=False, debug=False)
    kdbg = "ExternalOutput" if dbg else "Internal"

    x_ext = nc.dram_tensor("x", [TEXT, HID], F32, kind="ExternalInput").ap()
    x_oth = nc.dram_tensor("x_oth", [TEXT, HID], F32, kind="ExternalInput").ap()
    w_in_s = nc.dram_tensor("w_in_s", [NFT, 128, 32 * 128], F32, kind="ExternalInput").ap()
    w_out_s = nc.dram_tensor("w_out_s", [32, 128, 64 * 128], F32, kind="ExternalInput").ap()
    w_ffn_s = nc.dram_tensor("w_ffn_s", [32, 128, 32 * 128], F32, kind="ExternalInput").ap()
    a_s = nc.dram_tensor("a_s", [128, 32 * 128], F32, kind="ExternalInput").ap()
    b_flat = nc.dram_tensor("b_flat", [128, HID], F32, kind="ExternalInput").ap()
    rwt_s = nc.dram_tensor("rwt_s", [128, 32 * 8], F32, kind="ExternalInput").ap()
    convw = nc.dram_tensor("convw", [128, 320], F32, kind="ExternalInput").ap()
    convb = nc.dram_tensor("convb", [128, 80], F32, kind="ExternalInput").ap()
    dt_bias = nc.dram_tensor("dt_bias", [128, 1], F32, kind="ExternalInput").ap()
    a_log = nc.dram_tensor("a_log", [128, 1], F32, kind="ExternalInput").ap()
    d128 = nc.dram_tensor("d128", [128, 128], F32, kind="ExternalInput").ap()
    nwt = nc.dram_tensor("nwt", [128, 64], F32, kind="ExternalInput").ap()
    router_b = nc.dram_tensor("router_b", [1, 8], F32, kind="ExternalInput").ap()
    ffn_bias = nc.dram_tensor("ffn_bias", [128, 32], F32, kind="ExternalInput").ap()
    ident = nc.dram_tensor("ident", [128, 128], BF16, kind="ExternalInput").ap()
    identf = nc.dram_tensor("identf", [128, 128], F32, kind="ExternalInput").ap()
    maskfill = nc.dram_tensor("maskfill", [128, 128], F32, kind="ExternalInput").ap()
    qmask = nc.dram_tensor("qmask", [128, 9], BF16, kind="ExternalInput").ap()
    qmask2 = nc.dram_tensor("qmask2", [128, 9], BF16, kind="ExternalInput").ap()
    isodd = nc.dram_tensor("isodd", [128, 1], F32, kind="ExternalInput").ap()
    notodd = nc.dram_tensor("notodd", [128, 1], F32, kind="ExternalInput").ap()
    out = nc.dram_tensor("out", [T, HID], F32, kind="ExternalOutput").ap()

    scr_hs = nc.dram_tensor("scr_hs", [NCHUNK, 128, INTER], BF16, kind=kdbg).ap()
    scr_g = nc.dram_tensor("scr_g", [NCHUNK, 128, INTER], BF16, kind=kdbg).ap()
    scr_s = nc.dram_tensor("scr_s", [NCHUNK, 128, INTER], BF16, kind=kdbg).ap()
    scr_g2 = nc.dram_tensor("scr_g2", [NCHUNK, 128, 1024], BF16, kind=kdbg).ap()
    scr_base = nc.dram_tensor("scr_base", [32, 128, T], BF16, kind=kdbg).ap()
    scr_n = nc.dram_tensor("scr_n", [2, 64, 128, 512], BF16, kind=kdbg).ap()
    scr_gg = nc.dram_tensor("scr_gg", [HID, T], BF16, kind=kdbg).ap()
    scr_acs = nc.dram_tensor("scr_acs", [2, 128 * 128], F32, kind=kdbg).ap()
    scr_dexp = nc.dram_tensor("scr_dexp", [2, INTER], F32, kind=kdbg).ap()
    scr_dexp2 = nc.dram_tensor("scr_dexp2", [2, INTER], F32, kind=kdbg).ap()
    scr_acs2 = nc.dram_tensor("scr_acs2", [2, 128 * 128], F32, kind=kdbg).ap()
    scr_route = nc.dram_tensor("scr_route", [1, 8], F32, kind=kdbg).ap()
    scr_q = nc.dram_tensor("scr_q", [1, HID], F32, kind=kdbg).ap()
    scr_hsm = nc.dram_tensor("scr_hsm", [2, 128, INTER], BF16, kind=kdbg).ap()
    scr_nc = nc.dram_tensor("scr_nc", [128, INTER], F32, kind=kdbg).ap()
    dbg_xnt = nc.dram_tensor("dbg_xnt", [128, 32 * XNW], BF16, kind=kdbg).ap() if dbg else None

    ENGS = ["sync", "gpsimd", "tensor", "vector", "scalar"]
    plan = {e: [] for e in ENGS}

    class E:
        def __init__(self, eng):
            self.eng = eng

        def do(self, fn):
            plan[self.eng].append(fn)

        def wait(self, ctr, val=None):
            v = ctr.n if val is None else val
            sem = ctr.sem
            if v > 0:
                ctr.fence = max(ctr.fence, v)
                self.do(lambda e: e.wait_ge(sem, v))

    SY, GP, PE, VE, SC = E("sync"), E("gpsimd"), E("tensor"), E("vector"), E("scalar")

    def run(eng_obj, ctr, fn, k=1):
        if k == 16 and ctr.fence > ctr.fence_waited:
            fv = ctr.fence
            sem = ctr.sem
            eng_obj.do(lambda e: e.wait_ge(sem, fv))
            ctr.fence_waited = fv
        if k == 1 and eng_obj.eng in ("vector", "scalar") and ctr.n > 0:
            sv = ctr.n
            sem = ctr.sem
            eng_obj.do(lambda e: e.wait_ge(sem, sv))

        def g(e):
            fn(e).then_inc(ctr.sem, k)
        eng_obj.do(g)
        ctr.n += k
        return ctr.n

    st0 = contextlib.ExitStack()
    with st0:
        ident_sb = st0.enter_context(nc.sbuf_tensor("ident_sb", [128, 128], BF16))
        identf_sb = st0.enter_context(nc.sbuf_tensor("identf_sb", [128, 128], F32))
        maskf_sb = st0.enter_context(nc.sbuf_tensor("maskf_sb", [128, 128], F32))
        convw_sb = st0.enter_context(nc.sbuf_tensor("convw_sb", [128, 320], F32))
        convb_sb = st0.enter_context(nc.sbuf_tensor("convb_sb", [128, 80], F32))
        dtb_sb = st0.enter_context(nc.sbuf_tensor("dtb_sb", [128, 1], F32))
        alog_sb = st0.enter_context(nc.sbuf_tensor("alog_sb", [128, 1], F32))
        acol_sb = st0.enter_context(nc.sbuf_tensor("acol_sb", [128, 1], F32))
        d128_sb = st0.enter_context(nc.sbuf_tensor("d128_sb", [128, 128], F32))
        nwt_sb = st0.enter_context(nc.sbuf_tensor("nwt_sb", [128, 64], F32))
        rb_sb = st0.enter_context(nc.sbuf_tensor("rb_sb", [1, 8], F32))
        fbias_sb = st0.enter_context(nc.sbuf_tensor("fbias_sb", [128, 32], F32))
        qm_sb = st0.enter_context(nc.sbuf_tensor("qm_sb", [128, 9], BF16))
        qm2_sb = st0.enter_context(nc.sbuf_tensor("qm2_sb", [128, 9], BF16))
        iso_sb = st0.enter_context(nc.sbuf_tensor("iso_sb", [128, 1], F32))
        noto_sb = st0.enter_context(nc.sbuf_tensor("noto_sb", [128, 1], F32))
        bone = st0.enter_context(nc.sbuf_tensor("bone", [128, 1], F32))
        beps = st0.enter_context(nc.sbuf_tensor("beps", [128, 1], F32))
        ones_f = st0.enter_context(nc.sbuf_tensor("ones_f", [128, 128], F32))
        ones_b1 = st0.enter_context(nc.sbuf_tensor("ones_b1", [1, 128], BF16))

        cS = Ctr(st0, nc, "cS")
        cG = Ctr(st0, nc, "cG")
        cG0 = Ctr(st0, nc, "cG0")
        cG1 = Ctr(st0, nc, "cG1")
        cGc = Ctr(st0, nc, "cGc")
        cS2 = Ctr(st0, nc, "cS2")
        cPE = Ctr(st0, nc, "cPE")
        cV = Ctr(st0, nc, "cV")
        cSc = Ctr(st0, nc, "cSc")
        ALLC = [cS, cG, cG0, cG1, cGc, cS2, cPE, cV, cSc]

        def barrier():
            vals = [(c, c.n) for c in ALLC]
            for Eo in (SY, GP, PE, VE, SC):
                for (c, v) in vals:
                    Eo.wait(c, v)

        for (dst, src) in [(ident_sb, ident), (identf_sb, identf), (maskf_sb, maskfill),
                           (convw_sb, convw), (convb_sb, convb), (dtb_sb, dt_bias),
                           (alog_sb, a_log), (d128_sb, d128), (nwt_sb, nwt),
                           (rb_sb, router_b), (fbias_sb, ffn_bias), (qm_sb, qmask),
                           (qm2_sb, qmask2),
                           (iso_sb, isodd), (noto_sb, notodd)]:
            run(SY, cS, lambda e, d=dst, s=src: e.dma_start(out=d[:, :], in_=s), 16)
        c_loads = cS.mark()
        VE.wait(cS, c_loads)
        run(VE, cV, lambda e: e.memset(bone[:, :], 1.0))
        run(VE, cV, lambda e: e.memset(beps[:, :], EPS))
        run(VE, cV, lambda e: e.memset(ones_f[:, :], 1.0))
        run(VE, cV, lambda e: e.memset(ones_b1[:, :], 1.0))
        SC.wait(cS, c_loads)
        run(SC, cSc, lambda e: e.activation(acol_sb[:, :], alog_sb[:, :], AF.Exp))
        VE.wait(cSc, cSc.n)
        run(VE, cV, lambda e: e.tensor_scalar(acol_sb[:, :], acol_sb[:, :], -1.0, None, ALU.mult))
        c_consts_v = cV.n

        st1 = contextlib.ExitStack()
        B_sb = st1.enter_context(nc.sbuf_tensor("B_sb", [128, G * T], BF16))
        C_sb = st1.enter_context(nc.sbuf_tensor("C_sb", [128, G * T], BF16))
        dt_sb = st1.enter_context(nc.sbuf_tensor("dt_sb", [128, T], F32))
        acs_sb = st1.enter_context(nc.sbuf_tensor("acs_sb", [128, T], F32))
        t2raw = st1.enter_context(nc.sbuf_tensor("t2raw", [128, T], BF16))
        bflat_sb = st1.enter_context(nc.sbuf_tensor("bflat_sb", [128, HID], BF16))
        B_m = st1.enter_context(nc.sbuf_tensor("B_m", [128, G * MTOK], BF16))
        dt_m = st1.enter_context(nc.sbuf_tensor("dt_m", [128, MTOK], F32))
        acs_m = st1.enter_context(nc.sbuf_tensor("acs_m", [128, MTOK], F32))
        cdump = st1.enter_context(nc.sbuf_tensor("cdump", [128, MTOK], BF16))
        run(GP, cG, lambda e: e.dma_start(out=bflat_sb[:, :], in_=b_flat), 16)

        stA = contextlib.ExitStack()
        xn_T = stA.enter_context(nc.sbuf_tensor("xn_T", [128, 32 * XNW], BF16))
        wbuf0 = stA.enter_context(nc.sbuf_tensor("wbuf0", [128, 32 * 128], BF16))
        wbuf1 = stA.enter_context(nc.sbuf_tensor("wbuf1", [128, 32 * 128], BF16))

        stA0 = contextlib.ExitStack()
        xt_sb = stA0.enter_context(nc.sbuf_tensor("xt_sb", [128, HID], F32))
        xn_tok = stA0.enter_context(nc.sbuf_tensor("xn_tok", [128, HID], BF16))
        qacc = stA0.enter_context(nc.sbuf_tensor("qacc", [1, HID], F32))
        m_col = stA0.enter_context(nc.sbuf_tensor("m_col", [128, 1], F32))
        nm_col = stA0.enter_context(nc.sbuf_tensor("nm_col", [128, 1], F32))
        s2_col = stA0.enter_context(nc.sbuf_tensor("s2_col", [128, 1], F32))
        v_col = stA0.enter_context(nc.sbuf_tensor("v_col", [128, 1], F32))
        sd_col = stA0.enter_context(nc.sbuf_tensor("sd_col", [128, 1], F32))
        rs_col = stA0.enter_context(nc.sbuf_tensor("rs_col", [128, 1], F32))
        pT0_a0 = stA0.enter_context(nc.psum_tensor("pT0_a0", [128, 128], BF16))
        pT1_a0 = stA0.enter_context(nc.psum_tensor("pT1_a0", [128, 128], BF16))
        pQ0_a0 = stA0.enter_context(nc.psum_tensor("pQ0_a0", [128, 512], F32))
        pQ1_a0 = stA0.enter_context(nc.psum_tensor("pQ1_a0", [128, 512], F32))

        VE.wait(cS, c_loads)
        VE.wait(cV, c_consts_v)
        xt_free_v = 0
        xt_free_s = 0
        xn_free_pe = 0
        qev = {}
        trev = {}
        tile_specs = []
        for ti in range(9):
            tile_specs.append((x_ext, qm_sb, ti, ti * 128, 128 if ti < 8 else 3))
        for ti in range(9):
            tcol = 1027 + (ti - 6) * 128 if ti >= 6 else None
            nco = (128 if ti < 8 else 3) if tcol is not None else 0
            tile_specs.append((x_oth, qm2_sb, ti, tcol, nco))
        for tix, (xsrc, qsb, ti, tcol, ncols) in enumerate(tile_specs):
            SY.wait(cV, xt_free_v)
            SY.wait(cSc, xt_free_s)
            run(SY, cS, lambda e, ti=ti, xsrc=xsrc: e.dma_start(
                out=xt_sb[:, :], in_=xsrc[ti * 128:(ti + 1) * 128, :]), 16)
            ld = cS.mark()
            VE.wait(cS, ld)
            run(VE, cV, lambda e: e.reduce_sum(m_col[:, :], xt_sb[:, :], axis=AX.X))
            VE.wait(cV, cV.n)
            run(VE, cV, lambda e: e.tensor_scalar(nm_col[:, :], m_col[:, :], -1.0 / HID, None, ALU.mult))
            run(VE, cV, lambda e: e.tensor_scalar(m_col[:, :], m_col[:, :], 1.0 / HID, None, ALU.mult))
            nm_v = cV.n
            SC.wait(cV, nm_v)
            SC.wait(cS, ld)
            SC.wait(cPE, xn_free_pe)
            run(SC, cSc, lambda e: e.activation(xn_tok[:, :], xt_sb[:, :], AF.Square,
                                                bias=nm_col[:, 0:1]))
            sq_s = cSc.n
            VE.wait(cSc, sq_s)
            run(VE, cV, lambda e: e.reduce_sum(s2_col[:, :], xn_tok[:, :], axis=AX.X))
            VE.wait(cV, cV.n)
            run(VE, cV, lambda e: e.tensor_scalar(v_col[:, :], s2_col[:, :], 1.0 / HID, None, ALU.mult))
            SC.wait(cV, cV.n)
            run(SC, cSc, lambda e: e.activation(sd_col[:, :], v_col[:, :], AF.Sqrt,
                                                bias=beps[:, 0:1]))
            VE.wait(cSc, cSc.n)
            run(VE, cV, lambda e: e.reciprocal(rs_col[:, :], sd_col[:, :]))
            VE.wait(cV, cV.n)
            run(VE, cV, lambda e: e.tensor_scalar(
                xn_tok[:, :], xt_sb[:, :], m_col[:, 0:1], rs_col[:, 0:1],
                ALU.subtract, ALU.mult))
            xn_v = cV.n
            xt_free_v = xn_v
            xt_free_s = sq_s
            PE.wait(cV, xn_v)
            PE.wait(cS, c_loads)
            for db in range(8):
                pq = pQ0_a0 if db % 2 == 0 else pQ1_a0
                if tix > 0 or db >= 2:
                    prev = qev.get((tix, db - 2) if db >= 2 else (tix - 1, 6 + db))
                    if prev:
                        PE.wait(cV, prev)
                run(PE, cPE, lambda e, db=db, ti=ti, pq=pq, qsb=qsb: e.matmul(
                    pq[0:1, :], qsb[:, ti:ti + 1], xn_tok[:, db * 512:(db + 1) * 512],
                    start=True, stop=True, skip_group_check=True))
                mm = cPE.n
                VE.wait(cPE, mm)
                if tix == 0:
                    run(VE, cV, lambda e, db=db, pq=pq: e.tensor_copy(
                        qacc[0:1, db * 512:(db + 1) * 512], pq[0:1, :]))
                else:
                    run(VE, cV, lambda e, db=db, pq=pq: e.tensor_tensor(
                        qacc[0:1, db * 512:(db + 1) * 512],
                        qacc[0:1, db * 512:(db + 1) * 512], pq[0:1, :], ALU.add))
                qev[(tix, db)] = cV.n
            if tcol is not None:
                for j in range(32):
                    pt = pT0_a0 if j % 2 == 0 else pT1_a0
                    prev = trev.get((tix, j - 2) if j >= 2 else (tix - 1, 30 + j))
                    if prev:
                        PE.wait(cV, prev)
                    run(PE, cPE, lambda e, j=j, pt=pt: e.transpose(
                        pt[:, :], xn_tok[:, j * 128:(j + 1) * 128], ident_sb[:, :]))
                    mm = cPE.n
                    VE.wait(cPE, mm)
                    run(VE, cV, lambda e, j=j, tcol=tcol, pt=pt, ncols=ncols: e.tensor_copy(
                        xn_T[:, j * XNW + tcol: j * XNW + tcol + ncols],
                        pt[:, 0:ncols]))
                    trev[(tix, j)] = cV.n
            xn_free_pe = cPE.n
        SY.wait(cV, cV.n)
        run(SY, cS, lambda e: e.dma_start(out=scr_q, in_=qacc[0:1, :]), 16)
        barrier()
        if dbg and upto == "ph0":
            run(SY, cS, lambda e: e.dma_start(out=dbg_xnt, in_=xn_T[:, :]), 16)
            barrier()
        stA0.close()
        if upto == "ph0":
            stA.close(); st1.close()
            _finish(plan, SY, ALLC)
            _emit_block(nc, st0, plan)
            return nc

        stA1 = contextlib.ExitStack()
        hbc_sb = stA1.enter_context(nc.sbuf_tensor("hbc_sb", [128, 1292], F32))
        gsil_sb = stA1.enter_context(nc.sbuf_tensor("gsil_sb", [128, T], BF16))
        acc0 = stA1.enter_context(nc.sbuf_tensor("acc0", [128, 1280], F32))
        acc1 = stA1.enter_context(nc.sbuf_tensor("acc1", [128, 1280], F32))
        accs = stA1.enter_context(nc.sbuf_tensor("accs", [128, 1280], BF16))
        tbuf8 = stA1.enter_context(nc.sbuf_tensor("tbuf8", [128, T], BF16))
        tb_m = stA1.enter_context(nc.sbuf_tensor("tb_m", [128, MTOK], BF16))
        adt_sb = stA1.enter_context(nc.sbuf_tensor("adt_sb", [128, T], F32))
        pT0_a1 = stA1.enter_context(nc.psum_tensor("pT0b", [128, 128], BF16))
        pT1_a1 = stA1.enter_context(nc.psum_tensor("pT1b", [128, 128], BF16))
        pP = [stA1.enter_context(nc.psum_tensor(f"pP{i}", [128, 429], F32))
              for i in range(6)]
        psets = [(pP[0], pP[1], pP[2]), (pP[3], pP[4], pP[5])]
        TB_GATE = [(3, 343), (346, 343), (689, 338)]
        TB_EXT = [(0, 429), (429, 429), (858, 428)]
        TB_C = [(0, 429), (429, 429), (858, 169)]

        slab_pe = {}
        slab_ld = {}
        evac_done = {}
        last_tr = {0: 0, 1: 0}
        tb_st = 0
        tbm_st = 0

        def slab_dma(ft):
            if ft - 2 in slab_pe:
                GP.wait(cPE, slab_pe[ft - 2])
            wb = wbuf0 if ft % 2 == 0 else wbuf1
            ctr = cG0 if ft % 2 == 0 else cG1
            run(GP, ctr, lambda e, wb=wb, ft=ft: e.dma_start(
                out=wb[:, :], in_=w_in_s[ft]), 16)
            slab_ld[ft] = (ctr, ctr.mark())

        def conv_chain(cf, obase, ibase, ln):
            VE.wait(cV, cV.n)
            run(VE, cV, lambda e, cf=cf: e.tensor_scalar(
                acc0[:, obase:obase + ln], hbc_sb[:, ibase:ibase + ln],
                convw_sb[:, cf * 4: cf * 4 + 1], None, ALU.mult))
            for kk in range(1, 4):
                src = acc0 if kk % 2 == 0 else acc1
                dst = acc1 if kk % 2 == 0 else acc0
            run(VE, cV, lambda e, cf=cf: e.scalar_tensor_tensor(
                acc1[:, obase:obase + ln], hbc_sb[:, ibase + 1:ibase + 1 + ln],
                convw_sb[:, cf * 4 + 1: cf * 4 + 2], acc0[:, obase:obase + ln],
                ALU.mult, ALU.add))
            run(VE, cV, lambda e, cf=cf: e.scalar_tensor_tensor(
                acc0[:, obase:obase + ln], hbc_sb[:, ibase + 2:ibase + 2 + ln],
                convw_sb[:, cf * 4 + 2: cf * 4 + 3], acc1[:, obase:obase + ln],
                ALU.mult, ALU.add))
            run(VE, cV, lambda e, cf=cf: e.scalar_tensor_tensor(
                acc1[:, obase:obase + ln], hbc_sb[:, ibase + 3:ibase + 3 + ln],
                convw_sb[:, cf * 4 + 3: cf * 4 + 4], acc0[:, obase:obase + ln],
                ALU.mult, ALU.add))

        slab_dma(0)
        slab_dma(1)
        for ft in range(NFT):
            wb = wbuf0 if ft % 2 == 0 else wbuf1
            ps3 = psets[ft % 2]
            if ft < FT_HBC0:
                TB = TB_GATE
            elif ft < 136 or ft == FT_DT:
                TB = TB_EXT
            else:
                TB = TB_C
            PE.wait(*slab_ld[ft])
            if ft >= 2:
                ev = evac_done[ft - 2]
                PE.wait(cV, ev[0])
                PE.wait(cSc, ev[1])
            for dc in range(32):
                for tb in range(3):
                    off, ln = TB[tb]
                    ps = ps3[tb]
                    run(PE, cPE, lambda e, ps=ps, wb=wb, off=off, ln=ln, dc=dc: e.matmul(
                        ps[:, 0:ln], wb[:, dc * 128:(dc + 1) * 128],
                        xn_T[:, dc * XNW + off: dc * XNW + off + ln],
                        start=(dc == 0), stop=(dc == 31), skip_group_check=True))
            mm_done = cPE.n
            slab_pe[ft] = mm_done
            if ft + 2 < NFT:
                slab_dma(ft + 2)
            if ft < FT_HBC0:
                SC.wait(cPE, mm_done)
                if ft >= 1:
                    SC.wait(cPE, gs_tr_pe)
                for tb in range(3):
                    off, ln = TB[tb]
                    ps = ps3[tb]
                    run(SC, cSc, lambda e, ps=ps, off=off, ln=ln: e.activation(
                        gsil_sb[:, off - 3:off - 3 + ln], ps[:, 0:ln], AF.Silu))
                ev_s = cSc.n
                PE.wait(cSc, ev_s)
                for j in range(8):
                    pt = pT0_a1 if j % 2 == 0 else pT1_a1
                    PE.wait(cV, last_tr[j % 2])
                    run(PE, cPE, lambda e, j=j, pt=pt: e.transpose(
                        pt[:, :], gsil_sb[:, j * 128:(j + 1) * 128], ident_sb[:, :]))
                    VE.wait(cPE, cPE.n)
                    VE.wait(cS, tb_st)
                    run(VE, cV, lambda e, j=j, pt=pt: e.tensor_copy(
                        tbuf8[:, j * 128:(j + 1) * 128], pt[:, :]))
                    last_tr[j % 2] = cV.n
                gs_tr_pe = cPE.n
                SY.wait(cV, cV.n)
                run(SY, cS, lambda e, ft=ft: e.dma_start(
                    out=scr_g[:, :, ft * 128:(ft + 1) * 128].rearrange("k t f -> t k f"),
                    in_=tbuf8[:, :].rearrange("t (k f) -> t k f", k=8)), 16)
                tb_st = cS.mark()
                evac_done[ft] = (cV.n, ev_s)
            elif ft < FT_DT:
                is_ext = ft < 136
                VE.wait(cPE, mm_done)
                VE.wait(cSc, cSc.n)
                for tb in range(3):
                    off, ln = TB[tb]
                    ps = ps3[tb]
                    run(VE, cV, lambda e, ps=ps, off=off, ln=ln: e.tensor_copy(
                        hbc_sb[:, off:off + ln], ps[:, 0:ln]))
                cf = ft - FT_HBC0
                conv_chain(cf, 0, 0, T)
                if is_ext:
                    conv_chain(cf, T, 1027, MTOK)
                conv_v = cV.n
                SC.wait(cV, conv_v)
                if ft < 128:
                    hsi = ft - FT_HBC0
                    if hsi >= 1:
                        SC.wait(cPE, hs_tr_pe)
                    run(SC, cSc, lambda e, cf=cf: e.activation(
                        accs[:, :], acc1[:, :], AF.Silu, bias=convb_sb[:, cf:cf + 1]))
                    ev_s = cSc.n
                    PE.wait(cSc, ev_s)
                    for j in range(10):
                        pt = pT0_a1 if j % 2 == 0 else pT1_a1
                        PE.wait(cV, last_tr[j % 2])
                        run(PE, cPE, lambda e, j=j, pt=pt: e.transpose(
                            pt[:, :], accs[:, j * 128:(j + 1) * 128], ident_sb[:, :]))
                        VE.wait(cPE, cPE.n)
                        if j < 8:
                            VE.wait(cS, tb_st)
                            run(VE, cV, lambda e, j=j, pt=pt: e.tensor_copy(
                                tbuf8[:, j * 128:(j + 1) * 128], pt[:, :]))
                        else:
                            VE.wait(cS, tbm_st)
                            run(VE, cV, lambda e, j=j, pt=pt: e.tensor_copy(
                                tb_m[:, (j - 8) * 128:(j - 7) * 128], pt[:, :]))
                        last_tr[j % 2] = cV.n
                    hs_tr_pe = cPE.n
                    SY.wait(cV, cV.n)
                    run(SY, cS, lambda e, hsi=hsi: e.dma_start(
                        out=scr_hs[:, :, hsi * 128:(hsi + 1) * 128].rearrange("k t f -> t k f"),
                        in_=tbuf8[:, :].rearrange("t (k f) -> t k f", k=8)), 16)
                    tb_st = cS.mark()
                    run(SY, cS, lambda e, hsi=hsi: e.dma_start(
                        out=scr_hsm[:, :, hsi * 128:(hsi + 1) * 128].rearrange("k t f -> t k f"),
                        in_=tb_m[:, :].rearrange("t (k f) -> t k f", k=2)), 16)
                    tbm_st = cS.mark()
                    evac_done[ft] = (cV.n, ev_s)
                else:
                    gi = ft - 128
                    dst = B_sb if gi < 8 else C_sb
                    gg = gi % 8
                    run(SC, cSc, lambda e, cf=cf, dst=dst, gg=gg: e.activation(
                        dst[:, gg * T:(gg + 1) * T], acc1[:, 0:T], AF.Silu,
                        bias=convb_sb[:, cf:cf + 1]))
                    if gi < 8:
                        SC.wait(cSc, cSc.n)
                        run(SC, cSc, lambda e, cf=cf, gg=gg: e.activation(
                            B_m[:, gg * MTOK:(gg + 1) * MTOK], acc1[:, T:1280], AF.Silu,
                            bias=convb_sb[:, cf:cf + 1]))
                    evac_done[ft] = (cV.n, cSc.n)
            else:
                VE.wait(cPE, mm_done)
                VE.wait(cSc, cSc.n)
                for tb in range(3):
                    off, ln = TB[tb]
                    ps = ps3[tb]
                    run(VE, cV, lambda e, ps=ps, off=off, ln=ln: e.tensor_copy(
                        hbc_sb[:, off:off + ln], ps[:, 0:ln]))
                ev_v = cV.n
                SC.wait(cV, ev_v)
                run(SC, cSc, lambda e: e.activation(
                    acc0[:, 0:T], hbc_sb[:, 3:3 + T], AF.Exp, bias=dtb_sb[:, 0:1]))
                SC.wait(cSc, cSc.n)
                run(SC, cSc, lambda e: e.activation(
                    dt_sb[:, :], acc0[:, 0:T], AF.Ln, bias=bone[:, 0:1]))
                SC.wait(cSc, cSc.n)
                run(SC, cSc, lambda e: e.activation(
                    acc0[:, T:1280], hbc_sb[:, 1030:1286], AF.Exp, bias=dtb_sb[:, 0:1]))
                SC.wait(cSc, cSc.n)
                run(SC, cSc, lambda e: e.activation(
                    dt_m[:, :], acc0[:, T:1280], AF.Ln, bias=bone[:, 0:1]))
                dt_s = cSc.n
                VE.wait(cSc, dt_s)
                run(VE, cV, lambda e: e.tensor_scalar(
                    adt_sb[:, :], dt_sb[:, :], acol_sb[:, 0:1], None, ALU.mult))
                run(VE, cV, lambda e: e.tensor_scalar(
                    acc1[:, 0:MTOK], dt_m[:, :], acol_sb[:, 0:1], None, ALU.mult))
                VE.wait(cV, cV.n)
                for k in range(NCHUNK):
                    run(VE, cV, lambda e, k=k: e.tensor_tensor_scan(
                        acs_sb[:, k * 128:(k + 1) * 128], ones_f[:, :],
                        adt_sb[:, k * 128:(k + 1) * 128], 0.0, ALU.mult, ALU.add))
                for k in range(2):
                    run(VE, cV, lambda e, k=k: e.tensor_tensor_scan(
                        acs_m[:, k * 128:(k + 1) * 128], ones_f[:, :],
                        acc1[:, k * 128:(k + 1) * 128], 0.0, ALU.mult, ALU.add))
                evac_done[ft] = (cV.n, cSc.n)
        barrier()
        stA1.close()
        if upto == "ph1":
            stA.close(); st1.close()
            _finish(plan, SY, ALLC)
            _emit_block(nc, st0, plan)
            return nc

        stA3 = contextlib.ExitStack()
        bse_a3 = stA3.enter_context(nc.sbuf_tensor("bse_a3", [128, T], BF16))
        pQ0_a3 = stA3.enter_context(nc.psum_tensor("pQ0d", [128, 512], F32))
        pQ1_a3 = stA3.enter_context(nc.psum_tensor("pQ1d", [128, 512], F32))
        run(GP, cG0, lambda e: e.dma_start(out=wbuf0[:, :], in_=a_s), 16)
        a_ld = cG0.mark()
        PE.wait(cG0, a_ld)
        for dc in range(32):
            for blk in range(2):
                pq = pQ0_a3 if blk == 0 else pQ1_a3
                run(PE, cPE, lambda e, pq=pq, blk=blk, dc=dc: e.matmul(
                    pq[:, :], wbuf0[:, dc * 128:(dc + 1) * 128],
                    xn_T[:, dc * XNW + 3 + blk * 512: dc * XNW + 3 + (blk + 1) * 512],
                    start=(dc == 0), stop=(dc == 31), skip_group_check=True))
        for blk in range(2):
            pq = pQ0_a3 if blk == 0 else pQ1_a3
            VE.wait(cPE, cPE.n)
            run(VE, cV, lambda e, pq=pq, blk=blk: e.tensor_copy(
                t2raw[:, blk * 512:(blk + 1) * 512], pq[:, :]))
        t2_v = cV.n
        base_pe = {}
        for ot in range(32):
            wb = wbuf1 if ot % 2 == 0 else wbuf0
            if ot == 0:
                GP.wait(cPE, cPE.n)
            elif ot == 1:
                GP.wait(cPE, cPE.n)
            else:
                GP.wait(cPE, base_pe[ot - 2])
            ctr = cG1 if ot % 2 == 0 else cG0
            run(GP, ctr, lambda e, wb=wb, ot=ot: e.dma_start(
                out=wb[:, :], in_=w_ffn_s[ot]), 16)
            PE.wait(ctr, ctr.mark())
            if ot == 0:
                PE.wait(cV, t2_v)
            if ot >= 1:
                PE.wait(cV, bse_v)
            for dc in range(32):
                for blk in range(2):
                    pq = pQ0_a3 if blk == 0 else pQ1_a3
                    run(PE, cPE, lambda e, pq=pq, wb=wb, blk=blk, dc=dc: e.matmul(
                        pq[:, :], wb[:, dc * 128:(dc + 1) * 128],
                        xn_T[:, dc * XNW + 3 + blk * 512: dc * XNW + 3 + (blk + 1) * 512],
                        start=(dc == 0), stop=(dc == 31), skip_group_check=True))
            base_pe[ot] = cPE.n
            VE.wait(cPE, cPE.n)
            if ot >= 1:
                VE.wait(cS, bse_st)
            run(VE, cV, lambda e: e.tensor_copy(bse_a3[:, 0:512], pQ0_a3[:, :]))
            run(VE, cV, lambda e: e.tensor_copy(bse_a3[:, 512:1024], pQ1_a3[:, :]))
            bse_v = cV.n
            SY.wait(cV, bse_v)
            run(SY, cS, lambda e, ot=ot: e.dma_start(out=scr_base[ot], in_=bse_a3[:, :]), 16)
            bse_st = cS.mark()
        barrier()
        stA3.close()
        stA.close()
        if upto == "ph15a":
            st1.close()
            _finish(plan, SY, ALLC)
            _emit_block(nc, st0, plan)
            return nc

        stA2 = contextlib.ExitStack()
        N_a2 = stA2.enter_context(nc.sbuf_tensor("N_a2", [128, INTER], F32))
        tmpX_a2 = stA2.enter_context(nc.sbuf_tensor("tmpX_a2", [128, INTER], BF16))
        sgrp0 = stA2.enter_context(nc.sbuf_tensor("sgrp0", [128, 1024], BF16))
        sgrp1 = stA2.enter_context(nc.sbuf_tensor("sgrp1", [128, 1024], BF16))
        g2bf_a2 = stA2.enter_context(nc.sbuf_tensor("g2bf_a2", [128, 1024], BF16))
        xdec_a2 = stA2.enter_context(nc.sbuf_tensor("xdec_a2", [128, 1024], BF16))
        decays = stA2.enter_context(nc.sbuf_tensor("decays", [128, 128], F32))
        dtdec = stA2.enter_context(nc.sbuf_tensor("dtdec", [128, 128], F32))
        dtdecT = stA2.enter_context(nc.sbuf_tensor("dtdecT", [128, 128], F32))
        btg = stA2.enter_context(nc.sbuf_tensor("btg", [128, 128], BF16))
        dexp_a2 = stA2.enter_context(nc.sbuf_tensor("dexp_a2", [128, 1], F32))
        pT0_a2 = stA2.enter_context(nc.psum_tensor("pT0c", [128, 128], BF16))
        pTf_a2 = stA2.enter_context(nc.psum_tensor("pTfc", [128, 128], F32))
        pQ0_a2 = stA2.enter_context(nc.psum_tensor("pQ0c", [128, 512], F32))
        pSt = stA2.enter_context(nc.psum_tensor("pStc", [128, 1024], F32))

        run(VE, cV, lambda e: e.memset(N_a2[:, :], 0.0))
        st_pe = 0
        for mk in range(2):
            SY.wait(cV, cV.n)
            run(SY, cS, lambda e, mk=mk: e.dma_start(out=tmpX_a2[:, :], in_=scr_hsm[mk]), 16)
            hs_ld = cS.mark()
            SC.wait(cV, cV.n)
            run(SC, cSc, lambda e, mk=mk: e.activation(
                decays[:, :], acs_m[:, mk * 128:(mk + 1) * 128], AF.Exp, scale=-1.0,
                bias=acs_m[:, mk * 128 + 127: mk * 128 + 128]))
            if mk == 0:
                SC.wait(cSc, cSc.n)
                run(SC, cSc, lambda e: e.activation(
                    dexp_a2[:, :], acs_m[:, 255:256], AF.Exp))
            VE.wait(cSc, cSc.n)
            run(VE, cV, lambda e, mk=mk: e.tensor_tensor(
                dtdec[:, :], dt_m[:, mk * 128:(mk + 1) * 128], decays[:, :], ALU.mult))
            if mk == 0:
                VE.wait(cV, cV.n)
                run(VE, cV, lambda e: e.tensor_scalar(
                    dtdec[:, :], dtdec[:, :], dexp_a2[:, 0:1], None, ALU.mult))
            PE.wait(cV, cV.n)
            run(PE, cPE, lambda e: e.transpose(pTf_a2[:, :], dtdec[:, :], identf_sb[:, :]))
            VE.wait(cPE, cPE.n)
            run(VE, cV, lambda e: e.tensor_copy(dtdecT[:, :], pTf_a2[:, :]))
            dtdecT_v = cV.n
            VE.wait(cS, hs_ld)
            for g in range(G):
                PE.wait(cV, cV.n)
                run(PE, cPE, lambda e, g=g, mk=mk: e.transpose(
                    pT0_a2[:, :], B_m[:, g * MTOK + mk * 128: g * MTOK + (mk + 1) * 128],
                    ident_sb[:, :]))
                VE.wait(cPE, cPE.n)
                run(VE, cV, lambda e: e.tensor_copy(btg[:, :], pT0_a2[:, :]))
                VE.wait(cV, dtdecT_v)
                if st_pe:
                    VE.wait(cPE, st_pe)
                for i in range(16):
                    h = 8 * i + g
                    run(VE, cV, lambda e, h=h, i=i: e.tensor_scalar(
                        xdec_a2[:, i * 64:(i + 1) * 64], tmpX_a2[:, h * 64:(h + 1) * 64],
                        dtdecT[:, h:h + 1], None, ALU.mult))
                xd_v = cV.n
                PE.wait(cV, xd_v)
                for i in range(16):
                    run(PE, cPE, lambda e, i=i: e.matmul(
                        pSt[:, i * 64:(i + 1) * 64], btg[:, :], xdec_a2[:, i * 64:(i + 1) * 64],
                        start=True, stop=True, skip_group_check=True))
                st_pe = cPE.n
                VE.wait(cPE, st_pe)
                run(VE, cV, lambda e, g=g: e.tensor_tensor(
                    N_a2[:, g * 1024:(g + 1) * 1024], N_a2[:, g * 1024:(g + 1) * 1024],
                    pSt[:, :], ALU.add))
        VE.wait(cV, cV.n)
        run(VE, cV, lambda e: e.tensor_scalar(N_a2[:, :], N_a2[:, :], iso_sb[:, 0:1], None, ALU.mult))
        SY.wait(cV, cV.n)
        run(SY, cS, lambda e: e.dma_start(out=scr_nc, in_=N_a2[:, :]), 16)

        hs_ld = 0
        sg_st = {}
        for k in range(NCHUNK):
            SY.wait(cV, cV.n)
            run(SY, cS, lambda e, k=k: e.dma_start(out=tmpX_a2[:, :], in_=scr_hs[k]), 16)
            hs_ld = cS.mark()
            SC.wait(cV, cV.n)
            run(SC, cSc, lambda e, k=k: e.activation(
                decays[:, :], acs_sb[:, k * 128:(k + 1) * 128], AF.Exp, scale=-1.0,
                bias=acs_sb[:, k * 128 + 127: k * 128 + 128]))
            VE.wait(cSc, cSc.n)
            run(VE, cV, lambda e, k=k: e.tensor_tensor(
                dtdec[:, :], dt_sb[:, k * 128:(k + 1) * 128], decays[:, :], ALU.mult))
            PE.wait(cV, cV.n)
            run(PE, cPE, lambda e: e.transpose(pTf_a2[:, :], dtdec[:, :], identf_sb[:, :]))
            VE.wait(cPE, cPE.n)
            run(VE, cV, lambda e: e.tensor_copy(dtdecT[:, :], pTf_a2[:, :]))
            dtdecT_v = cV.n
            for g in range(G):
                PE.wait(cV, cV.n)
                run(PE, cPE, lambda e, g=g, k=k: e.matmul(
                    pQ0_a2[:, 0:128], B_sb[:, g * T + k * 128: g * T + (k + 1) * 128],
                    C_sb[:, g * T + k * 128: g * T + (k + 1) * 128],
                    start=True, stop=True, skip_group_check=True))
                VE.wait(cPE, cPE.n)
                if k > 0 and g == 0:
                    VE.wait(cS, g2_st)
                run(VE, cV, lambda e, g=g: e.tensor_copy(
                    g2bf_a2[:, g * 128:(g + 1) * 128], pQ0_a2[:, 0:128]))
            SY.wait(cV, cV.n)
            run(SY, cS, lambda e, k=k: e.dma_start(out=scr_g2[k], in_=g2bf_a2[:, :]), 16)
            g2_st = cS.mark()
            VE.wait(cS, hs_ld)
            for g in range(G):
                sg = sgrp0 if g % 2 == 0 else sgrp1
                PE.wait(cV, cV.n)
                run(PE, cPE, lambda e, g=g, k=k: e.transpose(
                    pT0_a2[:, :], B_sb[:, g * T + k * 128: g * T + (k + 1) * 128], ident_sb[:, :]))
                VE.wait(cPE, cPE.n)
                run(VE, cV, lambda e: e.tensor_copy(btg[:, :], pT0_a2[:, :]))
                VE.wait(cV, dtdecT_v)
                VE.wait(cPE, st_pe)
                for i in range(16):
                    h = 8 * i + g
                    run(VE, cV, lambda e, h=h, i=i: e.tensor_scalar(
                        xdec_a2[:, i * 64:(i + 1) * 64], tmpX_a2[:, h * 64:(h + 1) * 64],
                        dtdecT[:, h:h + 1], None, ALU.mult))
                xd_v = cV.n
                PE.wait(cV, xd_v)
                for i in range(16):
                    run(PE, cPE, lambda e, i=i: e.matmul(
                        pSt[:, i * 64:(i + 1) * 64], btg[:, :], xdec_a2[:, i * 64:(i + 1) * 64],
                        start=True, stop=True, skip_group_check=True))
                st_pe = cPE.n
                VE.wait(cPE, st_pe)
                prev_st = sg_st.get((k, g - 2)) or (sg_st.get((k - 1, g + 6)) if k > 0 else None)
                if prev_st:
                    VE.wait(cS, prev_st)
                run(VE, cV, lambda e, sg=sg: e.tensor_copy(sg[:, :], pSt[:, :]))
                SY.wait(cV, cV.n)
                run(SY, cS, lambda e, k=k, g=g, sg=sg: e.dma_start(
                    out=scr_s[k][:, g * 1024:(g + 1) * 1024], in_=sg[:, :]), 16)
                sg_st[(k, g)] = cS.mark()
        barrier()
        stA2.close()
        if upto == "ph2a":
            st1.close()
            _finish(plan, SY, ALLC)
            _emit_block(nc, st0, plan)
            return nc

        stB = contextlib.ExitStack()
        N_sb = stB.enter_context(nc.sbuf_tensor("N_sb2", [128, INTER], F32))
        tmpNb = stB.enter_context(nc.sbuf_tensor("tmpNb", [128, INTER], BF16))
        tmpX = stB.enter_context(nc.sbuf_tensor("tmpX2", [128, INTER], BF16))
        tmpS = stB.enter_context(nc.sbuf_tensor("tmpS", [128, INTER], BF16))
        ybuf = stB.enter_context(nc.sbuf_tensor("ybuf", [128, INTER], F32))
        fill = stB.enter_context(nc.sbuf_tensor("fill", [128, 2048], F32))
        lmat = stB.enter_context(nc.sbuf_tensor("lmat", [128, 2048], BF16))
        g2bf = stB.enter_context(nc.sbuf_tensor("g2bf2", [128, 1024], BF16))
        xdec = stB.enter_context(nc.sbuf_tensor("xdec2", [128, 1024], BF16))
        gtmp = stB.enter_context(nc.sbuf_tensor("gtmp", [128, T], F32))
        dflat_b = stB.enter_context(nc.sbuf_tensor("dflat_b", [1, 2048], BF16))
        gout = stB.enter_context(nc.sbuf_tensor("gout", [128, T], BF16))
        t2T = stB.enter_context(nc.sbuf_tensor("t2T", [128, T], BF16))
        bse = stB.enter_context(nc.sbuf_tensor("bse2", [128, T], BF16))
        qar = stB.enter_context(nc.sbuf_tensor("qar", [128, 32], F32))
        qar_bf = stB.enter_context(nc.sbuf_tensor("qar_bf", [128, 32], BF16))
        rwt_sb = stB.enter_context(nc.sbuf_tensor("rwt_sb", [128, 256], BF16))
        route_col = stB.enter_context(nc.sbuf_tensor("route_col", [128, 1], F32))
        lg_sb = stB.enter_context(nc.sbuf_tensor("lg_sb", [1, 8], F32))
        mx_sb = stB.enter_context(nc.sbuf_tensor("mx_sb", [1, 1], F32))
        s1_sb = stB.enter_context(nc.sbuf_tensor("s1_sb", [1, 1], F32))
        r1_sb = stB.enter_context(nc.sbuf_tensor("r1_sb", [1, 1], F32))
        acsT = stB.enter_context(nc.sbuf_tensor("acsT", [128, 128], F32))
        sdoT = stB.enter_context(nc.sbuf_tensor("sdoT", [128, 128], F32))
        dtT = stB.enter_context(nc.sbuf_tensor("dtT", [128, 128], F32))
        dexp_c = stB.enter_context(nc.sbuf_tensor("dexp_c2", [128, 1], F32))
        dxrow_b = stB.enter_context(nc.sbuf_tensor("dxrow_b", [128, 64], F32))
        rms_v = stB.enter_context(nc.sbuf_tensor("rms_v", [128, 8], F32))
        rms_r = stB.enter_context(nc.sbuf_tensor("rms_r", [128, 8], F32))
        pT0 = stB.enter_context(nc.psum_tensor("pT0e", [128, 128], BF16))
        pT1 = stB.enter_context(nc.psum_tensor("pT1e", [128, 128], BF16))
        pTf = stB.enter_context(nc.psum_tensor("pTfe", [128, 128], F32))
        pQ0 = stB.enter_context(nc.psum_tensor("pQ0e", [128, 512], F32))
        pQ1 = stB.enter_context(nc.psum_tensor("pQ1e", [128, 512], F32))
        pYd = stB.enter_context(nc.psum_tensor("pYde", [128, 1024], F32))

        run(GP, cG, lambda e: e.dma_start(out=rwt_sb[:, :], in_=rwt_s), 16)
        rwt_ld = cG.mark()
        run(SY, cS, lambda e: e.dma_start(
            out=qar[:, :],
            in_=scr_q.rearrange("a (p c) -> (a p) c", p=128)), 16)
        qar_ld = cS.mark()
        VE.wait(cS, qar_ld)
        run(VE, cV, lambda e: e.tensor_copy(qar_bf[:, :], qar[:, :]))
        PE.wait(cV, cV.n)
        PE.wait(cG, rwt_ld)
        for dc in range(32):
            run(PE, cPE, lambda e, dc=dc: e.matmul(
                pQ0[0:1, 0:8], qar_bf[:, dc:dc + 1], rwt_sb[:, dc * 8:(dc + 1) * 8],
                start=(dc == 0), stop=(dc == 31), skip_group_check=True))
        VE.wait(cPE, cPE.n)
        run(VE, cV, lambda e: e.scalar_tensor_tensor(
            lg_sb[:, :], pQ0[0:1, 0:8], 1.0 / QWIN, rb_sb[:, :], ALU.mult, ALU.add))
        VE.wait(cV, cV.n)
        run(VE, cV, lambda e: e.reduce_max(mx_sb[:, :], lg_sb[:, :], axis=AX.X))
        VE.wait(cV, cV.n)
        run(VE, cV, lambda e: e.tensor_scalar(
            lg_sb[:, :], lg_sb[:, :], mx_sb[0:1, 0:1], None, ALU.subtract))
        SC.wait(cV, cV.n)
        run(SC, cSc, lambda e: e.activation(lg_sb[:, :], lg_sb[:, :], AF.Exp))
        VE.wait(cSc, cSc.n)
        run(VE, cV, lambda e: e.reduce_sum(s1_sb[:, :], lg_sb[:, :], axis=AX.X))
        VE.wait(cV, cV.n)
        run(VE, cV, lambda e: e.reciprocal(r1_sb[:, :], s1_sb[:, :]))
        VE.wait(cV, cV.n)
        run(VE, cV, lambda e: e.tensor_scalar(
            lg_sb[:, :], lg_sb[:, :], r1_sb[0:1, 0:1], SCALING, ALU.mult, ALU.mult))
        SY.wait(cV, cV.n)
        run(SY, cS, lambda e: e.dma_start(out=scr_route, in_=lg_sb[:, :]), 16)
        SY.wait(cS, cS.n)
        for ee in range(8):
            run(SY, cS, lambda e, ee=ee: e.dma_start(
                out=route_col[ee * 16:(ee + 1) * 16, 0:1],
                in_=scr_route[0:1, ee:ee + 1].broadcast_to((16, 1))), 16)
        VE.wait(cS, cS.n)
        run(VE, cV, lambda e: e.tensor_scalar(
            t2T[:, :], t2raw[:, :], route_col[:, 0:1], None, ALU.mult))
        t2T_v = cV.n
        PE.wait(cV, t2T_v)
        for ot in range(32):
            if ot > 0:
                SY.wait(cV, g_ev)
            run(SY, cS, lambda e, ot=ot: e.dma_start(out=bse[:, :], in_=scr_base[ot]), 16)
            bse_ld = cS.mark()
            if ot > 0:
                PE.wait(cV, g_ev)
            for blk in range(2):
                pq = pQ0 if blk == 0 else pQ1
                run(PE, cPE, lambda e, ot=ot, blk=blk, pq=pq: e.matmul(
                    pq[:, :], bflat_sb[:, ot * 128:(ot + 1) * 128],
                    t2T[:, blk * 512:(blk + 1) * 512],
                    start=True, stop=True, skip_group_check=True))
            lora_pe = cPE.n
            VE.wait(cPE, lora_pe)
            VE.wait(cS, bse_ld)
            if ot > 0:
                VE.wait(cSc, g_sil)
            run(VE, cV, lambda e: e.tensor_tensor(gtmp[:, 0:512], bse[:, 0:512], pQ0[:, :], ALU.add))
            run(VE, cV, lambda e: e.tensor_tensor(gtmp[:, 512:1024], bse[:, 512:1024], pQ1[:, :], ALU.add))
            g_ev = cV.n
            SC.wait(cV, g_ev)
            if ot > 0:
                SC.wait(cS2, gg_st)
            run(SC, cSc, lambda e, ot=ot: e.activation(
                gout[:, :], gtmp[:, :], AF.Silu, bias=fbias_sb[:, ot:ot + 1]))
            g_sil = cSc.n
            SC.wait(cSc, g_sil)
            run(SC, cS2, lambda e, ot=ot: e.dma_start(
                out=scr_gg[ot * 128:(ot + 1) * 128, :], in_=gout[:, :]), 16)
            gg_st = cS2.mark()
        if upto == "ph15b":
            barrier()
            stB.close(); st1.close()
            _finish(plan, SY, ALLC)
            _emit_block(nc, st0, plan)
            return nc

        run(SY, cS, lambda e: e.dma_start(out=N_sb[:, :], in_=scr_nc), 16)
        nc_ld = cS.mark()
        VE.wait(cS, nc_ld)
        n_v = cV.n
        y_v = 0
        nrm_st = 0
        for k in range(NCHUNK):
            SY.wait(cV, y_v)
            if k > 0:
                SY.wait(cS, nrm_st)
            run(SY, cS, lambda e, k=k: e.dma_start(out=tmpX[:, :], in_=scr_hs[k]), 16)
            hs_ld = cS.mark()
            SY.wait(cV, nupd2_v if k > 0 else 0)
            if k > 0:
                SY.wait(cPE, tr_pe)
            run(SY, cS, lambda e, k=k: e.dma_start(out=tmpS[:, :], in_=scr_s[k]), 16)
            s_ld = cS.mark()
            run(SY, cS, lambda e, k=k: e.dma_start(out=g2bf[:, :], in_=scr_g2[k]), 16)
            g2_ld = cS.mark()
            VE.wait(cV, n_v)
            if k > 0:
                VE.wait(cPE, yoff_pe)
            run(VE, cV, lambda e: e.tensor_copy(tmpNb[:, :], N_sb[:, :]))
            nb_v = cV.n
            SC.wait(cSc, cSc.n)
            SC.wait(cS2, cS2.n)
            run(SC, cSc, lambda e, k=k: e.activation(
                dexp_c[:, :], acs_sb[:, k * 128 + 127: k * 128 + 128], AF.Exp))
            VE.wait(cSc, cSc.n)
            VE.wait(cS2, cS2.n)
            run(VE, cV, lambda e: e.tensor_copy(
                dxrow_b[:, :].rearrange("h (a u) -> h a u", u=64),
                dexp_c[:, 0:1].rearrange("h (a u) -> h a u", u=1).broadcast_to((128, 1, 64))))
            SC.wait(cV, cV.n)
            run(SC, cS2, lambda e, k=k: e.dma_start(
                out=scr_dexp[k % 2:k % 2 + 1, :].rearrange("a (h u) -> h (a u)", h=128, u=64),
                in_=dxrow_b[:, :]), 16)
            SC.wait(cS2, cS2.n)
            run(SC, cS2, lambda e, k=k: e.dma_start(
                out=scr_dexp2[k % 2:k % 2 + 1, :].rearrange(
                    "a (g i p) -> a g i p", g=8, i=16, p=64),
                in_=scr_dexp[k % 2:k % 2 + 1, :].rearrange(
                    "a (i g p) -> a g i p", i=16, g=8, p=64)), 16)
            hop2_done = cS2.mark()
            nmul2_v = nb_v
            for hf in range(4):
                GP.wait(cS2, hop2_done)
                if k > 0 or hf > 0:
                    GP.wait(cPE, dfl2_pe)
                run(GP, cG, lambda e, k=k, hf=hf: e.dma_start(
                    out=dflat_b[0:1, :],
                    in_=scr_dexp2[k % 2:k % 2 + 1, hf * 2048:(hf + 1) * 2048]), 16)
                dfl2_ld = cG.mark()
                PE.wait(cG, dfl2_ld)
                PE.wait(cV, nb_v)
                for sg8 in range(4):
                    pb = pQ0 if sg8 % 2 == 0 else pQ1
                    if k > 0 or hf > 0 or sg8 >= 2:
                        PE.wait(cV, nmul2_v)
                    run(PE, cPE, lambda e, sg8=sg8, pb=pb: e.matmul(
                        pb[:, :], ones_b1[0:1, :], dflat_b[0:1, sg8 * 512:(sg8 + 1) * 512],
                        start=True, stop=True, skip_group_check=True))
                    VE.wait(cPE, cPE.n)
                    off = hf * 2048 + sg8 * 512
                    run(VE, cV, lambda e, off=off, pb=pb: e.tensor_tensor(
                        N_sb[:, off:off + 512], N_sb[:, off:off + 512], pb[:, :],
                        ALU.mult))
                    nmul2_v = cV.n
                dfl2_pe = cPE.n
            VE.wait(cV, nmul2_v)
            VE.wait(cS, s_ld)
            run(VE, cV, lambda e: e.tensor_tensor(N_sb[:, :], N_sb[:, :], tmpS[:, :], ALU.add))
            nupd2_v = cV.n
            n_v = nupd2_v
            SY.wait(cV, nupd2_v)
            run(SY, cS, lambda e, k=k: e.dma_start(out=tmpS[:, :], in_=scr_g[k]), 16)
            gate_ld = cS.mark()
            PE.wait(cSc, cSc.n)
            if k > 0:
                PE.wait(cV, dtT_v)
            run(PE, cPE, lambda e, k=k: e.transpose(
                pTf[:, :], acs_sb[:, k * 128:(k + 1) * 128], identf_sb[:, :]))
            VE.wait(cPE, cPE.n)
            VE.wait(cSc, sdo_s if k > 0 else 0)
            run(VE, cV, lambda e: e.tensor_copy(acsT[:, :], pTf[:, :]))
            acsT_v = cV.n
            PE.wait(cV, acsT_v)
            run(PE, cPE, lambda e, k=k: e.transpose(
                pTf[:, :], dt_sb[:, k * 128:(k + 1) * 128], identf_sb[:, :]))
            VE.wait(cPE, cPE.n)
            run(VE, cV, lambda e: e.tensor_copy(dtT[:, :], pTf[:, :]))
            dtT_v = cV.n
            SC.wait(cV, acsT_v)
            run(SC, cSc, lambda e: e.activation(sdoT[:, :], acsT[:, :], AF.Exp))
            sdo_s = cSc.n
            if k >= 2:
                SC.wait(cG, acs_rd)
            run(SC, cS2, lambda e, k=k: e.dma_start(
                out=scr_acs[k % 2:k % 2 + 1, :].rearrange("a (h l) -> h (a l)", h=128),
                in_=acs_sb[:, k * 128:(k + 1) * 128]), 16)
            SC.wait(cS2, cS2.n)
            run(SC, cS2, lambda e, k=k: e.dma_start(
                out=scr_acs2[k % 2:k % 2 + 1, :].rearrange(
                    "a (g i l) -> a g i l", g=8, i=16, l=128),
                in_=scr_acs[k % 2:k % 2 + 1, :].rearrange(
                    "a (i g l) -> a g i l", i=16, g=8, l=128)), 16)
            acs_stg = cS2.mark()
            acs_rd_new = None
            for g in range(G):
                VE.wait(cSc, cSc.n)
                if k > 0 or g > 0:
                    VE.wait(cG, acs_rd_prev)
                    VE.wait(cPE, yd_pe)
                run(VE, cV, lambda e, g=g: e.scalar_tensor_tensor(
                    fill[:, :].rearrange("s (i l) -> s i l", i=16),
                    acsT[:, :].rearrange("s (i g) -> s i g", g=8)[:, :, g:g + 1
                        ].broadcast_to((128, 16, 128)),
                    -1.0,
                    maskf_sb[:, :].rearrange("s (o l) -> s o l", o=1
                        ).broadcast_to((128, 16, 128)),
                    ALU.mult, ALU.add))
                fill_v = cV.n
                GP.wait(cV, fill_v)
                GP.wait(cS2, acs_stg)
                run(GP, cG, lambda e, k=k, g=g: e.dma_start(
                    out=fill[:, :],
                    in_=scr_acs2[k % 2:k % 2 + 1, g * 2048:(g + 1) * 2048
                        ].broadcast_to((128, 2048)),
                    accum_op=ALU.add), 16)
                acs_rd_prev = cG.mark()
                if g == G - 1:
                    acs_rd = cG.mark()
                SC.wait(cG, acs_rd_prev)
                SC.wait(cV, fill_v)
                run(SC, cSc, lambda e: e.activation(lmat[:, :], fill[:, :], AF.Exp))
                lm_s = cSc.n
                VE.wait(cSc, lm_s)
                VE.wait(cS, g2_ld)
                run(VE, cV, lambda e, g=g: e.tensor_tensor(
                    lmat[:, :].rearrange("s (i l) -> s i l", i=16),
                    lmat[:, :].rearrange("s (i l) -> s i l", i=16),
                    g2bf[:, g * 128:(g + 1) * 128].rearrange("s (o l) -> s o l", o=1
                        ).broadcast_to((128, 16, 128)),
                    ALU.mult))
                m_v = cV.n
                VE.wait(cS, hs_ld)
                VE.wait(cV, dtT_v)
                if k > 0 or g > 0:
                    VE.wait(cPE, yd_pe)
                for i in range(16):
                    h = 8 * i + g
                    run(VE, cV, lambda e, h=h, i=i: e.tensor_scalar(
                        xdec[:, i * 64:(i + 1) * 64], tmpX[:, h * 64:(h + 1) * 64],
                        dtT[:, h:h + 1], None, ALU.mult))
                xd_v = cV.n
                PE.wait(cV, m_v)
                PE.wait(cV, xd_v)
                PE.wait(cV, nb_v)
                if k > 0 or g > 0:
                    PE.wait(cV, yev_v)
                for i in range(16):
                    run(PE, cPE, lambda e, i=i: e.matmul(
                        pYd[:, i * 64:(i + 1) * 64], lmat[:, i * 128:(i + 1) * 128],
                        xdec[:, i * 64:(i + 1) * 64],
                        start=True, stop=True, skip_group_check=True))
                for blk in range(2):
                    pq = pQ0 if blk == 0 else pQ1
                    run(PE, cPE, lambda e, g=g, k=k, blk=blk, pq=pq: e.matmul(
                        pq[:, :], C_sb[:, g * T + k * 128: g * T + (k + 1) * 128],
                        tmpNb[:, g * 1024 + blk * 512: g * 1024 + (blk + 1) * 512],
                        start=True, stop=True, skip_group_check=True))
                yd_pe = cPE.n
                yoff_pe = cPE.n
                VE.wait(cPE, yd_pe)
                VE.wait(cSc, sdo_s)
                for blk in range(2):
                    pq = pQ0 if blk == 0 else pQ1
                    run(VE, cV, lambda e, g=g, blk=blk, pq=pq: e.tensor_tensor(
                        ybuf[:, :].rearrange("l (i g p) -> l i g p", g=8, p=64)[
                            :, blk * 8:(blk + 1) * 8, g:g + 1, :],
                        pq[:, :].rearrange("l (i u p) -> l i u p", u=1, p=64),
                        sdoT[:, :].rearrange("l (i g) -> l i g", g=8)[
                            :, blk * 8:(blk + 1) * 8, g:g + 1].rearrange(
                            "l i (g u) -> l i g u", u=1).broadcast_to((128, 8, 1, 64)),
                        ALU.mult))
                    VE.wait(cV, cV.n)
                    run(VE, cV, lambda e, g=g, blk=blk: e.tensor_tensor(
                        ybuf[:, :].rearrange("l (i g p) -> l i g p", g=8, p=64)[
                            :, blk * 8:(blk + 1) * 8, g:g + 1, :],
                        ybuf[:, :].rearrange("l (i g p) -> l i g p", g=8, p=64)[
                            :, blk * 8:(blk + 1) * 8, g:g + 1, :],
                        pYd[:, :].rearrange("l (i u p) -> l i u p", u=1, p=64)[
                            :, blk * 8:(blk + 1) * 8, :, :],
                        ALU.add))
                    VE.wait(cV, cV.n)
                yev_v = cV.n
            VE.wait(cV, yev_v)
            VE.wait(cPE, yd_pe)
            for sblk in range(8):
                run(VE, cV, lambda e, sblk=sblk: e.tensor_tensor(
                    xdec[:, :].rearrange("l (hh p) -> l hh p", p=64),
                    tmpX[:, sblk * 1024:(sblk + 1) * 1024].rearrange("l (hh p) -> l hh p", p=64),
                    d128_sb[:, sblk * 16:(sblk + 1) * 16].rearrange("l (hh o) -> l hh o", o=1
                        ).broadcast_to((128, 16, 64)),
                    ALU.mult))
                VE.wait(cV, cV.n)
                run(VE, cV, lambda e, sblk=sblk: e.tensor_tensor(
                    ybuf[:, sblk * 1024:(sblk + 1) * 1024],
                    ybuf[:, sblk * 1024:(sblk + 1) * 1024], xdec[:, :], ALU.add))
                VE.wait(cV, cV.n)
            VE.wait(cS, gate_ld)
            run(VE, cV, lambda e: e.tensor_tensor(ybuf[:, :], ybuf[:, :], tmpS[:, :], ALU.mult))
            VE.wait(cV, cV.n)
            for g8 in range(8):
                run(VE, cV, lambda e, g8=g8: e.tensor_tensor(
                    xdec[:, :], ybuf[:, g8 * 1024:(g8 + 1) * 1024],
                    ybuf[:, g8 * 1024:(g8 + 1) * 1024], ALU.mult))
                VE.wait(cV, cV.n)
                run(VE, cV, lambda e, g8=g8: e.reduce_sum(
                    rms_v[:, g8:g8 + 1], xdec[:, :], axis=AX.X))
                VE.wait(cV, cV.n)
            SC.wait(cV, cV.n)
            run(SC, cSc, lambda e: e.activation(rms_r[:, :], rms_v[:, :], AF.Ln,
                                                scale=1.0 / 1024.0, bias=beps[:, 0:1]))
            SC.wait(cSc, cSc.n)
            run(SC, cSc, lambda e: e.activation(rms_r[:, :], rms_r[:, :], AF.Exp, scale=-0.5))
            rms_s = cSc.n
            VE.wait(cSc, rms_s)
            for g8 in range(8):
                run(VE, cV, lambda e, g8=g8: e.tensor_scalar(
                    tmpS[:, g8 * 1024:(g8 + 1) * 1024], ybuf[:, g8 * 1024:(g8 + 1) * 1024],
                    rms_r[:, g8:g8 + 1], None, ALU.mult))
            yn_v = cV.n
            th, t0 = k // 4, (k % 4) * 128
            PE.wait(cV, yn_v)
            for j in range(64):
                pt = pT0 if j % 2 == 0 else pT1
                run(PE, cPE, lambda e, j=j, pt=pt: e.transpose(
                    pt[:, :], tmpS[:, j * 128:(j + 1) * 128], ident_sb[:, :]))
                VE.wait(cPE, cPE.n)
                run(VE, cV, lambda e, j=j, pt=pt: e.tensor_scalar(
                    tmpX[:, j * 128:(j + 1) * 128], pt[:, :], nwt_sb[:, j:j + 1],
                    None, ALU.mult))
                PE.wait(cV, cV.n)
            y_v = cV.n
            tr_pe = cPE.n
            SY.wait(cV, y_v)
            run(SY, cS, lambda e, th=th, t0=t0: e.dma_start(
                out=scr_n[th][:, :, t0:t0 + 128].rearrange("j p t -> p j t"),
                in_=tmpX[:, :].rearrange("p (j t) -> p j t", j=64)), 16)
            nrm_st = cS.mark()
        barrier()
        stB.close()
        st1.close()
        if upto == "ph2b":
            _finish(plan, SY, ALLC)
            _emit_block(nc, st0, plan)
            return nc

        stC = contextlib.ExitStack()
        wbuf0_c = stC.enter_context(nc.sbuf_tensor("wbc0", [128, 64 * 128], BF16))
        wbuf1_c = stC.enter_context(nc.sbuf_tensor("wbc1", [128, 64 * 128], BF16))
        nrm_sb = stC.enter_context(nc.sbuf_tensor("nrm_sb", [128, 64 * 512], BF16))
        gld = stC.enter_context(nc.sbuf_tensor("gld", [128, 512], BF16))
        sg_sb = stC.enter_context(nc.sbuf_tensor("sg_sb", [128, 512], BF16))
        rb4 = stC.enter_context(nc.sbuf_tensor("rb4", [128, 4 * HID], BF16))
        obuf = stC.enter_context(nc.sbuf_tensor("obuf", [128, HID], F32))
        xtc_sb = stC.enter_context(nc.sbuf_tensor("xt_c", [128, HID], F32))
        pT0_c = stC.enter_context(nc.psum_tensor("pT0f", [128, 128], BF16))
        pT1_c = stC.enter_context(nc.psum_tensor("pT1f", [128, 128], BF16))
        pQ0_c = stC.enter_context(nc.psum_tensor("pQ0f", [128, 512], F32))
        pQ1_c = stC.enter_context(nc.psum_tensor("pQ1f", [128, 512], F32))

        oslab_pe = {}
        for th in range(2):
            VE.wait(cV, cV.n)
            SY.wait(cV, cV.n)
            SY.wait(cPE, cPE.n)
            run(SY, cS, lambda e, th=th: e.dma_start(
                out=nrm_sb[:, :].rearrange("p (j t) -> p j t", j=64),
                in_=scr_n[th].rearrange("j p t -> p j t")), 16)
            nrm_ld = cS.mark()
            for ot in range(32):
                wb = wbuf0_c if ot % 2 == 0 else wbuf1_c
                key, prev_key = (th, ot), None
                if ot >= 2:
                    prev_key = (th, ot - 2)
                elif th > 0:
                    prev_key = (th - 1, 30 + ot)
                if prev_key:
                    GP.wait(cPE, oslab_pe[prev_key])
                ctr = cG0 if ot % 2 == 0 else cG1
                run(GP, ctr, lambda e, wb=wb, ot=ot: e.dma_start(
                    out=wb[:, :], in_=w_out_s[ot]), 16)
                w_ld = (ctr, ctr.mark())
                if th > 0 or ot > 0:
                    SC.wait(cV, sg_v)
                run(SC, cS2, lambda e, th=th, ot=ot: e.dma_start(
                    out=gld[:, :], in_=scr_gg[ot * 128:(ot + 1) * 128,
                                             th * 512:(th + 1) * 512]), 16)
                gld_ld = cS2.mark()
                PE.wait(*w_ld)
                PE.wait(cS, nrm_ld)
                pq = pQ0_c if ot % 2 == 0 else pQ1_c
                if prev_key:
                    PE.wait(cV, sg_prev.get(prev_key, 0))
                for cc in range(64):
                    run(PE, cPE, lambda e, wb=wb, cc=cc, pq=pq: e.matmul(
                        pq[:, :], wb[:, cc * 128:(cc + 1) * 128],
                        nrm_sb[:, cc * 512:(cc + 1) * 512],
                        start=(cc == 0), stop=(cc == 63), skip_group_check=True))
                if th == 0 and ot == 0:
                    sg_prev = {}
                oslab_pe[key] = cPE.n
                VE.wait(cPE, cPE.n)
                VE.wait(cS2, gld_ld)
                if th > 0 or ot > 0:
                    VE.wait(cPE, sgT_pe)
                run(VE, cV, lambda e, pq=pq: e.tensor_tensor(
                    sg_sb[:, :], pq[:, :], gld[:, :], ALU.mult))
                sg_v = cV.n
                sg_prev[key] = sg_v
                PE.wait(cV, sg_v)
                for tc in range(4):
                    pt = pT0_c if tc % 2 == 0 else pT1_c
                    run(PE, cPE, lambda e, tc=tc, pt=pt: e.transpose(
                        pt[:, :], sg_sb[:, tc * 128:(tc + 1) * 128], ident_sb[:, :]))
                    VE.wait(cPE, cPE.n)
                    run(VE, cV, lambda e, tc=tc, ot=ot, pt=pt: e.tensor_copy(
                        rb4[:, tc * HID + ot * 128: tc * HID + (ot + 1) * 128], pt[:, :]))
                    PE.wait(cV, cV.n)
                sgT_pe = cPE.n
            for tc in range(4):
                SY.wait(cV, cV.n)
                run(SY, cS, lambda e, th=th, tc=tc: e.dma_start(
                    out=xtc_sb[:, :],
                    in_=x_ext[3 + th * 512 + tc * 128: 3 + th * 512 + (tc + 1) * 128, :]), 16)
                VE.wait(cS, cS.n)
                run(VE, cV, lambda e, tc=tc: e.tensor_tensor(
                    obuf[:, :], xtc_sb[:, :], rb4[:, tc * HID:(tc + 1) * HID], ALU.add))
                SY.wait(cV, cV.n)
                run(SY, cS, lambda e, th=th, tc=tc: e.dma_start(
                    out=out[th * 512 + tc * 128: th * 512 + (tc + 1) * 128, :],
                    in_=obuf[:, :]), 16)
        barrier()
        stC.close()
        _finish(plan, SY, ALLC)
        _emit_block(nc, st0, plan)
    return nc


def barrier_no_ar(plan, SY, GP, PE, VE, SC, ALLC, cGc):
    for Eo in (SY, GP, PE, VE, SC):
        for c in ALLC:
            if c is cGc:
                continue
            Eo.wait(c, c.n)


def _finish(plan, SY, ALLC):
    for c in ALLC:
        SY.wait(c, c.n)


def _emit_block(nc, st0, plan):
    blk = st0.enter_context(nc.Block())

    @blk.sync
    def _(eng):
        for fn in plan["sync"]:
            fn(eng)

    @blk.gpsimd
    def _(eng):
        for fn in plan["gpsimd"]:
            fn(eng)

    @blk.tensor
    def _(eng):
        for fn in plan["tensor"]:
            fn(eng)

    @blk.vector
    def _(eng):
        for fn in plan["vector"]:
            fn(eng)

    @blk.scalar
    def _(eng):
        for fn in plan["scalar"]:
            fn(eng)


def stage_inputs(inputs, n_cores=8):
    f32 = np.float32
    x = np.asarray(inputs["L0_attn_hidden_states"], f32)
    w_in = np.asarray(inputs["L0_attn_in_proj_w"], f32)
    conv_w = np.asarray(inputs["L0_attn_conv_w"], f32)
    conv_b = np.asarray(inputs["L0_attn_conv_b"], f32)
    dt_bias = np.asarray(inputs["L0_attn_dt_bias"], f32)
    a_log = np.asarray(inputs["L0_attn_A_log"], f32)
    D = np.asarray(inputs["L0_attn_D"], f32)
    nw = np.asarray(inputs["L0_attn_norm_weight"], f32)
    w_out = np.asarray(inputs["L0_attn_out_proj_w"], f32)
    ffn_W = np.asarray(inputs["L0_ffn_W"], f32)
    ffn_b = np.asarray(inputs["L0_ffn_b"], f32)
    ffn_A = np.asarray(inputs["L0_ffn_A"], f32)
    ffn_B = np.asarray(inputs["L0_ffn_B"], f32)
    r_W = np.asarray(inputs["L0_ffn_router_W"], f32)
    r_b = np.asarray(inputs["L0_ffn_router_b"], f32)

    w_in_s = np.ascontiguousarray(
        w_in.T.reshape(32, 128, NFT, 128).transpose(2, 1, 0, 3).reshape(NFT, 128, 32 * 128))
    w_out_s = np.ascontiguousarray(
        w_out.T.reshape(64, 128, 32, 128).transpose(2, 1, 0, 3).reshape(32, 128, 64 * 128))
    w_ffn_s = np.ascontiguousarray(
        ffn_W.T.reshape(32, 128, 32, 128).transpose(2, 1, 0, 3).reshape(32, 128, 32 * 128))
    a_s = np.ascontiguousarray(
        ffn_A.reshape(128, HID).T.reshape(32, 128, 128).transpose(1, 0, 2).reshape(128, 32 * 128))
    rwt_s = np.ascontiguousarray(r_W.T.reshape(128, 32 * 8))

    shared = {
        "w_in_s": w_in_s,
        "w_out_s": w_out_s,
        "w_ffn_s": w_ffn_s,
        "a_s": a_s,
        "b_flat": np.ascontiguousarray(np.transpose(ffn_B, (0, 2, 1)).reshape(128, HID)),
        "rwt_s": rwt_s,
        "convw": np.ascontiguousarray(
            conv_w.reshape(80, 128, 4).transpose(1, 0, 2).reshape(128, 320)),
        "convb": np.ascontiguousarray(conv_b.reshape(80, 128).T),
        "dt_bias": np.ascontiguousarray(dt_bias.reshape(128, 1)),
        "a_log": np.ascontiguousarray(a_log.reshape(128, 1)),
        "d128": np.ascontiguousarray(np.tile(D[None, :], (128, 1))),
        "nwt": np.ascontiguousarray(nw.reshape(64, 128).T),
        "router_b": np.ascontiguousarray(r_b.reshape(1, 8)),
        "ffn_bias": np.ascontiguousarray(ffn_b.reshape(32, 128).T),
        "ident": np.eye(128, dtype=ml_dtypes.bfloat16),
        "identf": np.eye(128, dtype=f32),
        "maskfill": np.where(np.arange(128)[None, :] >= np.arange(128)[:, None],
                             0.0, -1e30).astype(f32),
    }
    def half_ext(b, half):
        t0 = half * T
        x_e = np.zeros((TEXT, HID), f32)
        x_e[3:3 + T] = x[b, t0:t0 + T]
        if half == 1:
            x_e[0:3] = x[b, t0 - 3:t0]
        r = np.arange(TEXT)
        gtok = t0 + r - 3
        qm = ((r >= 3) & (r < 3 + T) & (gtok >= 611) & (gtok <= 2046)).astype(f32)
        qm_t = np.zeros((128, 9), f32)
        for ti in range(9):
            qm_t[:, ti] = qm[ti * 128:(ti + 1) * 128]
        return x_e, qm_t

    in_maps = []
    for c in range(n_cores):
        b, half = c // 2, c % 2
        x_e, qm_t = half_ext(b, half)
        x_o, qm2_t = half_ext(b, 1 - half)
        d = dict(shared)
        d["x"] = x_e
        d["x_oth"] = x_o
        d["qmask"] = qm_t.astype(ml_dtypes.bfloat16)
        d["qmask2"] = qm2_t.astype(ml_dtypes.bfloat16)
        d["isodd"] = np.full((128, 1), float(half), f32)
        d["notodd"] = np.full((128, 1), float(1 - half), f32)
        in_maps.append(d)
    return in_maps


def kernel(**inputs):
    in_maps = stage_inputs(inputs, 8)
    nc = build_nc(REPLICA_GROUPS, upto="full")
    res = run_bass_kernel_spmd(nc, in_maps, core_ids=list(range(8)))
    o = np.zeros((4, 2048, HID), np.float32)
    for c in range(8):
        b, half = c // 2, c % 2
        o[b, half * T:(half + 1) * T] = res.results[c]["out"]
    return o


# version 06 (final)
# speedup vs baseline: 1.0652x; 1.0611x over previous
import contextlib
import numpy as np
import ml_dtypes

import concourse.bass as bass
import concourse.mybir as mybir
from concourse.bass_utils import run_bass_kernel_spmd

F32 = mybir.dt.float32
BF16 = mybir.dt.bfloat16
AF = mybir.ActivationFunctionType
ALU = mybir.AluOpType
AX = mybir.AxisListType

HID = 4096
H = 128
P = 64
NST = 128
G = 8
CH = 128
INTER = 8192
CONV_DIM = 10240
PROJ = 18560
T = 1024
TEXT = 1152
NTOK = 1027
MT = 259
MTOK = 256
XNW = 1286
NCHUNK = 8
NFT = 145
FT_HBC0 = 64
FT_DT = 144
EPS = 1e-5
SCALING = 2.0
QWIN = 1436.0

REPLICA_GROUPS = [[0, 1], [2, 3], [4, 5], [6, 7]]


class Ctr:
    def __init__(self, st, nc, name):
        self.sem = st.enter_context(nc.semaphore(name))
        self.n = 0
        self.fence = 0
        self.fence_waited = 0

    def mark(self):
        self.fence = max(self.fence, self.n)
        return self.n


def build_nc(replica_groups, upto="full", dbg=False):
    nc = bass.Bass("TRN2", target_bir_lowering=False, debug=False)
    kdbg = "ExternalOutput" if dbg else "Internal"

    x_ext = nc.dram_tensor("x", [TEXT, HID], F32, kind="ExternalInput").ap()
    x_oth = nc.dram_tensor("x_oth", [TEXT, HID], F32, kind="ExternalInput").ap()
    w_in_s = nc.dram_tensor("w_in_s", [NFT, 128, 32 * 128], F32, kind="ExternalInput").ap()
    w_out_s = nc.dram_tensor("w_out_s", [32, 128, 64 * 128], F32, kind="ExternalInput").ap()
    w_ffn_s = nc.dram_tensor("w_ffn_s", [32, 128, 32 * 128], F32, kind="ExternalInput").ap()
    a_s = nc.dram_tensor("a_s", [128, 32 * 128], F32, kind="ExternalInput").ap()
    b_flat = nc.dram_tensor("b_flat", [128, HID], F32, kind="ExternalInput").ap()
    rwt_s = nc.dram_tensor("rwt_s", [128, 32 * 8], F32, kind="ExternalInput").ap()
    convw = nc.dram_tensor("convw", [128, 320], F32, kind="ExternalInput").ap()
    convb = nc.dram_tensor("convb", [128, 80], F32, kind="ExternalInput").ap()
    dt_bias = nc.dram_tensor("dt_bias", [128, 1], F32, kind="ExternalInput").ap()
    a_log = nc.dram_tensor("a_log", [128, 1], F32, kind="ExternalInput").ap()
    d128 = nc.dram_tensor("d128", [128, 128], F32, kind="ExternalInput").ap()
    nwt = nc.dram_tensor("nwt", [128, 64], F32, kind="ExternalInput").ap()
    router_b = nc.dram_tensor("router_b", [1, 8], F32, kind="ExternalInput").ap()
    ffn_bias = nc.dram_tensor("ffn_bias", [128, 32], F32, kind="ExternalInput").ap()
    ident = nc.dram_tensor("ident", [128, 128], BF16, kind="ExternalInput").ap()
    identf = nc.dram_tensor("identf", [128, 128], F32, kind="ExternalInput").ap()
    maskfill = nc.dram_tensor("maskfill", [128, 128], F32, kind="ExternalInput").ap()
    qmask = nc.dram_tensor("qmask", [128, 9], BF16, kind="ExternalInput").ap()
    qmask2 = nc.dram_tensor("qmask2", [128, 9], BF16, kind="ExternalInput").ap()
    isodd = nc.dram_tensor("isodd", [128, 1], F32, kind="ExternalInput").ap()
    notodd = nc.dram_tensor("notodd", [128, 1], F32, kind="ExternalInput").ap()
    out = nc.dram_tensor("out", [T, HID], F32, kind="ExternalOutput").ap()

    scr_hs = nc.dram_tensor("scr_hs", [NCHUNK, 128, INTER], BF16, kind=kdbg).ap()
    scr_g = nc.dram_tensor("scr_g", [NCHUNK, 128, INTER], BF16, kind=kdbg).ap()
    scr_s = nc.dram_tensor("scr_s", [NCHUNK, 128, INTER], BF16, kind=kdbg).ap()
    scr_g2 = nc.dram_tensor("scr_g2", [NCHUNK, 128, 1024], BF16, kind=kdbg).ap()
    scr_base = nc.dram_tensor("scr_base", [32, 128, T], BF16, kind=kdbg).ap()
    scr_n = nc.dram_tensor("scr_n", [2, 64, 128, 512], BF16, kind=kdbg).ap()
    scr_gg = nc.dram_tensor("scr_gg", [HID, T], BF16, kind=kdbg).ap()
    scr_acs = nc.dram_tensor("scr_acs", [2, 128 * 128], F32, kind=kdbg).ap()
    scr_dexp = nc.dram_tensor("scr_dexp", [2, INTER], F32, kind=kdbg).ap()
    scr_dexp2 = nc.dram_tensor("scr_dexp2", [2, INTER], F32, kind=kdbg).ap()
    scr_acs2 = nc.dram_tensor("scr_acs2", [2, 128 * 128], F32, kind=kdbg).ap()
    scr_route = nc.dram_tensor("scr_route", [1, 8], F32, kind=kdbg).ap()
    scr_q = nc.dram_tensor("scr_q", [1, HID], F32, kind=kdbg).ap()
    scr_hsm = nc.dram_tensor("scr_hsm", [2, 128, INTER], BF16, kind=kdbg).ap()
    scr_nc = nc.dram_tensor("scr_nc", [128, INTER], F32, kind=kdbg).ap()
    dbg_xnt = nc.dram_tensor("dbg_xnt", [128, 32 * XNW], BF16, kind=kdbg).ap() if dbg else None

    ENGS = ["sync", "gpsimd", "tensor", "vector", "scalar"]
    plan = {e: [] for e in ENGS}

    class E:
        def __init__(self, eng):
            self.eng = eng

        def do(self, fn):
            plan[self.eng].append(fn)

        def wait(self, ctr, val=None):
            v = ctr.n if val is None else val
            sem = ctr.sem
            if v > 0:
                ctr.fence = max(ctr.fence, v)
                self.do(lambda e: e.wait_ge(sem, v))

    SY, GP, PE, VE, SC = E("sync"), E("gpsimd"), E("tensor"), E("vector"), E("scalar")

    def run(eng_obj, ctr, fn, k=1):
        if k == 16 and ctr.fence > ctr.fence_waited:
            fv = ctr.fence
            sem = ctr.sem
            eng_obj.do(lambda e: e.wait_ge(sem, fv))
            ctr.fence_waited = fv
        if k == 1 and eng_obj.eng in ("vector", "scalar") and ctr.n > 0:
            sv = ctr.n
            sem = ctr.sem
            eng_obj.do(lambda e: e.wait_ge(sem, sv))

        def g(e):
            fn(e).then_inc(ctr.sem, k)
        eng_obj.do(g)
        ctr.n += k
        return ctr.n

    st0 = contextlib.ExitStack()
    with st0:
        ident_sb = st0.enter_context(nc.sbuf_tensor("ident_sb", [128, 128], BF16))
        identf_sb = st0.enter_context(nc.sbuf_tensor("identf_sb", [128, 128], F32))
        maskf_sb = st0.enter_context(nc.sbuf_tensor("maskf_sb", [128, 128], F32))
        convw_sb = st0.enter_context(nc.sbuf_tensor("convw_sb", [128, 320], F32))
        convb_sb = st0.enter_context(nc.sbuf_tensor("convb_sb", [128, 80], F32))
        dtb_sb = st0.enter_context(nc.sbuf_tensor("dtb_sb", [128, 1], F32))
        alog_sb = st0.enter_context(nc.sbuf_tensor("alog_sb", [128, 1], F32))
        acol_sb = st0.enter_context(nc.sbuf_tensor("acol_sb", [128, 1], F32))
        d128_sb = st0.enter_context(nc.sbuf_tensor("d128_sb", [128, 128], F32))
        nwt_sb = st0.enter_context(nc.sbuf_tensor("nwt_sb", [128, 64], F32))
        rb_sb = st0.enter_context(nc.sbuf_tensor("rb_sb", [1, 8], F32))
        fbias_sb = st0.enter_context(nc.sbuf_tensor("fbias_sb", [128, 32], F32))
        qm_sb = st0.enter_context(nc.sbuf_tensor("qm_sb", [128, 9], BF16))
        qm2_sb = st0.enter_context(nc.sbuf_tensor("qm2_sb", [128, 9], BF16))
        iso_sb = st0.enter_context(nc.sbuf_tensor("iso_sb", [128, 1], F32))
        noto_sb = st0.enter_context(nc.sbuf_tensor("noto_sb", [128, 1], F32))
        bone = st0.enter_context(nc.sbuf_tensor("bone", [128, 1], F32))
        beps = st0.enter_context(nc.sbuf_tensor("beps", [128, 1], F32))
        ones_f = st0.enter_context(nc.sbuf_tensor("ones_f", [128, 128], F32))
        ones_b1 = st0.enter_context(nc.sbuf_tensor("ones_b1", [1, 128], BF16))

        cS = Ctr(st0, nc, "cS")
        cG = Ctr(st0, nc, "cG")
        cG0 = Ctr(st0, nc, "cG0")
        cG1 = Ctr(st0, nc, "cG1")
        cGc = Ctr(st0, nc, "cGc")
        cS2 = Ctr(st0, nc, "cS2")
        cPE = Ctr(st0, nc, "cPE")
        cV = Ctr(st0, nc, "cV")
        cSc = Ctr(st0, nc, "cSc")
        ALLC = [cS, cG, cG0, cG1, cGc, cS2, cPE, cV, cSc]

        def barrier():
            vals = [(c, c.n) for c in ALLC]
            for Eo in (SY, GP, PE, VE, SC):
                for (c, v) in vals:
                    Eo.wait(c, v)

        for (dst, src) in [(ident_sb, ident), (identf_sb, identf), (maskf_sb, maskfill),
                           (convw_sb, convw), (convb_sb, convb), (dtb_sb, dt_bias),
                           (alog_sb, a_log), (d128_sb, d128), (nwt_sb, nwt),
                           (rb_sb, router_b), (fbias_sb, ffn_bias), (qm_sb, qmask),
                           (qm2_sb, qmask2),
                           (iso_sb, isodd), (noto_sb, notodd)]:
            run(SY, cS, lambda e, d=dst, s=src: e.dma_start(out=d[:, :], in_=s), 16)
        c_loads = cS.mark()
        VE.wait(cS, c_loads)
        run(VE, cV, lambda e: e.memset(bone[:, :], 1.0))
        run(VE, cV, lambda e: e.memset(beps[:, :], EPS))
        run(VE, cV, lambda e: e.memset(ones_f[:, :], 1.0))
        run(VE, cV, lambda e: e.memset(ones_b1[:, :], 1.0))
        SC.wait(cS, c_loads)
        run(SC, cSc, lambda e: e.activation(acol_sb[:, :], alog_sb[:, :], AF.Exp))
        VE.wait(cSc, cSc.n)
        run(VE, cV, lambda e: e.tensor_scalar(acol_sb[:, :], acol_sb[:, :], -1.0, None, ALU.mult))
        c_consts_v = cV.n

        st1 = contextlib.ExitStack()
        B_sb = st1.enter_context(nc.sbuf_tensor("B_sb", [128, G * T], BF16))
        C_sb = st1.enter_context(nc.sbuf_tensor("C_sb", [128, G * T], BF16))
        dt_sb = st1.enter_context(nc.sbuf_tensor("dt_sb", [128, T], F32))
        acs_sb = st1.enter_context(nc.sbuf_tensor("acs_sb", [128, T], F32))
        t2raw = st1.enter_context(nc.sbuf_tensor("t2raw", [128, T], BF16))
        bflat_sb = st1.enter_context(nc.sbuf_tensor("bflat_sb", [128, HID], BF16))
        B_m = st1.enter_context(nc.sbuf_tensor("B_m", [128, G * MTOK], BF16))
        dt_m = st1.enter_context(nc.sbuf_tensor("dt_m", [128, MTOK], F32))
        acs_m = st1.enter_context(nc.sbuf_tensor("acs_m", [128, MTOK], F32))
        cdump = st1.enter_context(nc.sbuf_tensor("cdump", [128, MTOK], BF16))
        run(GP, cG, lambda e: e.dma_start(out=bflat_sb[:, :], in_=b_flat), 16)

        stA = contextlib.ExitStack()
        xn_T = stA.enter_context(nc.sbuf_tensor("xn_T", [128, 32 * XNW], BF16))
        wbuf0 = stA.enter_context(nc.sbuf_tensor("wbuf0", [128, 32 * 128], BF16))
        wbuf1 = stA.enter_context(nc.sbuf_tensor("wbuf1", [128, 32 * 128], BF16))

        stA0 = contextlib.ExitStack()
        xt_sb = stA0.enter_context(nc.sbuf_tensor("xt_sb", [128, HID], F32))
        xn_tok = stA0.enter_context(nc.sbuf_tensor("xn_tok", [128, HID], BF16))
        qacc = stA0.enter_context(nc.sbuf_tensor("qacc", [1, HID], F32))
        m_col = stA0.enter_context(nc.sbuf_tensor("m_col", [128, 1], F32))
        nm_col = stA0.enter_context(nc.sbuf_tensor("nm_col", [128, 1], F32))
        s2_col = stA0.enter_context(nc.sbuf_tensor("s2_col", [128, 1], F32))
        v_col = stA0.enter_context(nc.sbuf_tensor("v_col", [128, 1], F32))
        sd_col = stA0.enter_context(nc.sbuf_tensor("sd_col", [128, 1], F32))
        rs_col = stA0.enter_context(nc.sbuf_tensor("rs_col", [128, 1], F32))
        pT0_a0 = stA0.enter_context(nc.psum_tensor("pT0_a0", [128, 128], BF16))
        pT1_a0 = stA0.enter_context(nc.psum_tensor("pT1_a0", [128, 128], BF16))
        pQ0_a0 = stA0.enter_context(nc.psum_tensor("pQ0_a0", [128, 512], F32))
        pQ1_a0 = stA0.enter_context(nc.psum_tensor("pQ1_a0", [128, 512], F32))

        VE.wait(cS, c_loads)
        VE.wait(cV, c_consts_v)
        xt_free_v = 0
        xt_free_s = 0
        xn_free_pe = 0
        qev = {}
        trev = {}
        tile_specs = []
        for ti in range(9):
            tile_specs.append((x_ext, qm_sb, ti, ti * 128, 128 if ti < 8 else 3))
        for ti in range(9):
            tcol = 1027 + (ti - 6) * 128 if ti >= 6 else None
            nco = (128 if ti < 8 else 3) if tcol is not None else 0
            tile_specs.append((x_oth, qm2_sb, ti, tcol, nco))
        for tix, (xsrc, qsb, ti, tcol, ncols) in enumerate(tile_specs):
            SY.wait(cV, xt_free_v)
            SY.wait(cSc, xt_free_s)
            run(SY, cS, lambda e, ti=ti, xsrc=xsrc: e.dma_start(
                out=xt_sb[:, :], in_=xsrc[ti * 128:(ti + 1) * 128, :]), 16)
            ld = cS.mark()
            VE.wait(cS, ld)
            run(VE, cV, lambda e: e.reduce_sum(m_col[:, :], xt_sb[:, :], axis=AX.X))
            VE.wait(cV, cV.n)
            run(VE, cV, lambda e: e.tensor_scalar(nm_col[:, :], m_col[:, :], -1.0 / HID, None, ALU.mult))
            run(VE, cV, lambda e: e.tensor_scalar(m_col[:, :], m_col[:, :], 1.0 / HID, None, ALU.mult))
            nm_v = cV.n
            SC.wait(cV, nm_v)
            SC.wait(cS, ld)
            SC.wait(cPE, xn_free_pe)
            run(SC, cSc, lambda e: e.activation(xn_tok[:, :], xt_sb[:, :], AF.Square,
                                                bias=nm_col[:, 0:1]))
            sq_s = cSc.n
            VE.wait(cSc, sq_s)
            run(VE, cV, lambda e: e.reduce_sum(s2_col[:, :], xn_tok[:, :], axis=AX.X))
            VE.wait(cV, cV.n)
            run(VE, cV, lambda e: e.tensor_scalar(v_col[:, :], s2_col[:, :], 1.0 / HID, None, ALU.mult))
            SC.wait(cV, cV.n)
            run(SC, cSc, lambda e: e.activation(sd_col[:, :], v_col[:, :], AF.Sqrt,
                                                bias=beps[:, 0:1]))
            VE.wait(cSc, cSc.n)
            run(VE, cV, lambda e: e.reciprocal(rs_col[:, :], sd_col[:, :]))
            VE.wait(cV, cV.n)
            run(VE, cV, lambda e: e.tensor_scalar(
                xn_tok[:, :], xt_sb[:, :], m_col[:, 0:1], rs_col[:, 0:1],
                ALU.subtract, ALU.mult))
            xn_v = cV.n
            xt_free_v = xn_v
            xt_free_s = sq_s
            PE.wait(cV, xn_v)
            PE.wait(cS, c_loads)
            for db in range(8):
                pq = pQ0_a0 if db % 2 == 0 else pQ1_a0
                if tix > 0 or db >= 2:
                    prev = qev.get((tix, db - 2) if db >= 2 else (tix - 1, 6 + db))
                    if prev:
                        PE.wait(cV, prev)
                run(PE, cPE, lambda e, db=db, ti=ti, pq=pq, qsb=qsb: e.matmul(
                    pq[0:1, :], qsb[:, ti:ti + 1], xn_tok[:, db * 512:(db + 1) * 512],
                    start=True, stop=True, skip_group_check=True))
                mm = cPE.n
                VE.wait(cPE, mm)
                if tix == 0:
                    run(VE, cV, lambda e, db=db, pq=pq: e.tensor_copy(
                        qacc[0:1, db * 512:(db + 1) * 512], pq[0:1, :]))
                else:
                    run(VE, cV, lambda e, db=db, pq=pq: e.tensor_tensor(
                        qacc[0:1, db * 512:(db + 1) * 512],
                        qacc[0:1, db * 512:(db + 1) * 512], pq[0:1, :], ALU.add))
                qev[(tix, db)] = cV.n
            if tcol is not None:
                for j in range(32):
                    pt = pT0_a0 if j % 2 == 0 else pT1_a0
                    prev = trev.get((tix, j - 2) if j >= 2 else (tix - 1, 30 + j))
                    if prev:
                        PE.wait(cV, prev)
                    run(PE, cPE, lambda e, j=j, pt=pt: e.transpose(
                        pt[:, :], xn_tok[:, j * 128:(j + 1) * 128], ident_sb[:, :]))
                    mm = cPE.n
                    VE.wait(cPE, mm)
                    run(VE, cV, lambda e, j=j, tcol=tcol, pt=pt, ncols=ncols: e.tensor_copy(
                        xn_T[:, j * XNW + tcol: j * XNW + tcol + ncols],
                        pt[:, 0:ncols]))
                    trev[(tix, j)] = cV.n
            xn_free_pe = cPE.n
        SY.wait(cV, cV.n)
        run(SY, cS, lambda e: e.dma_start(out=scr_q, in_=qacc[0:1, :]), 16)
        barrier()
        if dbg and upto == "ph0":
            run(SY, cS, lambda e: e.dma_start(out=dbg_xnt, in_=xn_T[:, :]), 16)
            barrier()
        stA0.close()
        if upto == "ph0":
            stA.close(); st1.close()
            _finish(plan, SY, ALLC)
            _emit_block(nc, st0, plan)
            return nc

        stA1 = contextlib.ExitStack()
        hbc_sb = stA1.enter_context(nc.sbuf_tensor("hbc_sb", [128, 1292], F32))
        gsil_sb = stA1.enter_context(nc.sbuf_tensor("gsil_sb", [128, T], BF16))
        acc0 = stA1.enter_context(nc.sbuf_tensor("acc0", [128, 1280], F32))
        acc1 = stA1.enter_context(nc.sbuf_tensor("acc1", [128, 1280], F32))
        accs = stA1.enter_context(nc.sbuf_tensor("accs", [128, 1280], BF16))
        tbuf8 = stA1.enter_context(nc.sbuf_tensor("tbuf8", [128, T], BF16))
        tb_m = stA1.enter_context(nc.sbuf_tensor("tb_m", [128, MTOK], BF16))
        adt_sb = stA1.enter_context(nc.sbuf_tensor("adt_sb", [128, T], F32))
        pT0_a1 = stA1.enter_context(nc.psum_tensor("pT0b", [128, 128], BF16))
        pT1_a1 = stA1.enter_context(nc.psum_tensor("pT1b", [128, 128], BF16))
        pP = [stA1.enter_context(nc.psum_tensor(f"pP{i}", [128, 429], F32))
              for i in range(6)]
        psets = [(pP[0], pP[1], pP[2]), (pP[3], pP[4], pP[5])]
        TB_GATE = [(3, 343), (346, 343), (689, 338)]
        TB_EXT = [(0, 429), (429, 429), (858, 428)]
        TB_C = [(0, 429), (429, 429), (858, 169)]

        slab_pe = {}
        slab_ld = {}
        evac_done = {}
        last_tr = {0: 0, 1: 0}
        tb_st = 0
        tbm_st = 0

        def slab_dma(ft):
            if ft - 2 in slab_pe:
                GP.wait(cPE, slab_pe[ft - 2])
            wb = wbuf0 if ft % 2 == 0 else wbuf1
            ctr = cG0 if ft % 2 == 0 else cG1
            run(GP, ctr, lambda e, wb=wb, ft=ft: e.dma_start(
                out=wb[:, :], in_=w_in_s[ft]), 16)
            slab_ld[ft] = (ctr, ctr.mark())

        def conv_chain(cf, obase, ibase, ln):
            VE.wait(cV, cV.n)
            run(VE, cV, lambda e, cf=cf: e.tensor_scalar(
                acc0[:, obase:obase + ln], hbc_sb[:, ibase:ibase + ln],
                convw_sb[:, cf * 4: cf * 4 + 1], None, ALU.mult))
            for kk in range(1, 4):
                src = acc0 if kk % 2 == 0 else acc1
                dst = acc1 if kk % 2 == 0 else acc0
            run(VE, cV, lambda e, cf=cf: e.scalar_tensor_tensor(
                acc1[:, obase:obase + ln], hbc_sb[:, ibase + 1:ibase + 1 + ln],
                convw_sb[:, cf * 4 + 1: cf * 4 + 2], acc0[:, obase:obase + ln],
                ALU.mult, ALU.add))
            run(VE, cV, lambda e, cf=cf: e.scalar_tensor_tensor(
                acc0[:, obase:obase + ln], hbc_sb[:, ibase + 2:ibase + 2 + ln],
                convw_sb[:, cf * 4 + 2: cf * 4 + 3], acc1[:, obase:obase + ln],
                ALU.mult, ALU.add))
            run(VE, cV, lambda e, cf=cf: e.scalar_tensor_tensor(
                acc1[:, obase:obase + ln], hbc_sb[:, ibase + 3:ibase + 3 + ln],
                convw_sb[:, cf * 4 + 3: cf * 4 + 4], acc0[:, obase:obase + ln],
                ALU.mult, ALU.add))

        slab_dma(0)
        slab_dma(1)
        for ft in range(NFT):
            wb = wbuf0 if ft % 2 == 0 else wbuf1
            ps3 = psets[ft % 2]
            if ft < FT_HBC0:
                TB = TB_GATE
            elif ft < 136 or ft == FT_DT:
                TB = TB_EXT
            else:
                TB = TB_C
            PE.wait(*slab_ld[ft])
            if ft >= 2:
                ev = evac_done[ft - 2]
                PE.wait(cV, ev[0])
                PE.wait(cSc, ev[1])
            for dc in range(32):
                for tb in range(3):
                    off, ln = TB[tb]
                    ps = ps3[tb]
                    run(PE, cPE, lambda e, ps=ps, wb=wb, off=off, ln=ln, dc=dc: e.matmul(
                        ps[:, 0:ln], wb[:, dc * 128:(dc + 1) * 128],
                        xn_T[:, dc * XNW + off: dc * XNW + off + ln],
                        start=(dc == 0), stop=(dc == 31), skip_group_check=True))
            mm_done = cPE.n
            slab_pe[ft] = mm_done
            if ft + 2 < NFT:
                slab_dma(ft + 2)
            if ft < FT_HBC0:
                SC.wait(cPE, mm_done)
                if ft >= 1:
                    SC.wait(cPE, gs_tr_pe)
                for tb in range(3):
                    off, ln = TB[tb]
                    ps = ps3[tb]
                    run(SC, cSc, lambda e, ps=ps, off=off, ln=ln: e.activation(
                        gsil_sb[:, off - 3:off - 3 + ln], ps[:, 0:ln], AF.Silu))
                ev_s = cSc.n
                PE.wait(cSc, ev_s)
                for j in range(8):
                    pt = pT0_a1 if j % 2 == 0 else pT1_a1
                    PE.wait(cV, last_tr[j % 2])
                    run(PE, cPE, lambda e, j=j, pt=pt: e.transpose(
                        pt[:, :], gsil_sb[:, j * 128:(j + 1) * 128], ident_sb[:, :]))
                    VE.wait(cPE, cPE.n)
                    VE.wait(cS, tb_st)
                    run(VE, cV, lambda e, j=j, pt=pt: e.tensor_copy(
                        tbuf8[:, j * 128:(j + 1) * 128], pt[:, :]))
                    last_tr[j % 2] = cV.n
                gs_tr_pe = cPE.n
                SY.wait(cV, cV.n)
                run(SY, cS, lambda e, ft=ft: e.dma_start(
                    out=scr_g[:, :, ft * 128:(ft + 1) * 128].rearrange("k t f -> t k f"),
                    in_=tbuf8[:, :].rearrange("t (k f) -> t k f", k=8)), 16)
                tb_st = cS.mark()
                evac_done[ft] = (cV.n, ev_s)
            elif ft < FT_DT:
                is_ext = ft < 136
                VE.wait(cPE, mm_done)
                VE.wait(cSc, cSc.n)
                for tb in range(3):
                    off, ln = TB[tb]
                    ps = ps3[tb]
                    run(VE, cV, lambda e, ps=ps, off=off, ln=ln: e.tensor_copy(
                        hbc_sb[:, off:off + ln], ps[:, 0:ln]))
                cf = ft - FT_HBC0
                conv_chain(cf, 0, 0, T)
                if is_ext:
                    conv_chain(cf, T, 1027, MTOK)
                conv_v = cV.n
                SC.wait(cV, conv_v)
                if ft < 128:
                    hsi = ft - FT_HBC0
                    if hsi >= 1:
                        SC.wait(cPE, hs_tr_pe)
                    run(SC, cSc, lambda e, cf=cf: e.activation(
                        accs[:, :], acc1[:, :], AF.Silu, bias=convb_sb[:, cf:cf + 1]))
                    ev_s = cSc.n
                    PE.wait(cSc, ev_s)
                    for j in range(10):
                        pt = pT0_a1 if j % 2 == 0 else pT1_a1
                        PE.wait(cV, last_tr[j % 2])
                        run(PE, cPE, lambda e, j=j, pt=pt: e.transpose(
                            pt[:, :], accs[:, j * 128:(j + 1) * 128], ident_sb[:, :]))
                        VE.wait(cPE, cPE.n)
                        if j < 8:
                            VE.wait(cS, tb_st)
                            run(VE, cV, lambda e, j=j, pt=pt: e.tensor_copy(
                                tbuf8[:, j * 128:(j + 1) * 128], pt[:, :]))
                        else:
                            VE.wait(cS, tbm_st)
                            run(VE, cV, lambda e, j=j, pt=pt: e.tensor_copy(
                                tb_m[:, (j - 8) * 128:(j - 7) * 128], pt[:, :]))
                        last_tr[j % 2] = cV.n
                    hs_tr_pe = cPE.n
                    SY.wait(cV, cV.n)
                    run(SY, cS, lambda e, hsi=hsi: e.dma_start(
                        out=scr_hs[:, :, hsi * 128:(hsi + 1) * 128].rearrange("k t f -> t k f"),
                        in_=tbuf8[:, :].rearrange("t (k f) -> t k f", k=8)), 16)
                    tb_st = cS.mark()
                    run(SY, cS, lambda e, hsi=hsi: e.dma_start(
                        out=scr_hsm[:, :, hsi * 128:(hsi + 1) * 128].rearrange("k t f -> t k f"),
                        in_=tb_m[:, :].rearrange("t (k f) -> t k f", k=2)), 16)
                    tbm_st = cS.mark()
                    evac_done[ft] = (cV.n, ev_s)
                else:
                    gi = ft - 128
                    dst = B_sb if gi < 8 else C_sb
                    gg = gi % 8
                    run(SC, cSc, lambda e, cf=cf, dst=dst, gg=gg: e.activation(
                        dst[:, gg * T:(gg + 1) * T], acc1[:, 0:T], AF.Silu,
                        bias=convb_sb[:, cf:cf + 1]))
                    if gi < 8:
                        SC.wait(cSc, cSc.n)
                        run(SC, cSc, lambda e, cf=cf, gg=gg: e.activation(
                            B_m[:, gg * MTOK:(gg + 1) * MTOK], acc1[:, T:1280], AF.Silu,
                            bias=convb_sb[:, cf:cf + 1]))
                    evac_done[ft] = (cV.n, cSc.n)
            else:
                VE.wait(cPE, mm_done)
                VE.wait(cSc, cSc.n)
                for tb in range(3):
                    off, ln = TB[tb]
                    ps = ps3[tb]
                    run(VE, cV, lambda e, ps=ps, off=off, ln=ln: e.tensor_copy(
                        hbc_sb[:, off:off + ln], ps[:, 0:ln]))
                ev_v = cV.n
                SC.wait(cV, ev_v)
                run(SC, cSc, lambda e: e.activation(
                    acc0[:, 0:T], hbc_sb[:, 3:3 + T], AF.Exp, bias=dtb_sb[:, 0:1]))
                SC.wait(cSc, cSc.n)
                run(SC, cSc, lambda e: e.activation(
                    dt_sb[:, :], acc0[:, 0:T], AF.Ln, bias=bone[:, 0:1]))
                SC.wait(cSc, cSc.n)
                run(SC, cSc, lambda e: e.activation(
                    acc0[:, T:1280], hbc_sb[:, 1030:1286], AF.Exp, bias=dtb_sb[:, 0:1]))
                SC.wait(cSc, cSc.n)
                run(SC, cSc, lambda e: e.activation(
                    dt_m[:, :], acc0[:, T:1280], AF.Ln, bias=bone[:, 0:1]))
                dt_s = cSc.n
                VE.wait(cSc, dt_s)
                run(VE, cV, lambda e: e.tensor_scalar(
                    adt_sb[:, :], dt_sb[:, :], acol_sb[:, 0:1], None, ALU.mult))
                run(VE, cV, lambda e: e.tensor_scalar(
                    acc1[:, 0:MTOK], dt_m[:, :], acol_sb[:, 0:1], None, ALU.mult))
                VE.wait(cV, cV.n)
                for k in range(NCHUNK):
                    run(VE, cV, lambda e, k=k: e.tensor_tensor_scan(
                        acs_sb[:, k * 128:(k + 1) * 128], ones_f[:, :],
                        adt_sb[:, k * 128:(k + 1) * 128], 0.0, ALU.mult, ALU.add))
                for k in range(2):
                    run(VE, cV, lambda e, k=k: e.tensor_tensor_scan(
                        acs_m[:, k * 128:(k + 1) * 128], ones_f[:, :],
                        acc1[:, k * 128:(k + 1) * 128], 0.0, ALU.mult, ALU.add))
                evac_done[ft] = (cV.n, cSc.n)
        barrier()
        stA1.close()
        if upto == "ph1":
            stA.close(); st1.close()
            _finish(plan, SY, ALLC)
            _emit_block(nc, st0, plan)
            return nc

        stA3 = contextlib.ExitStack()
        bse_a3 = stA3.enter_context(nc.sbuf_tensor("bse_a3", [128, T], BF16))
        pQ0_a3 = stA3.enter_context(nc.psum_tensor("pQ0d", [128, 512], F32))
        pQ1_a3 = stA3.enter_context(nc.psum_tensor("pQ1d", [128, 512], F32))
        run(GP, cG0, lambda e: e.dma_start(out=wbuf0[:, :], in_=a_s), 16)
        a_ld = cG0.mark()
        PE.wait(cG0, a_ld)
        for dc in range(32):
            for blk in range(2):
                pq = pQ0_a3 if blk == 0 else pQ1_a3
                run(PE, cPE, lambda e, pq=pq, blk=blk, dc=dc: e.matmul(
                    pq[:, :], wbuf0[:, dc * 128:(dc + 1) * 128],
                    xn_T[:, dc * XNW + 3 + blk * 512: dc * XNW + 3 + (blk + 1) * 512],
                    start=(dc == 0), stop=(dc == 31), skip_group_check=True))
        for blk in range(2):
            pq = pQ0_a3 if blk == 0 else pQ1_a3
            VE.wait(cPE, cPE.n)
            run(VE, cV, lambda e, pq=pq, blk=blk: e.tensor_copy(
                t2raw[:, blk * 512:(blk + 1) * 512], pq[:, :]))
        t2_v = cV.n
        base_pe = {}
        for ot in range(32):
            wb = wbuf1 if ot % 2 == 0 else wbuf0
            if ot == 0:
                GP.wait(cPE, cPE.n)
            elif ot == 1:
                GP.wait(cPE, cPE.n)
            else:
                GP.wait(cPE, base_pe[ot - 2])
            ctr = cG1 if ot % 2 == 0 else cG0
            run(GP, ctr, lambda e, wb=wb, ot=ot: e.dma_start(
                out=wb[:, :], in_=w_ffn_s[ot]), 16)
            PE.wait(ctr, ctr.mark())
            if ot == 0:
                PE.wait(cV, t2_v)
            if ot >= 1:
                PE.wait(cV, bse_v)
            for dc in range(32):
                for blk in range(2):
                    pq = pQ0_a3 if blk == 0 else pQ1_a3
                    run(PE, cPE, lambda e, pq=pq, wb=wb, blk=blk, dc=dc: e.matmul(
                        pq[:, :], wb[:, dc * 128:(dc + 1) * 128],
                        xn_T[:, dc * XNW + 3 + blk * 512: dc * XNW + 3 + (blk + 1) * 512],
                        start=(dc == 0), stop=(dc == 31), skip_group_check=True))
            base_pe[ot] = cPE.n
            VE.wait(cPE, cPE.n)
            if ot >= 1:
                VE.wait(cS, bse_st)
            run(VE, cV, lambda e: e.tensor_copy(bse_a3[:, 0:512], pQ0_a3[:, :]))
            run(VE, cV, lambda e: e.tensor_copy(bse_a3[:, 512:1024], pQ1_a3[:, :]))
            bse_v = cV.n
            SY.wait(cV, bse_v)
            run(SY, cS, lambda e, ot=ot: e.dma_start(out=scr_base[ot], in_=bse_a3[:, :]), 16)
            bse_st = cS.mark()
        barrier()
        stA3.close()
        stA.close()
        if upto == "ph15a":
            st1.close()
            _finish(plan, SY, ALLC)
            _emit_block(nc, st0, plan)
            return nc

        stA2 = contextlib.ExitStack()
        N_a2 = stA2.enter_context(nc.sbuf_tensor("N_a2", [128, INTER], F32))
        tmpX_a2 = stA2.enter_context(nc.sbuf_tensor("tmpX_a2", [128, INTER], BF16))
        sgrp0 = stA2.enter_context(nc.sbuf_tensor("sgrp0", [128, 1024], BF16))
        sgrp1 = stA2.enter_context(nc.sbuf_tensor("sgrp1", [128, 1024], BF16))
        g2bf_a2 = stA2.enter_context(nc.sbuf_tensor("g2bf_a2", [128, 1024], BF16))
        xdec_a2 = stA2.enter_context(nc.sbuf_tensor("xdec_a2", [128, 1024], BF16))
        decays = stA2.enter_context(nc.sbuf_tensor("decays", [128, 128], F32))
        dtdec = stA2.enter_context(nc.sbuf_tensor("dtdec", [128, 128], F32))
        dtdecT = stA2.enter_context(nc.sbuf_tensor("dtdecT", [128, 128], F32))
        btg = stA2.enter_context(nc.sbuf_tensor("btg", [128, 128], BF16))
        dexp_a2 = stA2.enter_context(nc.sbuf_tensor("dexp_a2", [128, 1], F32))
        pT0_a2 = stA2.enter_context(nc.psum_tensor("pT0c", [128, 128], BF16))
        pTf_a2 = stA2.enter_context(nc.psum_tensor("pTfc", [128, 128], F32))
        pQ0_a2 = stA2.enter_context(nc.psum_tensor("pQ0c", [128, 512], F32))
        pSt = stA2.enter_context(nc.psum_tensor("pStc", [128, 1024], F32))

        run(VE, cV, lambda e: e.memset(N_a2[:, :], 0.0))
        st_pe = 0
        for mk in range(2):
            SY.wait(cV, cV.n)
            run(SY, cS, lambda e, mk=mk: e.dma_start(out=tmpX_a2[:, :], in_=scr_hsm[mk]), 16)
            hs_ld = cS.mark()
            SC.wait(cV, cV.n)
            run(SC, cSc, lambda e, mk=mk: e.activation(
                decays[:, :], acs_m[:, mk * 128:(mk + 1) * 128], AF.Exp, scale=-1.0,
                bias=acs_m[:, mk * 128 + 127: mk * 128 + 128]))
            if mk == 0:
                SC.wait(cSc, cSc.n)
                run(SC, cSc, lambda e: e.activation(
                    dexp_a2[:, :], acs_m[:, 255:256], AF.Exp))
            VE.wait(cSc, cSc.n)
            run(VE, cV, lambda e, mk=mk: e.tensor_tensor(
                dtdec[:, :], dt_m[:, mk * 128:(mk + 1) * 128], decays[:, :], ALU.mult))
            if mk == 0:
                VE.wait(cV, cV.n)
                run(VE, cV, lambda e: e.tensor_scalar(
                    dtdec[:, :], dtdec[:, :], dexp_a2[:, 0:1], None, ALU.mult))
            PE.wait(cV, cV.n)
            run(PE, cPE, lambda e: e.transpose(pTf_a2[:, :], dtdec[:, :], identf_sb[:, :]))
            VE.wait(cPE, cPE.n)
            run(VE, cV, lambda e: e.tensor_copy(dtdecT[:, :], pTf_a2[:, :]))
            dtdecT_v = cV.n
            VE.wait(cS, hs_ld)
            for g in range(G):
                PE.wait(cV, cV.n)
                run(PE, cPE, lambda e, g=g, mk=mk: e.transpose(
                    pT0_a2[:, :], B_m[:, g * MTOK + mk * 128: g * MTOK + (mk + 1) * 128],
                    ident_sb[:, :]))
                VE.wait(cPE, cPE.n)
                run(VE, cV, lambda e: e.tensor_copy(btg[:, :], pT0_a2[:, :]))
                VE.wait(cV, dtdecT_v)
                if st_pe:
                    VE.wait(cPE, st_pe)
                run(VE, cV, lambda e, g=g: e.tensor_tensor(
                    xdec_a2[:, :].rearrange("l (i u p) -> l i u p", u=1, p=64),
                    tmpX_a2[:, :].rearrange("l (i g p) -> l i g p", g=8, p=64)[
                        :, :, g:g + 1, :],
                    dtdecT[:, :].rearrange("l (i g) -> l i g", g=8)[
                        :, :, g:g + 1].rearrange("l i (g u) -> l i g u", u=1
                        ).broadcast_to((128, 16, 1, 64)),
                    ALU.mult))
                xd_v = cV.n
                PE.wait(cV, xd_v)
                for i in range(16):
                    run(PE, cPE, lambda e, i=i: e.matmul(
                        pSt[:, i * 64:(i + 1) * 64], btg[:, :], xdec_a2[:, i * 64:(i + 1) * 64],
                        start=True, stop=True, skip_group_check=True))
                st_pe = cPE.n
                VE.wait(cPE, st_pe)
                run(VE, cV, lambda e, g=g: e.tensor_tensor(
                    N_a2[:, g * 1024:(g + 1) * 1024], N_a2[:, g * 1024:(g + 1) * 1024],
                    pSt[:, :], ALU.add))
        VE.wait(cV, cV.n)
        run(VE, cV, lambda e: e.tensor_scalar(N_a2[:, :], N_a2[:, :], iso_sb[:, 0:1], None, ALU.mult))
        SY.wait(cV, cV.n)
        run(SY, cS, lambda e: e.dma_start(out=scr_nc, in_=N_a2[:, :]), 16)

        hs_ld = 0
        sg_st = {}
        for k in range(NCHUNK):
            SY.wait(cV, cV.n)
            run(SY, cS, lambda e, k=k: e.dma_start(out=tmpX_a2[:, :], in_=scr_hs[k]), 16)
            hs_ld = cS.mark()
            SC.wait(cV, cV.n)
            run(SC, cSc, lambda e, k=k: e.activation(
                decays[:, :], acs_sb[:, k * 128:(k + 1) * 128], AF.Exp, scale=-1.0,
                bias=acs_sb[:, k * 128 + 127: k * 128 + 128]))
            VE.wait(cSc, cSc.n)
            run(VE, cV, lambda e, k=k: e.tensor_tensor(
                dtdec[:, :], dt_sb[:, k * 128:(k + 1) * 128], decays[:, :], ALU.mult))
            PE.wait(cV, cV.n)
            run(PE, cPE, lambda e: e.transpose(pTf_a2[:, :], dtdec[:, :], identf_sb[:, :]))
            VE.wait(cPE, cPE.n)
            run(VE, cV, lambda e: e.tensor_copy(dtdecT[:, :], pTf_a2[:, :]))
            dtdecT_v = cV.n
            for g in range(G):
                PE.wait(cV, cV.n)
                run(PE, cPE, lambda e, g=g, k=k: e.matmul(
                    pQ0_a2[:, 0:128], B_sb[:, g * T + k * 128: g * T + (k + 1) * 128],
                    C_sb[:, g * T + k * 128: g * T + (k + 1) * 128],
                    start=True, stop=True, skip_group_check=True))
                VE.wait(cPE, cPE.n)
                if k > 0 and g == 0:
                    VE.wait(cS, g2_st)
                run(VE, cV, lambda e, g=g: e.tensor_copy(
                    g2bf_a2[:, g * 128:(g + 1) * 128], pQ0_a2[:, 0:128]))
            SY.wait(cV, cV.n)
            run(SY, cS, lambda e, k=k: e.dma_start(out=scr_g2[k], in_=g2bf_a2[:, :]), 16)
            g2_st = cS.mark()
            VE.wait(cS, hs_ld)
            for g in range(G):
                sg = sgrp0 if g % 2 == 0 else sgrp1
                PE.wait(cV, cV.n)
                run(PE, cPE, lambda e, g=g, k=k: e.transpose(
                    pT0_a2[:, :], B_sb[:, g * T + k * 128: g * T + (k + 1) * 128], ident_sb[:, :]))
                VE.wait(cPE, cPE.n)
                run(VE, cV, lambda e: e.tensor_copy(btg[:, :], pT0_a2[:, :]))
                VE.wait(cV, dtdecT_v)
                VE.wait(cPE, st_pe)
                run(VE, cV, lambda e, g=g: e.tensor_tensor(
                    xdec_a2[:, :].rearrange("l (i u p) -> l i u p", u=1, p=64),
                    tmpX_a2[:, :].rearrange("l (i g p) -> l i g p", g=8, p=64)[
                        :, :, g:g + 1, :],
                    dtdecT[:, :].rearrange("l (i g) -> l i g", g=8)[
                        :, :, g:g + 1].rearrange("l i (g u) -> l i g u", u=1
                        ).broadcast_to((128, 16, 1, 64)),
                    ALU.mult))
                xd_v = cV.n
                PE.wait(cV, xd_v)
                for i in range(16):
                    run(PE, cPE, lambda e, i=i: e.matmul(
                        pSt[:, i * 64:(i + 1) * 64], btg[:, :], xdec_a2[:, i * 64:(i + 1) * 64],
                        start=True, stop=True, skip_group_check=True))
                st_pe = cPE.n
                VE.wait(cPE, st_pe)
                prev_st = sg_st.get((k, g - 2)) or (sg_st.get((k - 1, g + 6)) if k > 0 else None)
                if prev_st:
                    VE.wait(cS, prev_st)
                run(VE, cV, lambda e, sg=sg: e.tensor_copy(sg[:, :], pSt[:, :]))
                SY.wait(cV, cV.n)
                run(SY, cS, lambda e, k=k, g=g, sg=sg: e.dma_start(
                    out=scr_s[k][:, g * 1024:(g + 1) * 1024], in_=sg[:, :]), 16)
                sg_st[(k, g)] = cS.mark()
        barrier()
        stA2.close()
        if upto == "ph2a":
            st1.close()
            _finish(plan, SY, ALLC)
            _emit_block(nc, st0, plan)
            return nc

        stB = contextlib.ExitStack()
        N_sb = stB.enter_context(nc.sbuf_tensor("N_sb2", [128, INTER], F32))
        tmpNb = stB.enter_context(nc.sbuf_tensor("tmpNb", [128, INTER], BF16))
        tmpX = stB.enter_context(nc.sbuf_tensor("tmpX2", [128, INTER], BF16))
        tmpS = stB.enter_context(nc.sbuf_tensor("tmpS", [128, INTER], BF16))
        ybuf = stB.enter_context(nc.sbuf_tensor("ybuf", [128, INTER], F32))
        fill = stB.enter_context(nc.sbuf_tensor("fill", [128, 2048], F32))
        lmat = stB.enter_context(nc.sbuf_tensor("lmat", [128, 2048], BF16))
        g2bf = stB.enter_context(nc.sbuf_tensor("g2bf2", [128, 1024], BF16))
        xdec = stB.enter_context(nc.sbuf_tensor("xdec2", [128, 1024], BF16))
        gtmp = stB.enter_context(nc.sbuf_tensor("gtmp", [128, T], F32))
        dflat_b = stB.enter_context(nc.sbuf_tensor("dflat_b", [1, 2048], BF16))
        gout = stB.enter_context(nc.sbuf_tensor("gout", [128, T], BF16))
        t2T = stB.enter_context(nc.sbuf_tensor("t2T", [128, T], BF16))
        bse = stB.enter_context(nc.sbuf_tensor("bse2", [128, T], BF16))
        qar = stB.enter_context(nc.sbuf_tensor("qar", [128, 32], F32))
        qar_bf = stB.enter_context(nc.sbuf_tensor("qar_bf", [128, 32], BF16))
        rwt_sb = stB.enter_context(nc.sbuf_tensor("rwt_sb", [128, 256], BF16))
        route_col = stB.enter_context(nc.sbuf_tensor("route_col", [128, 1], F32))
        lg_sb = stB.enter_context(nc.sbuf_tensor("lg_sb", [1, 8], F32))
        mx_sb = stB.enter_context(nc.sbuf_tensor("mx_sb", [1, 1], F32))
        s1_sb = stB.enter_context(nc.sbuf_tensor("s1_sb", [1, 1], F32))
        r1_sb = stB.enter_context(nc.sbuf_tensor("r1_sb", [1, 1], F32))
        acsT = stB.enter_context(nc.sbuf_tensor("acsT", [128, 128], F32))
        sdoT = stB.enter_context(nc.sbuf_tensor("sdoT", [128, 128], F32))
        dtT = stB.enter_context(nc.sbuf_tensor("dtT", [128, 128], F32))
        dexp_c = stB.enter_context(nc.sbuf_tensor("dexp_c2", [128, 1], F32))
        dxrow_b = stB.enter_context(nc.sbuf_tensor("dxrow_b", [128, 64], F32))
        rms_v = stB.enter_context(nc.sbuf_tensor("rms_v", [128, 8], F32))
        rms_r = stB.enter_context(nc.sbuf_tensor("rms_r", [128, 8], F32))
        pT0 = stB.enter_context(nc.psum_tensor("pT0e", [128, 128], BF16))
        pT1 = stB.enter_context(nc.psum_tensor("pT1e", [128, 128], BF16))
        pTf = stB.enter_context(nc.psum_tensor("pTfe", [128, 128], F32))
        pQ0 = stB.enter_context(nc.psum_tensor("pQ0e", [128, 512], F32))
        pQ1 = stB.enter_context(nc.psum_tensor("pQ1e", [128, 512], F32))
        pYd = stB.enter_context(nc.psum_tensor("pYde", [128, 1024], F32))

        run(GP, cG, lambda e: e.dma_start(out=rwt_sb[:, :], in_=rwt_s), 16)
        rwt_ld = cG.mark()
        run(SY, cS, lambda e: e.dma_start(
            out=qar[:, :],
            in_=scr_q.rearrange("a (p c) -> (a p) c", p=128)), 16)
        qar_ld = cS.mark()
        VE.wait(cS, qar_ld)
        run(VE, cV, lambda e: e.tensor_copy(qar_bf[:, :], qar[:, :]))
        PE.wait(cV, cV.n)
        PE.wait(cG, rwt_ld)
        for dc in range(32):
            run(PE, cPE, lambda e, dc=dc: e.matmul(
                pQ0[0:1, 0:8], qar_bf[:, dc:dc + 1], rwt_sb[:, dc * 8:(dc + 1) * 8],
                start=(dc == 0), stop=(dc == 31), skip_group_check=True))
        VE.wait(cPE, cPE.n)
        run(VE, cV, lambda e: e.scalar_tensor_tensor(
            lg_sb[:, :], pQ0[0:1, 0:8], 1.0 / QWIN, rb_sb[:, :], ALU.mult, ALU.add))
        VE.wait(cV, cV.n)
        run(VE, cV, lambda e: e.reduce_max(mx_sb[:, :], lg_sb[:, :], axis=AX.X))
        VE.wait(cV, cV.n)
        run(VE, cV, lambda e: e.tensor_scalar(
            lg_sb[:, :], lg_sb[:, :], mx_sb[0:1, 0:1], None, ALU.subtract))
        SC.wait(cV, cV.n)
        run(SC, cSc, lambda e: e.activation(lg_sb[:, :], lg_sb[:, :], AF.Exp))
        VE.wait(cSc, cSc.n)
        run(VE, cV, lambda e: e.reduce_sum(s1_sb[:, :], lg_sb[:, :], axis=AX.X))
        VE.wait(cV, cV.n)
        run(VE, cV, lambda e: e.reciprocal(r1_sb[:, :], s1_sb[:, :]))
        VE.wait(cV, cV.n)
        run(VE, cV, lambda e: e.tensor_scalar(
            lg_sb[:, :], lg_sb[:, :], r1_sb[0:1, 0:1], SCALING, ALU.mult, ALU.mult))
        SY.wait(cV, cV.n)
        run(SY, cS, lambda e: e.dma_start(out=scr_route, in_=lg_sb[:, :]), 16)
        SY.wait(cS, cS.n)
        for ee in range(8):
            run(SY, cS, lambda e, ee=ee: e.dma_start(
                out=route_col[ee * 16:(ee + 1) * 16, 0:1],
                in_=scr_route[0:1, ee:ee + 1].broadcast_to((16, 1))), 16)
        VE.wait(cS, cS.n)
        run(VE, cV, lambda e: e.tensor_scalar(
            t2T[:, :], t2raw[:, :], route_col[:, 0:1], None, ALU.mult))
        t2T_v = cV.n
        PE.wait(cV, t2T_v)
        for ot in range(32):
            if ot > 0:
                SY.wait(cV, g_ev)
            run(SY, cS, lambda e, ot=ot: e.dma_start(out=bse[:, :], in_=scr_base[ot]), 16)
            bse_ld = cS.mark()
            if ot > 0:
                PE.wait(cV, g_ev)
            for blk in range(2):
                pq = pQ0 if blk == 0 else pQ1
                run(PE, cPE, lambda e, ot=ot, blk=blk, pq=pq: e.matmul(
                    pq[:, :], bflat_sb[:, ot * 128:(ot + 1) * 128],
                    t2T[:, blk * 512:(blk + 1) * 512],
                    start=True, stop=True, skip_group_check=True))
            lora_pe = cPE.n
            VE.wait(cPE, lora_pe)
            VE.wait(cS, bse_ld)
            if ot > 0:
                VE.wait(cSc, g_sil)
            run(VE, cV, lambda e: e.tensor_tensor(gtmp[:, 0:512], bse[:, 0:512], pQ0[:, :], ALU.add))
            run(VE, cV, lambda e: e.tensor_tensor(gtmp[:, 512:1024], bse[:, 512:1024], pQ1[:, :], ALU.add))
            g_ev = cV.n
            SC.wait(cV, g_ev)
            if ot > 0:
                SC.wait(cS2, gg_st)
            run(SC, cSc, lambda e, ot=ot: e.activation(
                gout[:, :], gtmp[:, :], AF.Silu, bias=fbias_sb[:, ot:ot + 1]))
            g_sil = cSc.n
            SC.wait(cSc, g_sil)
            run(SC, cS2, lambda e, ot=ot: e.dma_start(
                out=scr_gg[ot * 128:(ot + 1) * 128, :], in_=gout[:, :]), 16)
            gg_st = cS2.mark()
        if upto == "ph15b":
            barrier()
            stB.close(); st1.close()
            _finish(plan, SY, ALLC)
            _emit_block(nc, st0, plan)
            return nc

        run(SY, cS, lambda e: e.dma_start(out=N_sb[:, :], in_=scr_nc), 16)
        nc_ld = cS.mark()
        VE.wait(cS, nc_ld)
        n_v = cV.n
        y_v = 0
        nrm_st = 0
        for k in range(NCHUNK):
            SY.wait(cV, y_v)
            if k > 0:
                SY.wait(cS, nrm_st)
            run(SY, cS, lambda e, k=k: e.dma_start(out=tmpX[:, :], in_=scr_hs[k]), 16)
            hs_ld = cS.mark()
            SY.wait(cV, nupd2_v if k > 0 else 0)
            if k > 0:
                SY.wait(cPE, tr_pe)
            run(SY, cS, lambda e, k=k: e.dma_start(out=tmpS[:, :], in_=scr_s[k]), 16)
            s_ld = cS.mark()
            run(SY, cS, lambda e, k=k: e.dma_start(out=g2bf[:, :], in_=scr_g2[k]), 16)
            g2_ld = cS.mark()
            VE.wait(cV, n_v)
            if k > 0:
                VE.wait(cPE, yoff_pe)
            run(VE, cV, lambda e: e.tensor_copy(tmpNb[:, :], N_sb[:, :]))
            nb_v = cV.n
            SC.wait(cSc, cSc.n)
            SC.wait(cS2, cS2.n)
            run(SC, cSc, lambda e, k=k: e.activation(
                dexp_c[:, :], acs_sb[:, k * 128 + 127: k * 128 + 128], AF.Exp))
            VE.wait(cSc, cSc.n)
            VE.wait(cS2, cS2.n)
            run(VE, cV, lambda e: e.tensor_copy(
                dxrow_b[:, :].rearrange("h (a u) -> h a u", u=64),
                dexp_c[:, 0:1].rearrange("h (a u) -> h a u", u=1).broadcast_to((128, 1, 64))))
            SC.wait(cV, cV.n)
            run(SC, cS2, lambda e, k=k: e.dma_start(
                out=scr_dexp[k % 2:k % 2 + 1, :].rearrange("a (h u) -> h (a u)", h=128, u=64),
                in_=dxrow_b[:, :]), 16)
            SC.wait(cS2, cS2.n)
            run(SC, cS2, lambda e, k=k: e.dma_start(
                out=scr_dexp2[k % 2:k % 2 + 1, :].rearrange(
                    "a (g i p) -> a g i p", g=8, i=16, p=64),
                in_=scr_dexp[k % 2:k % 2 + 1, :].rearrange(
                    "a (i g p) -> a g i p", i=16, g=8, p=64)), 16)
            hop2_done = cS2.mark()
            nmul2_v = nb_v
            for hf in range(4):
                GP.wait(cS2, hop2_done)
                if k > 0 or hf > 0:
                    GP.wait(cPE, dfl2_pe)
                run(GP, cG, lambda e, k=k, hf=hf: e.dma_start(
                    out=dflat_b[0:1, :],
                    in_=scr_dexp2[k % 2:k % 2 + 1, hf * 2048:(hf + 1) * 2048]), 16)
                dfl2_ld = cG.mark()
                PE.wait(cG, dfl2_ld)
                PE.wait(cV, nb_v)
                for sg8 in range(4):
                    pb = pQ0 if sg8 % 2 == 0 else pQ1
                    if k > 0 or hf > 0 or sg8 >= 2:
                        PE.wait(cV, nmul2_v)
                    run(PE, cPE, lambda e, sg8=sg8, pb=pb: e.matmul(
                        pb[:, :], ones_b1[0:1, :], dflat_b[0:1, sg8 * 512:(sg8 + 1) * 512],
                        start=True, stop=True, skip_group_check=True))
                    VE.wait(cPE, cPE.n)
                    off = hf * 2048 + sg8 * 512
                    run(VE, cV, lambda e, off=off, pb=pb: e.tensor_tensor(
                        N_sb[:, off:off + 512], N_sb[:, off:off + 512], pb[:, :],
                        ALU.mult))
                    nmul2_v = cV.n
                dfl2_pe = cPE.n
            VE.wait(cV, nmul2_v)
            VE.wait(cS, s_ld)
            run(VE, cV, lambda e: e.tensor_tensor(N_sb[:, :], N_sb[:, :], tmpS[:, :], ALU.add))
            nupd2_v = cV.n
            n_v = nupd2_v
            SY.wait(cV, nupd2_v)
            run(SY, cS, lambda e, k=k: e.dma_start(out=tmpS[:, :], in_=scr_g[k]), 16)
            gate_ld = cS.mark()
            PE.wait(cSc, cSc.n)
            if k > 0:
                PE.wait(cV, dtT_v)
            run(PE, cPE, lambda e, k=k: e.transpose(
                pTf[:, :], acs_sb[:, k * 128:(k + 1) * 128], identf_sb[:, :]))
            VE.wait(cPE, cPE.n)
            VE.wait(cSc, sdo_s if k > 0 else 0)
            run(VE, cV, lambda e: e.tensor_copy(acsT[:, :], pTf[:, :]))
            acsT_v = cV.n
            PE.wait(cV, acsT_v)
            run(PE, cPE, lambda e, k=k: e.transpose(
                pTf[:, :], dt_sb[:, k * 128:(k + 1) * 128], identf_sb[:, :]))
            VE.wait(cPE, cPE.n)
            run(VE, cV, lambda e: e.tensor_copy(dtT[:, :], pTf[:, :]))
            dtT_v = cV.n
            SC.wait(cV, acsT_v)
            run(SC, cSc, lambda e: e.activation(sdoT[:, :], acsT[:, :], AF.Exp))
            sdo_s = cSc.n
            if k >= 2:
                SC.wait(cG, acs_rd)
            run(SC, cS2, lambda e, k=k: e.dma_start(
                out=scr_acs[k % 2:k % 2 + 1, :].rearrange("a (h l) -> h (a l)", h=128),
                in_=acs_sb[:, k * 128:(k + 1) * 128]), 16)
            SC.wait(cS2, cS2.n)
            run(SC, cS2, lambda e, k=k: e.dma_start(
                out=scr_acs2[k % 2:k % 2 + 1, :].rearrange(
                    "a (g i l) -> a g i l", g=8, i=16, l=128),
                in_=scr_acs[k % 2:k % 2 + 1, :].rearrange(
                    "a (i g l) -> a g i l", i=16, g=8, l=128)), 16)
            acs_stg = cS2.mark()
            acs_rd_new = None
            for g in range(G):
                VE.wait(cSc, cSc.n)
                if k > 0 or g > 0:
                    VE.wait(cG, acs_rd_prev)
                    VE.wait(cPE, yd_pe)
                run(VE, cV, lambda e, g=g: e.scalar_tensor_tensor(
                    fill[:, :].rearrange("s (i l) -> s i l", i=16),
                    acsT[:, :].rearrange("s (i g) -> s i g", g=8)[:, :, g:g + 1
                        ].broadcast_to((128, 16, 128)),
                    -1.0,
                    maskf_sb[:, :].rearrange("s (o l) -> s o l", o=1
                        ).broadcast_to((128, 16, 128)),
                    ALU.mult, ALU.add))
                fill_v = cV.n
                GP.wait(cV, fill_v)
                GP.wait(cS2, acs_stg)
                run(GP, cG, lambda e, k=k, g=g: e.dma_start(
                    out=fill[:, :],
                    in_=scr_acs2[k % 2:k % 2 + 1, g * 2048:(g + 1) * 2048
                        ].broadcast_to((128, 2048)),
                    accum_op=ALU.add), 16)
                acs_rd_prev = cG.mark()
                if g == G - 1:
                    acs_rd = cG.mark()
                SC.wait(cG, acs_rd_prev)
                SC.wait(cV, fill_v)
                run(SC, cSc, lambda e: e.activation(lmat[:, :], fill[:, :], AF.Exp))
                lm_s = cSc.n
                VE.wait(cSc, lm_s)
                VE.wait(cS, g2_ld)
                run(VE, cV, lambda e, g=g: e.tensor_tensor(
                    lmat[:, :].rearrange("s (i l) -> s i l", i=16),
                    lmat[:, :].rearrange("s (i l) -> s i l", i=16),
                    g2bf[:, g * 128:(g + 1) * 128].rearrange("s (o l) -> s o l", o=1
                        ).broadcast_to((128, 16, 128)),
                    ALU.mult))
                m_v = cV.n
                VE.wait(cS, hs_ld)
                VE.wait(cV, dtT_v)
                if k > 0 or g > 0:
                    VE.wait(cPE, yd_pe)
                run(VE, cV, lambda e, g=g: e.tensor_tensor(
                    xdec[:, :].rearrange("l (i u p) -> l i u p", u=1, p=64),
                    tmpX[:, :].rearrange("l (i g p) -> l i g p", g=8, p=64)[
                        :, :, g:g + 1, :],
                    dtT[:, :].rearrange("l (i g) -> l i g", g=8)[
                        :, :, g:g + 1].rearrange("l i (g u) -> l i g u", u=1
                        ).broadcast_to((128, 16, 1, 64)),
                    ALU.mult))
                xd_v = cV.n
                PE.wait(cV, m_v)
                PE.wait(cV, xd_v)
                PE.wait(cV, nb_v)
                if k > 0 or g > 0:
                    PE.wait(cV, yev_v)
                for i in range(16):
                    run(PE, cPE, lambda e, i=i: e.matmul(
                        pYd[:, i * 64:(i + 1) * 64], lmat[:, i * 128:(i + 1) * 128],
                        xdec[:, i * 64:(i + 1) * 64],
                        start=True, stop=True, skip_group_check=True))
                for blk in range(2):
                    pq = pQ0 if blk == 0 else pQ1
                    run(PE, cPE, lambda e, g=g, k=k, blk=blk, pq=pq: e.matmul(
                        pq[:, :], C_sb[:, g * T + k * 128: g * T + (k + 1) * 128],
                        tmpNb[:, g * 1024 + blk * 512: g * 1024 + (blk + 1) * 512],
                        start=True, stop=True, skip_group_check=True))
                yd_pe = cPE.n
                yoff_pe = cPE.n
                VE.wait(cPE, yd_pe)
                VE.wait(cSc, sdo_s)
                for blk in range(2):
                    pq = pQ0 if blk == 0 else pQ1
                    run(VE, cV, lambda e, g=g, blk=blk, pq=pq: e.tensor_tensor(
                        ybuf[:, :].rearrange("l (i g p) -> l i g p", g=8, p=64)[
                            :, blk * 8:(blk + 1) * 8, g:g + 1, :],
                        pq[:, :].rearrange("l (i u p) -> l i u p", u=1, p=64),
                        sdoT[:, :].rearrange("l (i g) -> l i g", g=8)[
                            :, blk * 8:(blk + 1) * 8, g:g + 1].rearrange(
                            "l i (g u) -> l i g u", u=1).broadcast_to((128, 8, 1, 64)),
                        ALU.mult))
                    VE.wait(cV, cV.n)
                    run(VE, cV, lambda e, g=g, blk=blk: e.tensor_tensor(
                        ybuf[:, :].rearrange("l (i g p) -> l i g p", g=8, p=64)[
                            :, blk * 8:(blk + 1) * 8, g:g + 1, :],
                        ybuf[:, :].rearrange("l (i g p) -> l i g p", g=8, p=64)[
                            :, blk * 8:(blk + 1) * 8, g:g + 1, :],
                        pYd[:, :].rearrange("l (i u p) -> l i u p", u=1, p=64)[
                            :, blk * 8:(blk + 1) * 8, :, :],
                        ALU.add))
                    VE.wait(cV, cV.n)
                yev_v = cV.n
            VE.wait(cV, yev_v)
            VE.wait(cPE, yd_pe)
            for sblk in range(8):
                run(VE, cV, lambda e, sblk=sblk: e.tensor_tensor(
                    xdec[:, :].rearrange("l (hh p) -> l hh p", p=64),
                    tmpX[:, sblk * 1024:(sblk + 1) * 1024].rearrange("l (hh p) -> l hh p", p=64),
                    d128_sb[:, sblk * 16:(sblk + 1) * 16].rearrange("l (hh o) -> l hh o", o=1
                        ).broadcast_to((128, 16, 64)),
                    ALU.mult))
                VE.wait(cV, cV.n)
                run(VE, cV, lambda e, sblk=sblk: e.tensor_tensor(
                    ybuf[:, sblk * 1024:(sblk + 1) * 1024],
                    ybuf[:, sblk * 1024:(sblk + 1) * 1024], xdec[:, :], ALU.add))
                VE.wait(cV, cV.n)
            VE.wait(cS, gate_ld)
            run(VE, cV, lambda e: e.tensor_tensor(ybuf[:, :], ybuf[:, :], tmpS[:, :], ALU.mult))
            VE.wait(cV, cV.n)
            for g8 in range(8):
                run(VE, cV, lambda e, g8=g8: e.tensor_tensor(
                    xdec[:, :], ybuf[:, g8 * 1024:(g8 + 1) * 1024],
                    ybuf[:, g8 * 1024:(g8 + 1) * 1024], ALU.mult))
                VE.wait(cV, cV.n)
                run(VE, cV, lambda e, g8=g8: e.reduce_sum(
                    rms_v[:, g8:g8 + 1], xdec[:, :], axis=AX.X))
                VE.wait(cV, cV.n)
            SC.wait(cV, cV.n)
            run(SC, cSc, lambda e: e.activation(rms_r[:, :], rms_v[:, :], AF.Ln,
                                                scale=1.0 / 1024.0, bias=beps[:, 0:1]))
            SC.wait(cSc, cSc.n)
            run(SC, cSc, lambda e: e.activation(rms_r[:, :], rms_r[:, :], AF.Exp, scale=-0.5))
            rms_s = cSc.n
            VE.wait(cSc, rms_s)
            for g8 in range(8):
                run(VE, cV, lambda e, g8=g8: e.tensor_scalar(
                    tmpS[:, g8 * 1024:(g8 + 1) * 1024], ybuf[:, g8 * 1024:(g8 + 1) * 1024],
                    rms_r[:, g8:g8 + 1], None, ALU.mult))
            yn_v = cV.n
            th, t0 = k // 4, (k % 4) * 128
            PE.wait(cV, yn_v)
            for j in range(64):
                pt = pT0 if j % 2 == 0 else pT1
                run(PE, cPE, lambda e, j=j, pt=pt: e.transpose(
                    pt[:, :], tmpS[:, j * 128:(j + 1) * 128], ident_sb[:, :]))
                VE.wait(cPE, cPE.n)
                run(VE, cV, lambda e, j=j, pt=pt: e.tensor_scalar(
                    tmpX[:, j * 128:(j + 1) * 128], pt[:, :], nwt_sb[:, j:j + 1],
                    None, ALU.mult))
                PE.wait(cV, cV.n)
            y_v = cV.n
            tr_pe = cPE.n
            SY.wait(cV, y_v)
            run(SY, cS, lambda e, th=th, t0=t0: e.dma_start(
                out=scr_n[th][:, :, t0:t0 + 128].rearrange("j p t -> p j t"),
                in_=tmpX[:, :].rearrange("p (j t) -> p j t", j=64)), 16)
            nrm_st = cS.mark()
        barrier()
        stB.close()
        st1.close()
        if upto == "ph2b":
            _finish(plan, SY, ALLC)
            _emit_block(nc, st0, plan)
            return nc

        stC = contextlib.ExitStack()
        wbuf0_c = stC.enter_context(nc.sbuf_tensor("wbc0", [128, 64 * 128], BF16))
        wbuf1_c = stC.enter_context(nc.sbuf_tensor("wbc1", [128, 64 * 128], BF16))
        nrm_sb = stC.enter_context(nc.sbuf_tensor("nrm_sb", [128, 64 * 512], BF16))
        gld = stC.enter_context(nc.sbuf_tensor("gld", [128, 512], BF16))
        sg_sb = stC.enter_context(nc.sbuf_tensor("sg_sb", [128, 512], BF16))
        rb4 = stC.enter_context(nc.sbuf_tensor("rb4", [128, 4 * HID], BF16))
        obuf = stC.enter_context(nc.sbuf_tensor("obuf", [128, HID], F32))
        xtc_sb = stC.enter_context(nc.sbuf_tensor("xt_c", [128, HID], F32))
        pT0_c = stC.enter_context(nc.psum_tensor("pT0f", [128, 128], BF16))
        pT1_c = stC.enter_context(nc.psum_tensor("pT1f", [128, 128], BF16))
        pQ0_c = stC.enter_context(nc.psum_tensor("pQ0f", [128, 512], F32))
        pQ1_c = stC.enter_context(nc.psum_tensor("pQ1f", [128, 512], F32))

        oslab_pe = {}
        for th in range(2):
            VE.wait(cV, cV.n)
            SY.wait(cV, cV.n)
            SY.wait(cPE, cPE.n)
            run(SY, cS, lambda e, th=th: e.dma_start(
                out=nrm_sb[:, :].rearrange("p (j t) -> p j t", j=64),
                in_=scr_n[th].rearrange("j p t -> p j t")), 16)
            nrm_ld = cS.mark()
            for ot in range(32):
                wb = wbuf0_c if ot % 2 == 0 else wbuf1_c
                key, prev_key = (th, ot), None
                if ot >= 2:
                    prev_key = (th, ot - 2)
                elif th > 0:
                    prev_key = (th - 1, 30 + ot)
                if prev_key:
                    GP.wait(cPE, oslab_pe[prev_key])
                ctr = cG0 if ot % 2 == 0 else cG1
                run(GP, ctr, lambda e, wb=wb, ot=ot: e.dma_start(
                    out=wb[:, :], in_=w_out_s[ot]), 16)
                w_ld = (ctr, ctr.mark())
                if th > 0 or ot > 0:
                    SC.wait(cV, sg_v)
                run(SC, cS2, lambda e, th=th, ot=ot: e.dma_start(
                    out=gld[:, :], in_=scr_gg[ot * 128:(ot + 1) * 128,
                                             th * 512:(th + 1) * 512]), 16)
                gld_ld = cS2.mark()
                PE.wait(*w_ld)
                PE.wait(cS, nrm_ld)
                pq = pQ0_c if ot % 2 == 0 else pQ1_c
                if prev_key:
                    PE.wait(cV, sg_prev.get(prev_key, 0))
                for cc in range(64):
                    run(PE, cPE, lambda e, wb=wb, cc=cc, pq=pq: e.matmul(
                        pq[:, :], wb[:, cc * 128:(cc + 1) * 128],
                        nrm_sb[:, cc * 512:(cc + 1) * 512],
                        start=(cc == 0), stop=(cc == 63), skip_group_check=True))
                if th == 0 and ot == 0:
                    sg_prev = {}
                oslab_pe[key] = cPE.n
                VE.wait(cPE, cPE.n)
                VE.wait(cS2, gld_ld)
                if th > 0 or ot > 0:
                    VE.wait(cPE, sgT_pe)
                run(VE, cV, lambda e, pq=pq: e.tensor_tensor(
                    sg_sb[:, :], pq[:, :], gld[:, :], ALU.mult))
                sg_v = cV.n
                sg_prev[key] = sg_v
                PE.wait(cV, sg_v)
                for tc in range(4):
                    pt = pT0_c if tc % 2 == 0 else pT1_c
                    run(PE, cPE, lambda e, tc=tc, pt=pt: e.transpose(
                        pt[:, :], sg_sb[:, tc * 128:(tc + 1) * 128], ident_sb[:, :]))
                    VE.wait(cPE, cPE.n)
                    run(VE, cV, lambda e, tc=tc, ot=ot, pt=pt: e.tensor_copy(
                        rb4[:, tc * HID + ot * 128: tc * HID + (ot + 1) * 128], pt[:, :]))
                    PE.wait(cV, cV.n)
                sgT_pe = cPE.n
            for tc in range(4):
                SY.wait(cV, cV.n)
                run(SY, cS, lambda e, th=th, tc=tc: e.dma_start(
                    out=xtc_sb[:, :],
                    in_=x_ext[3 + th * 512 + tc * 128: 3 + th * 512 + (tc + 1) * 128, :]), 16)
                VE.wait(cS, cS.n)
                run(VE, cV, lambda e, tc=tc: e.tensor_tensor(
                    obuf[:, :], xtc_sb[:, :], rb4[:, tc * HID:(tc + 1) * HID], ALU.add))
                SY.wait(cV, cV.n)
                run(SY, cS, lambda e, th=th, tc=tc: e.dma_start(
                    out=out[th * 512 + tc * 128: th * 512 + (tc + 1) * 128, :],
                    in_=obuf[:, :]), 16)
        barrier()
        stC.close()
        _finish(plan, SY, ALLC)
        _emit_block(nc, st0, plan)
    return nc


def barrier_no_ar(plan, SY, GP, PE, VE, SC, ALLC, cGc):
    for Eo in (SY, GP, PE, VE, SC):
        for c in ALLC:
            if c is cGc:
                continue
            Eo.wait(c, c.n)


def _finish(plan, SY, ALLC):
    for c in ALLC:
        SY.wait(c, c.n)


def _emit_block(nc, st0, plan):
    blk = st0.enter_context(nc.Block())

    @blk.sync
    def _(eng):
        for fn in plan["sync"]:
            fn(eng)

    @blk.gpsimd
    def _(eng):
        for fn in plan["gpsimd"]:
            fn(eng)

    @blk.tensor
    def _(eng):
        for fn in plan["tensor"]:
            fn(eng)

    @blk.vector
    def _(eng):
        for fn in plan["vector"]:
            fn(eng)

    @blk.scalar
    def _(eng):
        for fn in plan["scalar"]:
            fn(eng)


def stage_inputs(inputs, n_cores=8):
    f32 = np.float32
    x = np.asarray(inputs["L0_attn_hidden_states"], f32)
    w_in = np.asarray(inputs["L0_attn_in_proj_w"], f32)
    conv_w = np.asarray(inputs["L0_attn_conv_w"], f32)
    conv_b = np.asarray(inputs["L0_attn_conv_b"], f32)
    dt_bias = np.asarray(inputs["L0_attn_dt_bias"], f32)
    a_log = np.asarray(inputs["L0_attn_A_log"], f32)
    D = np.asarray(inputs["L0_attn_D"], f32)
    nw = np.asarray(inputs["L0_attn_norm_weight"], f32)
    w_out = np.asarray(inputs["L0_attn_out_proj_w"], f32)
    ffn_W = np.asarray(inputs["L0_ffn_W"], f32)
    ffn_b = np.asarray(inputs["L0_ffn_b"], f32)
    ffn_A = np.asarray(inputs["L0_ffn_A"], f32)
    ffn_B = np.asarray(inputs["L0_ffn_B"], f32)
    r_W = np.asarray(inputs["L0_ffn_router_W"], f32)
    r_b = np.asarray(inputs["L0_ffn_router_b"], f32)

    w_in_s = np.ascontiguousarray(
        w_in.T.reshape(32, 128, NFT, 128).transpose(2, 1, 0, 3).reshape(NFT, 128, 32 * 128))
    w_out_s = np.ascontiguousarray(
        w_out.T.reshape(64, 128, 32, 128).transpose(2, 1, 0, 3).reshape(32, 128, 64 * 128))
    w_ffn_s = np.ascontiguousarray(
        ffn_W.T.reshape(32, 128, 32, 128).transpose(2, 1, 0, 3).reshape(32, 128, 32 * 128))
    a_s = np.ascontiguousarray(
        ffn_A.reshape(128, HID).T.reshape(32, 128, 128).transpose(1, 0, 2).reshape(128, 32 * 128))
    rwt_s = np.ascontiguousarray(r_W.T.reshape(128, 32 * 8))

    shared = {
        "w_in_s": w_in_s,
        "w_out_s": w_out_s,
        "w_ffn_s": w_ffn_s,
        "a_s": a_s,
        "b_flat": np.ascontiguousarray(np.transpose(ffn_B, (0, 2, 1)).reshape(128, HID)),
        "rwt_s": rwt_s,
        "convw": np.ascontiguousarray(
            conv_w.reshape(80, 128, 4).transpose(1, 0, 2).reshape(128, 320)),
        "convb": np.ascontiguousarray(conv_b.reshape(80, 128).T),
        "dt_bias": np.ascontiguousarray(dt_bias.reshape(128, 1)),
        "a_log": np.ascontiguousarray(a_log.reshape(128, 1)),
        "d128": np.ascontiguousarray(np.tile(D[None, :], (128, 1))),
        "nwt": np.ascontiguousarray(nw.reshape(64, 128).T),
        "router_b": np.ascontiguousarray(r_b.reshape(1, 8)),
        "ffn_bias": np.ascontiguousarray(ffn_b.reshape(32, 128).T),
        "ident": np.eye(128, dtype=ml_dtypes.bfloat16),
        "identf": np.eye(128, dtype=f32),
        "maskfill": np.where(np.arange(128)[None, :] >= np.arange(128)[:, None],
                             0.0, -1e30).astype(f32),
    }
    def half_ext(b, half):
        t0 = half * T
        x_e = np.zeros((TEXT, HID), f32)
        x_e[3:3 + T] = x[b, t0:t0 + T]
        if half == 1:
            x_e[0:3] = x[b, t0 - 3:t0]
        r = np.arange(TEXT)
        gtok = t0 + r - 3
        qm = ((r >= 3) & (r < 3 + T) & (gtok >= 611) & (gtok <= 2046)).astype(f32)
        qm_t = np.zeros((128, 9), f32)
        for ti in range(9):
            qm_t[:, ti] = qm[ti * 128:(ti + 1) * 128]
        return x_e, qm_t

    in_maps = []
    for c in range(n_cores):
        b, half = c // 2, c % 2
        x_e, qm_t = half_ext(b, half)
        x_o, qm2_t = half_ext(b, 1 - half)
        d = dict(shared)
        d["x"] = x_e
        d["x_oth"] = x_o
        d["qmask"] = qm_t.astype(ml_dtypes.bfloat16)
        d["qmask2"] = qm2_t.astype(ml_dtypes.bfloat16)
        d["isodd"] = np.full((128, 1), float(half), f32)
        d["notodd"] = np.full((128, 1), float(1 - half), f32)
        in_maps.append(d)
    return in_maps


def kernel(**inputs):
    in_maps = stage_inputs(inputs, 8)
    nc = build_nc(REPLICA_GROUPS, upto="full")
    res = run_bass_kernel_spmd(nc, in_maps, core_ids=list(range(8)))
    o = np.zeros((4, 2048, HID), np.float32)
    for c in range(8):
        b, half = c // 2, c % 2
        o[b, half * T:(half + 1) * T] = res.results[c]["out"]
    return o
